# version 07
# baseline (speedup 1.0000x reference)
import numpy as np
import ml_dtypes
from contextlib import ExitStack

import concourse.bass as bass
import concourse.mybir as mybir
from concourse.bass_utils import run_bass_kernel_spmd
from concourse.aot_env import allow_ici_all_to_all

F32 = mybir.dt.float32
BF16 = mybir.dt.bfloat16
I32 = mybir.dt.int32
AF = mybir.ActivationFunctionType
ALU = mybir.AluOpType
AXL = mybir.AxisListType

B = 2
T = 8192
D = 1024
NH = 16
HD = 64
HPC = 4
TPC = 2048
CHUNK = 128
NCHUNK = T // CHUNK


class Sem:
    def __init__(self, h):
        self.h = h
        self.c = 0


class Sched:
    ENGS = ("pe", "act", "dve", "pool", "sync")

    def __init__(self, nc, es):
        self.nc = nc
        self.q = {e: [] for e in ("pe", "act", "dve", "pool", "sync")}
        self.sem = {n: Sem(es.enter_context(nc.semaphore(f"s_{n}")))
                    for n in ("ld", "ld2", "lx0", "lx1", "lst", "st1", "pe", "act",
                              "dve", "pool", "sync", "st", "cc")}
        self.pending = {}

    def op(self, eng, fn, waits=(), inc=None, dma=False):
        s = self.sem[inc] if inc else None
        n = 16 if dma else 1
        val = None
        if s is not None:
            s.c += n
            val = s.c
        if self.pending.get(eng):
            waits = tuple(self.pending.pop(eng)) + tuple(waits)
        waits = tuple((self.sem[w], v) for (w, v) in waits)

        def g(e, fn=fn, waits=waits, s=s, n=n):
            for (ss, v) in waits:
                e.wait_ge(ss.h, v)
            ins = fn(e)
            if s is not None:
                ins.then_inc(s.h, n)
        self.q[eng].append(g)
        return val

    def c(self, name):
        return self.sem[name].c

    def barrier(self):
        snap = [(n, s.c) for n, s in self.sem.items() if s.c > 0]
        for e in self.ENGS:
            self.pending[e] = list(snap)

    def flush(self, block):
        self.emit(block)
        self.q = {e: [] for e in self.ENGS}

    def emit(self, block):
        @block.tensor
        def _(eng):
            for f in self.q["pe"]:
                f(eng)

        @block.scalar
        def _(eng):
            for f in self.q["act"]:
                f(eng)

        @block.vector
        def _(eng):
            for f in self.q["dve"]:
                f(eng)

        @block.gpsimd
        def _(eng):
            for f in self.q["pool"]:
                f(eng)

        @block.sync
        def _(eng):
            for f in self.q["sync"]:
                f(eng)


def make_ident(sc, ident):
    v = sc.op("pool", lambda g: g.memset(ident[:], 1.0), inc="pool")
    return sc.op("pool", lambda g: g.affine_select(
        out=ident[:], in_=ident[:], pattern=[[-1, 128]],
        compare_op=ALU.is_equal, fill=0.0, base=0, channel_multiplier=1),
        waits=[("pool", v)], inc="pool")


def make_triu(sc, triu):
    v = sc.op("pool", lambda g: g.memset(triu[:], 1.0), inc="pool")
    return sc.op("pool", lambda g: g.affine_select(
        out=triu[:], in_=triu[:], pattern=[[1, 128]],
        compare_op=ALU.is_ge, fill=0.0, base=0, channel_multiplier=-1),
        waits=[("pool", v)], inc="pool")


def build_A(debug=False):
    nc = bass.Bass("TRN2", target_bir_lowering=False)
    dram = {}
    dram["xfm"] = nc.dram_tensor("xfm", [D, T], F32, kind="ExternalInput").ap()
    for w in ("q", "k", "v"):
        dram[f"w{w}T"] = nc.dram_tensor(f"w{w}T", [D, HPC * HD], F32, kind="ExternalInput").ap()
    dram["bqkv"] = nc.dram_tensor("bqkv", [3, HPC * HD], F32, kind="ExternalInput").ap()
    dram["yfm"] = nc.dram_tensor("yfm", [HPC * HD, T], BF16, kind="ExternalOutput").ap()
    es0 = ExitStack()
    with es0:
        sc = Sched(nc, es0)
        block = es0.enter_context(nc.Block())
        phase_A(nc, sc, block, dram, fused=False)
    return nc


def phase_A(nc, sc, block, dram, fused):
    xfm = dram["xfm"]
    wT = {w: dram[f"w{w}T"] for w in ("q", "k", "v")}
    bqkv = dram["bqkv"]
    TT = 512
    NTT = T // TT
    es = ExitStack()
    with es:
        def sb(name, shape, dt=F32):
            return es.enter_context(nc.sbuf_tensor("A_" + name, shape, dt))

        def psum(name, shape, dt=F32):
            return es.enter_context(nc.psum_tensor("A_" + name, shape, dt))

        w_sb = {w: sb(f"w{w}", [128, 8 * 256], BF16) for w in ("q", "k", "v")}
        bias_sb = sb("bias", [128, 6], F32)
        x_t = [sb(f"x_t{i}", [128, 8 * TT], BF16) for i in range(2)]
        q_fm = sb("q_fm", [128, 2 * T], BF16)
        k_fm = sb("k_fm", [128, 2 * T], BF16)
        kt = sb("kt", [128, HPC * NCHUNK * HD], BF16)
        vt = sb("vt", [128, HPC * NCHUNK * 65], BF16)
        v_stage = sb("v_stage", [128, TT], BF16)
        phi_m = sb("phi_m", [128, TT], F32)
        phi_e = sb("phi_e", [128, TT], F32)
        phi_r = sb("phi_r", [128, TT], F32)
        masked = [sb(f"masked{i}", [128, 128], BF16) for i in range(2)]
        triu = sb("triu", [128, 128], BF16)
        ident = sb("ident", [128, 128], BF16)
        S_arena = sb("S_arena", [128, 2 * 65], BF16)
        ones1 = sb("ones1", [1, 64], F32)
        rden3 = sb("rden3", [1, 384], F32)
        bcast_sb = sb("bcast_sb", [64, 128], F32)
        y_fm = sb("y_fm", [128, 2 * T], BF16)

        pp = [psum(f"pp{i}", [128, TT]) for i in range(6)]
        ptr = psum("ptr", [128, 128], BF16)
        psc_bank = psum("psc", [128, 512])
        psc = [psc_bank[:, 0:128], psc_bank[:, 128:256]]
        pnum3 = [pp[0][0:65, 0:128], pp[0][0:65, 128:256], pp[0][0:65, 256:384]]
        pS_h = [pp[1], pp[3], pp[4], pp[5]]
        pbc = pp[2][0:64, 0:128]

        make_ident(sc, ident)
        make_triu(sc, triu)
        sc.op("pool", lambda g: g.memset(ones1[:], 1.0), inc="pool")
        sc.op("pool", lambda g: g.memset(vt[:], 1.0), inc="pool")
        sc.op("pool", lambda g: g.memset(S_arena[:], 0.0), inc="pool")
        init_done = sc.c("pool")

        for w in ("q", "k", "v"):
            for j in range(8):
                sc.op("pool", lambda g, w=w, j=j: g.dma_start(
                    out=w_sb[w][:, j * 256:(j + 1) * 256],
                    in_=wT[w][j * 128:(j + 1) * 128, :]), inc="ld", dma=True)
        for wi in range(3):
            for p in range(2):
                sc.op("pool", lambda g, wi=wi, p=p: g.dma_start(
                    out=bias_sb[:, wi * 2 + p: wi * 2 + p + 1],
                    in_=bqkv[wi:wi + 1, p * 128:(p + 1) * 128].rearrange("a e -> e a")),
                    inc="ld", dma=True)
        ld_w = sc.c("ld")

        pe_tile_done = {}
        dve_tile_done = {}
        for ti in range(NTT):
            buf = ti % 2
            waits = [("pe", pe_tile_done[ti - 2])] if ti >= 2 else []
            xv = sc.op("pool", lambda g, buf=buf, ti=ti: g.dma_start(
                out=x_t[buf][:].rearrange("p (j t) -> p j t", j=8),
                in_=xfm[:, ti * TT:(ti + 1) * TT].rearrange("(j p) t -> p j t", p=128)),
                waits=waits, inc=f"lx{buf}", dma=True)

            for wi, w in enumerate(("q", "k", "v")):
                for p in range(2):
                    bank = pp[wi * 2 + p]
                    bcol = bias_sb[:, wi * 2 + p: wi * 2 + p + 1]
                    for j in range(8):
                        wts = []
                        if j == 0:
                            wts = [("ld", ld_w), (f"lx{buf}", xv)]
                            if ti > 0:
                                wts.append(("dve", dve_tile_done[ti - 1]))
                        mm = sc.op("pe", lambda t, bank=bank, w=w, j=j, p=p, buf=buf: t.matmul(
                            bank[:],
                            w_sb[w][:, j * 256 + p * 128: j * 256 + (p + 1) * 128],
                            x_t[buf][:, j * TT:(j + 1) * TT],
                            start=(j == 0), stop=(j == 7)),
                            waits=wts, inc="pe" if j == 7 else None)
                    if w in ("q", "k"):
                        dst = q_fm if w == "q" else k_fm
                        dslc = dst[:, p * T + ti * TT: p * T + (ti + 1) * TT]
                        v1 = sc.op("dve", lambda v, bank=bank, bcol=bcol: v.tensor_scalar(
                            out=phi_m[:], in0=bank[:], scalar1=bcol, scalar2=0.0,
                            op0=ALU.add, op1=ALU.min),
                            waits=[("pe", mm), ("ld", ld_w)], inc="dve")
                        a1 = sc.op("act", lambda a: a.activation(
                            out=phi_e[:], in_=phi_m[:], func=AF.Exp),
                            waits=[("dve", v1)], inc="act")
                        sc.op("dve", lambda v, bank=bank, bcol=bcol: v.tensor_scalar(
                            out=phi_r[:], in0=bank[:], scalar1=bcol, scalar2=0.0,
                            op0=ALU.add, op1=ALU.max), inc="dve")
                        v3 = sc.op("dve", lambda v, dslc=dslc: v.tensor_add(
                            out=dslc, in0=phi_e[:], in1=phi_r[:]),
                            waits=[("act", a1)], inc="dve")
                        if w == "k":
                            for sti in range(4):
                                c = ti * 4 + sti
                                src = dst[:, p * T + c * 128:p * T + (c + 1) * 128]
                                tv = sc.op("pe", lambda t, src=src: t.transpose(
                                    ptr[:], src, ident[:]),
                                    waits=[("dve", sc.c("dve")), ("pool", init_done)],
                                    inc="pe")
                                for hh in range(2):
                                    h = 2 * p + hh
                                    sc.op("dve", lambda v, hh=hh, h=h, c=c: v.tensor_copy(
                                        out=kt[:, (h * NCHUNK + c) * 64:(h * NCHUNK + c) * 64 + 64],
                                        in_=ptr[:, hh * 64:(hh + 1) * 64]),
                                        waits=[("pe", tv)], inc="dve")
                    else:
                        vv = sc.op("dve", lambda v, bank=bank, bcol=bcol: v.tensor_scalar_add(
                            out=v_stage[:], in0=bank[:], scalar1=bcol),
                            waits=[("pe", mm), ("ld", ld_w)], inc="dve")
                        for sti in range(4):
                            c = ti * 4 + sti
                            tv = sc.op("pe", lambda t, sti=sti: t.transpose(
                                ptr[:], v_stage[:, sti * 128:(sti + 1) * 128], ident[:]),
                                waits=[("dve", sc.c("dve")), ("pool", init_done)],
                                inc="pe")
                            for hh in range(2):
                                h = 2 * p + hh
                                sc.op("dve", lambda v, hh=hh, h=h, c=c: v.tensor_copy(
                                    out=vt[:, (h * NCHUNK + c) * 65:(h * NCHUNK + c) * 65 + 64],
                                    in_=ptr[:, hh * 64:(hh + 1) * 64]),
                                    waits=[("pe", tv)], inc="dve")
            pe_tile_done[ti] = sc.c("pe")
            dve_tile_done[ti] = sc.c("dve")
        phase1_dve = sc.c("dve")
        phase1_pe = sc.c("pe")

        NJ = NCHUNK * HPC
        mask_done = [0] * NJ
        num1_done = [0] * NJ
        supd_done = [0] * NJ
        ymult_done = [0] * NJ
        recip_done = [0] * NJ
        sevac_done = [0] * NJ
        bc_done = [0] * NJ
        bcevac_done = [0] * NJ

        def jparts(j):
            i, h = j // HPC, j % HPC
            return i, h, h % 2, h // 2

        for j in range(NJ + 2):
            if j >= 2:
                jj = j - 2
                bcw = [("dve", recip_done[jj])]
                if jj > 0:
                    bcw.append(("act", bcevac_done[jj - 1]))
                bc_done[jj] = sc.op("pe", lambda t, jj=jj: t.matmul(
                    pbc, ones1[:], rden3[:, (jj % 3) * 128:(jj % 3) * 128 + 128],
                    start=True, stop=True),
                    waits=bcw, inc="pe")
            if j < NJ:
                i, h, par, pair = jparts(j)
                rowoff = par * 64
                qk_off = pair * T + i * 128
                wts = [("dve", phase1_dve)]
                if j >= 2:
                    wts.append(("dve", mask_done[j - 2]))
                sc.op("pe", lambda t, rowoff=rowoff, qk_off=qk_off, buf=j % 2: t.matmul(
                    psc[buf][:],
                    k_fm[rowoff:rowoff + 64, qk_off:qk_off + 128],
                    q_fm[rowoff:rowoff + 64, qk_off:qk_off + 128],
                    start=True, stop=True, tile_position=(rowoff, 0)),
                    waits=wts, inc="pe")
                sc_done = sc.c("pe")
            if 1 <= j <= NJ:
                jj = j - 1
                i, h, par, pair = jparts(jj)
                rowoff = par * 64
                qk_off = pair * T + i * 128
                Sap = S_arena[rowoff:rowoff + 64, pair * 65: pair * 65 + 65]
                pSap = pS_h[h][rowoff:rowoff + 64, 0:65]
                vslc = vt[:, (h * NCHUNK + i) * 65:(h * NCHUNK + i) * 65 + 65]
                kslc = kt[:, (h * NCHUNK + i) * 64:(h * NCHUNK + i) * 64 + 64]
                nb = pnum3[jj % 3]
                nwts = [("dve", mask_done[jj])]
                if jj >= 3:
                    nwts.append(("dve", ymult_done[jj - 3]))
                num1_done[jj] = sc.op("pe", lambda t, nb=nb, vslc=vslc, buf=jj % 2: t.matmul(
                    nb, vslc, masked[buf][:], start=True, stop=False),
                    waits=nwts, inc="pe")
                n2w = []
                if jj >= HPC:
                    n2w = [("act", sevac_done[jj - HPC])]
                sc.op("pe", lambda t, nb=nb, Sap=Sap, rowoff=rowoff, qk_off=qk_off: t.matmul(
                    nb, Sap, q_fm[rowoff:rowoff + 64, qk_off:qk_off + 128],
                    start=False, stop=True, tile_position=(rowoff, 0)),
                    waits=n2w, inc="pe")
                supd_done[jj] = sc.op("pe", lambda t, pSap=pSap, kslc=kslc, vslc=vslc, i=i, rowoff=rowoff: t.matmul(
                    pSap, kslc, vslc, start=(i == 0), stop=(i == NCHUNK - 1),
                    tile_position=(0, rowoff)),
                    inc="pe")
                sevac_done[jj] = sc.op("act", lambda a, Sap=Sap, pSap=pSap: a.copy(
                    out=Sap, in_=pSap),
                    waits=[("pe", supd_done[jj])], inc="act")
                recip_done[jj] = sc.op("dve", lambda v, nb=nb, jj=jj: v.reciprocal(
                    out=rden3[:, (jj % 3) * 128:(jj % 3) * 128 + 128],
                    in_=nb[64:65, 0:128]),
                    waits=[("pe", supd_done[jj])], inc="dve")
            if j < NJ:
                mwts = [("pe", sc_done)]
                if j >= 2:
                    mwts.append(("pe", num1_done[j - 2]))
                mask_done[j] = sc.op("dve", lambda v, buf=j % 2: v.tensor_mul(
                    out=masked[buf][:], in0=psc[buf][:], in1=triu[:]),
                    waits=mwts, inc="dve")
            if j >= 2:
                jj = j - 2
                i, h, par, pair = jparts(jj)
                bw = [("pe", bc_done[jj])]
                if jj >= 1:
                    bw.append(("dve", ymult_done[jj - 1]))
                bcevac_done[jj] = sc.op("act", lambda a: a.copy(
                    out=bcast_sb[:], in_=pbc),
                    waits=bw, inc="act")
                ydst = y_fm[(h % 2) * 64:(h % 2) * 64 + 64,
                            (h // 2) * T + i * 128:(h // 2) * T + (i + 1) * 128]
                ymult_done[jj] = sc.op("dve", lambda v, nb=pnum3[jj % 3], ydst=ydst: v.tensor_mul(
                    out=ydst, in0=nb[0:64, 0:128], in1=bcast_sb[:]),
                    waits=[("act", bcevac_done[jj])], inc="dve")
        yl = sc.c("dve")

        if not fused:
            for p in range(2):
                sc.op("sync", lambda e, p=p: e.dma_start(
                    out=dram["yfm"][p * 128:(p + 1) * 128, :], in_=y_fm[:, p * T:(p + 1) * T]),
                    waits=[("dve", yl)], inc="st", dma=True)
            sc.op("sync", lambda e: None, waits=[("st", sc.c("st"))])
        else:
            a2a_in, a2a_out = dram["a2a1_in"], dram["a2a1_out"]
            for jj in range(8):
                for p in range(2):
                    sc.op("sync", lambda e, jj=jj, p=p: e.dma_start(
                        out=a2a_in[jj * 256 + p * 128: jj * 256 + (p + 1) * 128, :],
                        in_=y_fm[:, p * T + (jj % 4) * TPC: p * T + (jj % 4) * TPC + TPC]),
                        waits=[("dve", yl)], inc="st1", dma=True)
            def _a2a(g, a2a_in=a2a_in, a2a_out=a2a_out):
                with allow_ici_all_to_all():
                    return g.collective_compute(
                        "AllToAll", ALU.bypass, replica_groups=[list(range(8))],
                        ins=[a2a_in.opt()], outs=[a2a_out.opt()])
            sc.op("pool", _a2a, waits=[("st1", sc.c("st1"))], inc="cc")
        sc.flush(block)




def build_B(debug=False):
    nc = bass.Bass("TRN2", target_bir_lowering=False)
    dram = {}
    dram["yfm_sl"] = nc.dram_tensor("yfm_sl", [D, TPC], BF16, kind="ExternalInput").ap()
    dram["x_sl"] = nc.dram_tensor("x_sl", [TPC, D], F32, kind="ExternalInput").ap()
    dram["woT"] = nc.dram_tensor("woT", [D, D], F32, kind="ExternalInput").ap()
    dram["bo"] = nc.dram_tensor("bo", [1, D], F32, kind="ExternalInput").ap()
    dram["wgT"] = nc.dram_tensor("wgT", [D, 3072], I32, kind="ExternalInput").ap()
    dram["wuT"] = nc.dram_tensor("wuT", [D, 3072], I32, kind="ExternalInput").ap()
    dram["wdT"] = nc.dram_tensor("wdT", [3072, D], I32, kind="ExternalInput").ap()
    dram["x2o"] = nc.dram_tensor("x2o", [TPC, D], F32, kind="ExternalOutput").ap()
    dram["x1_scr"] = nc.dram_tensor("x1_scr", [TPC, D], F32).ap()
    dram["x1fm_scr"] = nc.dram_tensor("x1fm_scr", [D, TPC], BF16).ap()
    es0 = ExitStack()
    with es0:
        sc = Sched(nc, es0)
        block = es0.enter_context(nc.Block())
        phase_B(nc, sc, block, dram, fused=False)
    return nc


def phase_B(nc, sc, block, dram, fused):
    x_sl = dram["x_sl"]
    woT_d = dram.get("woT")
    bo_d = dram["bo"]
    wgT_d = dram["wgT"]
    wuT_d = dram["wuT"]
    wdT_d = dram["wdT"]
    x1_dram = dram["x1_scr"]
    x1fm_dram = dram["x1fm_scr"]
    if fused:
        yfm = dram["a2a1_out"]
        wo2_d = dram["wo2T"]
        NCJ = 16
        x2_dram = dram["x2_scr"]
        x2fm_dram = dram["ag2_in"]
    else:
        yfm = dram["yfm_sl"]
        wo2_d = dram["woT"]
        NCJ = 8
        x2_dram = None
        x2fm_dram = None
    es = ExitStack()
    with es:
        def sb(name, shape, dt=F32):
            return es.enter_context(nc.sbuf_tensor("B_" + name, shape, dt))

        def psum(name, shape, dt=F32):
            return es.enter_context(nc.psum_tensor("B_" + name, shape, dt))

        bo_bc = sb("bo_bc", [128, D], F32)
        bo_sb = sb("bo_sb", [1, D], F32)
        xres = sb("xres", [128, D], F32)
        pre = sb("pre", [128, D], F32)
        mean = sb("mean", [128, 1], F32)
        nmean = sb("nmean", [128, 1], F32)
        var = sb("var", [128, 1], F32)
        rstd = sb("rstd", [128, 1], F32)
        xout = sb("xout", [128, D], F32)
        xout_bf = sb("xout_bf", [128, D], BF16)
        ident = sb("ident", [128, 128], BF16)
        ones1 = sb("ones1", [1, 128], F32)
        x1fm_t = sb("x1fm_t", [128, 8 * 128], BF16)

        pb = [psum(f"pb{i}", [128, 512]) for i in range(6)]
        ptr = psum("ptr", [128, 128], BF16)

        make_ident(sc, ident)
        sc.op("pool", lambda g: g.memset(ones1[:], 1.0), inc="pool")
        init_done = sc.c("pool")

        es1 = ExitStack()
        with es1:
            yfm_t = es1.enter_context(nc.sbuf_tensor("B_yfm_t", [128, NCJ * 128], BF16))
            woT_sb = es1.enter_context(nc.sbuf_tensor("B_woT_sb", [128, NCJ * D], BF16))
            for j in range(NCJ):
                sc.op("pool", lambda g, j=j: g.dma_start(
                    out=woT_sb[:, j * D:(j + 1) * D],
                    in_=wo2_d[j * 128:(j + 1) * 128, :]), inc="ld", dma=True)
            ld_w = sc.c("ld")

            bv = sc.op("pool", lambda g: g.dma_start(out=bo_sb[:], in_=bo_d), inc="ld", dma=True)
            for oj in range(2):
                mm = sc.op("pe", lambda t, oj=oj: t.matmul(
                    pb[4][:, 0:512], ones1[:], bo_sb[:, oj * 512:(oj + 1) * 512],
                    start=True, stop=True),
                    waits=[("ld", bv), ("pool", init_done)] + ([("act", sc.c("act"))] if oj else []),
                    inc="pe")
                sc.op("act", lambda a, oj=oj: a.copy(
                    out=bo_bc[:, oj * 512:(oj + 1) * 512], in_=pb[4][:, 0:512]),
                    waits=[("pe", mm)], inc="act")
            bo_done = sc.c("act")

            NT1 = TPC // 128
            for tt in range(NT1):
                xv = sc.op("sync", lambda e, tt=tt: e.dma_start(
                    out=xres[:], in_=x_sl[tt * 128:(tt + 1) * 128, :]),
                    waits=([("dve", sc.c("dve"))] if tt else []), inc="ld2", dma=True)
                ccw = [("cc", dram["cc_wait"])] if fused else []
                yv = sc.op("sync", lambda e, tt=tt: e.dma_start(
                    out=yfm_t[:].rearrange("p (j t) -> p j t", j=NCJ),
                    in_=yfm[:, tt * 128:(tt + 1) * 128].rearrange("(j p) t -> p j t", p=128)),
                    waits=([("pe", sc.c("pe"))] if tt else []) + ccw, inc="ld2", dma=True)
                mms = []
                for oj in range(2):
                    for j in range(NCJ):
                        w = []
                        if j == 0:
                            w = [("ld", ld_w), ("ld2", max(xv, yv))]
                            if tt > 0:
                                w.append(("dve", sc.c("dve")))
                        mm = sc.op("pe", lambda t, oj=oj, j=j: t.matmul(
                            pb[oj][:],
                            yfm_t[:, j * 128:(j + 1) * 128],
                            woT_sb[:, j * D + oj * 512: j * D + (oj + 1) * 512],
                            start=(j == 0), stop=(j == NCJ - 1)),
                            waits=w, inc="pe" if j == NCJ - 1 else None)
                    mms.append(mm)
                for oj in range(2):
                    sc.op("dve", lambda v, oj=oj: v.tensor_add(
                        out=pre[:, oj * 512:(oj + 1) * 512], in0=pb[oj][:],
                        in1=bo_bc[:, oj * 512:(oj + 1) * 512]),
                        waits=[("pe", mms[oj]), ("act", bo_done)], inc="dve")
                v1 = sc.op("dve", lambda v: v.tensor_add(out=pre[:], in0=pre[:], in1=xres[:]),
                           waits=[("ld2", max(xv, yv))], inc="dve")
                v2 = sc.op("dve", lambda v: v.reduce_sum(out=mean[:, 0:1], in_=pre[:], axis=AXL.X),
                           inc="dve")
                v3 = sc.op("dve", lambda v: v.tensor_scalar_mul(
                    out=nmean[:, 0:1], in0=mean[:, 0:1], scalar1=-1.0 / D),
                    waits=[("dve", v2)], inc="dve")
                a1 = sc.op("act", lambda a: a.activation(
                    out=xout[:], in_=pre[:], func=AF.Square, bias=nmean[:, 0:1], scale=1.0,
                    accum_out=var[:, 0:1]),
                    waits=[("dve", v3), ("st1", sc.c("st1")), ("st", sc.c("st"))], inc="act")
                v4 = sc.op("dve", lambda v: v.tensor_scalar(
                    out=var[:, 0:1], in0=var[:, 0:1], scalar1=1.0 / D, scalar2=1e-5,
                    op0=ALU.mult, op1=ALU.add),
                    waits=[("act", a1)], inc="dve")
                a2 = sc.op("act", lambda a: a.activation(
                    out=rstd[:, 0:1], in_=var[:, 0:1], func=AF.Sqrt),
                    waits=[("dve", v4)], inc="act")
                v5 = sc.op("dve", lambda v: v.reciprocal(out=rstd[:, 0:1], in_=rstd[:, 0:1]),
                           waits=[("act", a2)], inc="dve")
                v6 = sc.op("dve", lambda v: v.tensor_scalar(
                    out=xout[:], in0=pre[:], scalar1=nmean[:, 0:1], scalar2=rstd[:, 0:1],
                    op0=ALU.add, op1=ALU.mult),
                    waits=[("dve", v5)], inc="dve")
                v7 = sc.op("dve", lambda v: v.tensor_copy(out=xout_bf[:], in_=xout[:]),
                           inc="dve")
                for sj in range(8):
                    tv = sc.op("pe", lambda t, sj=sj: t.transpose(
                        ptr[:], xout_bf[:, sj * 128:(sj + 1) * 128], ident[:]),
                        waits=[("dve", v7), ("pool", init_done)] +
                              ([("dve", sc.c("dve"))] if sj else []), inc="pe")
                    sc.op("dve", lambda v, sj=sj: v.tensor_copy(
                        out=x1fm_t[:, sj * 128:(sj + 1) * 128], in_=ptr[:]),
                        waits=[("pe", tv)], inc="dve")
                vlast = sc.c("dve")
                sc.op("sync", lambda e, tt=tt: e.dma_start(
                    out=x1_dram[tt * 128:(tt + 1) * 128, :], in_=xout[:]),
                    waits=[("dve", vlast)], inc="st1", dma=True)
                sc.op("sync", lambda e, tt=tt: e.dma_start(
                    out=x1fm_dram[:, tt * 128:(tt + 1) * 128].rearrange("(j p) t -> p j t", p=128),
                    in_=x1fm_t[:].rearrange("p (j t) -> p j t", j=8)),
                    waits=[("dve", vlast)], inc="st1", dma=True)

            b1_st = sc.c("st1")
            sc.flush(block)
        es2 = ExitStack()
        es2.__enter__()
        wg_sb = es2.enter_context(nc.sbuf_tensor("B_wg_sb", [128, 8 * 3072], BF16))
        wu_sb = es2.enter_context(nc.sbuf_tensor("B_wu_sb", [128, 8 * 3072], BF16))
        wd_sb = es2.enter_context(nc.sbuf_tensor("B_wd_sb", [128, 24 * D], BF16))
        stage_i = es2.enter_context(nc.sbuf_tensor("B_stage_i", [128, 512], I32))
        h_fm = es2.enter_context(nc.sbuf_tensor("B_h_fm", [128, 24 * 512], BF16))
        silu_t = es2.enter_context(nc.sbuf_tensor("B_silu_t", [128, 512], F32))
        x1fm_blk = es2.enter_context(nc.sbuf_tensor("B_x1fm_blk", [128, 8 * 512], BF16))
        for (wsb, wdr, nch, wid) in ((wg_sb, wgT_d, 8, 3072), (wu_sb, wuT_d, 8, 3072),
                                     (wd_sb, wdT_d, 24, D)):
            for j in range(nch):
                for cc2 in range(wid // 512):
                    lv = sc.op("pool", lambda g, wdr=wdr, j=j, cc2=cc2: g.dma_start(
                        out=stage_i[:], in_=wdr[j * 128:(j + 1) * 128, cc2 * 512:(cc2 + 1) * 512]),
                        waits=[("dve", sc.c("dve"))], inc="lst", dma=True)
                    sc.op("dve", lambda v, wsb=wsb, j=j, wid=wid, cc2=cc2: v.tensor_scalar_mul(
                        out=wsb[:, j * wid + cc2 * 512: j * wid + (cc2 + 1) * 512], in0=stage_i[:],
                        scalar1=1.0 / 127.0),
                        waits=[("lst", lv)], inc="dve")
        dq_done = sc.c("dve")

        NB = TPC // 512
        for bb in range(NB):
            xv = sc.op("sync", lambda e, bb=bb: e.dma_start(
                out=x1fm_blk[:].rearrange("p (j t) -> p j t", j=8),
                in_=x1fm_dram[:, bb * 512:(bb + 1) * 512].rearrange("(j p) t -> p j t", p=128)),
                waits=[("st1", b1_st)] + ([("dve", sc.c("dve"))] if bb else []),
                inc="ld2", dma=True)
            for hj in range(24):
                gmm = umm = None
                for j in range(8):
                    w = []
                    if j == 0:
                        w = [("ld2", xv), ("dve", dq_done)]
                        if hj >= 2:
                            w.append(("dve", sc.c("dve")))
                    gmm = sc.op("pe", lambda t, hj=hj, j=j: t.matmul(
                        pb[hj % 2][:],
                        wg_sb[:, j * 3072 + hj * 128: j * 3072 + (hj + 1) * 128],
                        x1fm_blk[:, j * 512:(j + 1) * 512],
                        start=(j == 0), stop=(j == 7)),
                        waits=w, inc="pe" if j == 7 else None)
                for j in range(8):
                    umm = sc.op("pe", lambda t, hj=hj, j=j: t.matmul(
                        pb[2 + hj % 2][:],
                        wu_sb[:, j * 3072 + hj * 128: j * 3072 + (hj + 1) * 128],
                        x1fm_blk[:, j * 512:(j + 1) * 512],
                        start=(j == 0), stop=(j == 7)),
                        inc="pe" if j == 7 else None)
                a1 = sc.op("act", lambda a, hj=hj: a.activation(
                    out=silu_t[:], in_=pb[hj % 2][:], func=AF.Silu),
                    waits=[("pe", gmm)] + ([("dve", sc.c("dve"))] if hj >= 1 else []),
                    inc="act")
                sc.op("dve", lambda v, hj=hj: v.tensor_mul(
                    out=h_fm[:, hj * 512:(hj + 1) * 512], in0=pb[2 + hj % 2][:],
                    in1=silu_t[:]),
                    waits=[("act", a1), ("pe", umm)], inc="dve")
            hdone = sc.c("dve")
            for st_i in range(4):
                tt = bb * 4 + st_i
                xv2 = sc.op("sync", lambda e, tt=tt: e.dma_start(
                    out=xres[:], in_=x1_dram[tt * 128:(tt + 1) * 128, :]),
                    waits=[("st1", b1_st), ("dve", sc.c("dve"))], inc="ld2", dma=True)
                dmm = []
                for oj in range(2):
                    for hj in range(24):
                        w = []
                        if hj == 0:
                            w = [("dve", hdone)]
                            w.append(("dve", sc.c("dve")))
                        mm = sc.op("pe", lambda t, oj=oj, hj=hj, st_i=st_i: t.matmul(
                            pb[4 + oj][:],
                            h_fm[:, hj * 512 + st_i * 128: hj * 512 + (st_i + 1) * 128],
                            wd_sb[:, hj * D + oj * 512: hj * D + (oj + 1) * 512],
                            start=(hj == 0), stop=(hj == 23)),
                            waits=w, inc="pe" if hj == 23 else None)
                    dmm.append(mm)
                for oj in range(2):
                    sc.op("dve", lambda v, oj=oj: v.tensor_add(
                        out=pre[:, oj * 512:(oj + 1) * 512], in0=pb[4 + oj][:],
                        in1=xres[:, oj * 512:(oj + 1) * 512]),
                        waits=[("pe", dmm[oj]), ("ld2", xv2)], inc="dve")
                v2 = sc.op("dve", lambda v: v.reduce_sum(out=mean[:, 0:1], in_=pre[:], axis=AXL.X),
                           inc="dve")
                v3 = sc.op("dve", lambda v: v.tensor_scalar_mul(
                    out=nmean[:, 0:1], in0=mean[:, 0:1], scalar1=-1.0 / D),
                    waits=[("dve", v2)], inc="dve")
                a1 = sc.op("act", lambda a: a.activation(
                    out=xout[:], in_=pre[:], func=AF.Square, bias=nmean[:, 0:1], scale=1.0,
                    accum_out=var[:, 0:1]),
                    waits=[("dve", v3), ("st", sc.c("st")), ("st1", sc.c("st1"))], inc="act")
                v4 = sc.op("dve", lambda v: v.tensor_scalar(
                    out=var[:, 0:1], in0=var[:, 0:1], scalar1=1.0 / D, scalar2=1e-5,
                    op0=ALU.mult, op1=ALU.add),
                    waits=[("act", a1)], inc="dve")
                a2 = sc.op("act", lambda a: a.activation(
                    out=rstd[:, 0:1], in_=var[:, 0:1], func=AF.Sqrt),
                    waits=[("dve", v4)], inc="act")
                v5 = sc.op("dve", lambda v: v.reciprocal(out=rstd[:, 0:1], in_=rstd[:, 0:1]),
                           waits=[("act", a2)], inc="dve")
                v6 = sc.op("dve", lambda v: v.tensor_scalar(
                    out=xout[:], in0=pre[:], scalar1=nmean[:, 0:1], scalar2=rstd[:, 0:1],
                    op0=ALU.add, op1=ALU.mult),
                    waits=[("dve", v5)], inc="dve")
                if not fused:
                    sc.op("sync", lambda e, tt=tt: e.dma_start(
                        out=dram["x2o"][tt * 128:(tt + 1) * 128, :], in_=xout[:]),
                        waits=[("dve", v6)], inc="st", dma=True)
                else:
                    sc.op("sync", lambda e, tt=tt: e.dma_start(
                        out=x2_dram[tt * 128:(tt + 1) * 128, :], in_=xout[:]),
                        waits=[("dve", v6)], inc="st1", dma=True)
                    v7 = sc.op("dve", lambda v: v.tensor_copy(out=xout_bf[:], in_=xout[:]),
                               waits=[("dve", v6)], inc="dve")
                    for sj in range(8):
                        tv = sc.op("pe", lambda t, sj=sj: t.transpose(
                            ptr[:], xout_bf[:, sj * 128:(sj + 1) * 128], ident[:]),
                            waits=[("dve", sc.c("dve"))], inc="pe")
                        sc.op("dve", lambda v, sj=sj: v.tensor_copy(
                            out=x1fm_t[:, sj * 128:(sj + 1) * 128], in_=ptr[:]),
                            waits=[("pe", tv)], inc="dve")
                    vl2 = sc.c("dve")
                    sc.op("sync", lambda e, tt=tt: e.dma_start(
                        out=x2fm_dram[:, tt * 128:(tt + 1) * 128].rearrange("(j p) t -> p j t", p=128),
                        in_=x1fm_t[:].rearrange("p (j t) -> p j t", j=8)),
                        waits=[("dve", vl2)], inc="st1", dma=True)
        if not fused:
            sc.op("sync", lambda e: None, waits=[("st", sc.c("st"))])
        else:
            sc.op("pool", lambda g, ai=dram["ag2_in"], ao=dram["ag2_out"]: g.collective_compute(
                "AllGather", ALU.bypass, replica_groups=[[0, 1, 2, 3], [4, 5, 6, 7]],
                ins=[ai.opt()], outs=[ao.opt()]),
                waits=[("st1", sc.c("st1"))], inc="cc")
        sc.flush(block)
        es2.__exit__(None, None, None)




def build_C(sign=1.0):
    nc = bass.Bass("TRN2", target_bir_lowering=False)
    dram = {}
    dram["x2fm"] = nc.dram_tensor("x2fm", [D, T], F32, kind="ExternalInput").ap()
    for w in ("q", "k", "v"):
        dram[f"w{w}C"] = nc.dram_tensor(f"w{w}C", [D, HPC * HD], F32, kind="ExternalInput").ap()
    dram["bqkvC"] = nc.dram_tensor("bqkvC", [3, HPC * HD], F32, kind="ExternalInput").ap()
    dram["gain"] = nc.dram_tensor("gain", [1, 1], F32, kind="ExternalInput").ap()
    dram["yfmC"] = nc.dram_tensor("yfm", [HPC * HD, T], BF16, kind="ExternalOutput").ap()
    es0 = ExitStack()
    with es0:
        sc = Sched(nc, es0)
        block = es0.enter_context(nc.Block())
        phase_C(nc, sc, block, dram, sign, fused=False)
    return nc


def phase_C(nc, sc, block, dram, sign, fused):
    wC = {w: dram[f"w{w}C"] for w in ("q", "k", "v")}
    bqkv = dram["bqkvC"]
    gain_d = dram["gain"]
    xfm = None if fused else dram["x2fm"]
    ag2_out = dram.get("ag2_out")

    TT = 512
    NTT = T // TT
    NG = T // 64
    NQT = T // 512
    es = ExitStack()
    with es:
        def sb(name, shape, dt=F32):
            return es.enter_context(nc.sbuf_tensor("C_" + name, shape, dt))

        bias_sb = sb("bias", [128, 6], F32)
        w_sb = {w: sb("w" + w, [128, 8 * 256], BF16) for w in ("q", "k", "v")}
        x_t = [sb(f"x_t{i}", [128, 8 * TT], BF16) for i in range(2)]
        q65 = [sb(f"q65_{i}", [65, T], BF16) for i in range(2)]
        k65 = [sb(f"k65_{i}", [65, T], BF16) for i in range(2)]
        k_fm = sb("k_fm", [128, T], BF16)
        q_blk = sb("q_blk", [128, 2 * T], BF16)
        vt = sb("vt", [128, 2 * NCHUNK * 65], BF16)
        ident = sb("ident", [128, 128], BF16)
        identf = sb("identf", [128, 128], F32)
        onesf = sb("onesf", [1, 128], F32)
        ksum = sb("ksum", [64, 2], F32)
        ksumb = sb("ksumb", [64, 2], BF16)
        ksum_pair = sb("ksum_pair", [128, 1], BF16)
        negmu_i = sb("negmu_i", [128, NG], F32)
        absum_i = sb("absum_i", [128, NG], F32)
        epsc_i = sb("epsc_i", [128, NG], F32)
        acc8 = [sb(f"acc8_{i}", [128, 8], F32) for i in range(2)]
        uabs = sb("uabs", [128, 1024], BF16)
        uabs_d = sb("uabs_d", [128, 1024], BF16)
        v_stage = sb("v_stage", [128, 512], F32)
        gaincol = sb("gaincol", [128, 1], F32)
        g4inv = sb("g4inv", [128, 1], F32)
        scr = sb("scr", [128, 2], F32)
        madt = sb("madt", [128, NG], F32)
        m2t = sb("m2t", [128, NG], F32)
        u_t = [sb(f"u_t{i}", [128, 1024], BF16) for i in range(3)]
        r_t = [sb(f"r_t{i}", [128, 1024], BF16) for i in range(3)]
        dmask = [sb(f"dmask{i}", [128, 512], BF16) for i in range(4)]
        epsrow = [sb(f"epsrow{i}", [1, 512], F32) for i in range(2)]
        denrow = [sb(f"denrow{i}", [1, 512], F32) for i in range(2)]
        rden = [sb(f"rdenC{i}", [1, 512], F32) for i in range(2)]
        bcast_sb = [sb(f"bcastC{i}", [64, 512], F32) for i in range(2)]
        y_t = [sb(f"y_t{i}", [128, 512], BF16) for i in range(2)]

        pa = es.enter_context(nc.psum_tensor("C_pa", [128, 2048], F32))
        pa2 = es.enter_context(nc.psum_tensor("C_pa2", [128, 2048], F32))
        ptrf = pa2[:, 1536:1664]
        pproj = {0: [pa[:, 0:512], pa[:, 512:1024], pa[:, 1024:1536]],
                 1: [pa2[:, 0:512], pa2[:, 512:1024], pa2[:, 1024:1536]]}
        psF = [pa[:, 0:1024], pa[:, 1024:2048]]
        psc2 = [pa[:, 0:1024], pa[:, 1024:2048]]
        py = [pa2[0:65, 0:512], pa2[0:65, 512:1024]]
        prow = [pa2[:, 1024:1536], pa2[:, 1536:2048]]
        pmucol = pa2[:, 1536:1664]
        pgain = pa2[:, 2047:2048]

        make_ident(sc, ident)
        v = sc.op("pool", lambda g: g.memset(identf[:], 1.0), inc="pool")
        sc.op("pool", lambda g: g.affine_select(
            out=identf[:], in_=identf[:], pattern=[[-1, 128]],
            compare_op=ALU.is_equal, fill=0.0, base=0, channel_multiplier=1),
            waits=[("pool", v)], inc="pool")
        sc.op("pool", lambda g: g.memset(onesf[:], 1.0), inc="pool")
        sc.op("pool", lambda g: g.memset(vt[:], 1.0), inc="pool")
        sc.op("pool", lambda g: g.memset(q_blk[:], 0.0), inc="pool")
        for i in range(2):
            sc.op("pool", lambda g, i=i: g.memset(k65[i][:], 1.0), inc="pool")
        for j in range(4):
            v = sc.op("pool", lambda g, j=j: g.memset(dmask[j][:], 1.0), inc="pool")
            v = sc.op("pool", lambda g, j=j: g.affine_select(
                out=dmask[j][:], in_=dmask[j][:], pattern=[[1, 512]],
                compare_op=ALU.is_ge, fill=0.0, base=-128 * j, channel_multiplier=-1),
                waits=[("pool", v)], inc="pool")
        init_done = sc.c("pool")

        for w in ("q", "k", "v"):
            for j in range(8):
                sc.op("pool", lambda g, w=w, j=j: g.dma_start(
                    out=w_sb[w][:, j * 256:(j + 1) * 256],
                    in_=wC[w][j * 128:(j + 1) * 128, :]), inc="ld", dma=True)
        for wi in range(3):
            for hh in range(2):
                sc.op("pool", lambda g, wi=wi, hh=hh: g.dma_start(
                    out=bias_sb[:, wi * 2 + hh: wi * 2 + hh + 1],
                    in_=bqkv[wi:wi + 1, hh * 128:(hh + 1) * 128].rearrange("a e -> e a")),
                    inc="ld", dma=True)
        gv = sc.op("pool", lambda g: g.dma_start(out=scr[0:1, 0:1], in_=gain_d),
                   inc="ld", dma=True)
        ld_w = sc.c("ld")

        gmm = sc.op("pe", lambda t: t.matmul(
            pgain, onesf[:], scr[0:1, 0:1], start=True, stop=True),
            waits=[("ld", gv), ("pool", init_done)], inc="pe")
        a_g = sc.op("act", lambda a: a.copy(out=gaincol[:, 0:1], in_=pgain),
                    waits=[("pe", gmm)], inc="act")
        v1 = sc.op("dve", lambda v: v.tensor_mul(out=scr[:, 0:1], in0=gaincol[:, 0:1],
                                                 in1=gaincol[:, 0:1]),
                   waits=[("act", a_g)], inc="dve")
        v2 = sc.op("dve", lambda v: v.tensor_mul(out=scr[:, 1:2], in0=scr[:, 0:1],
                                                 in1=scr[:, 0:1]),
                   waits=[("dve", v1)], inc="dve")
        g4_done = sc.op("dve", lambda v: v.reciprocal(out=g4inv[:, 0:1], in_=scr[:, 1:2]),
                        waits=[("dve", v2)], inc="dve")

        pair_end = None
        yst = {}
        for pair in range(2):
            pe_tile_done = {}
            dve_tile_done = {}
            for ti in range(NTT):
                buf = ti % 2
                waits = [("pe", pe_tile_done[ti - 2])] if ti >= 2 else []
                if fused:
                    rr = ti // 4
                    t0 = (ti % 4) * TT
                    xv = sc.op("pool", lambda g, buf=buf, rr=rr, t0=t0: g.dma_start(
                        out=x_t[buf][:].rearrange("p (j t) -> p j t", j=8),
                        in_=ag2_out[rr * D:(rr + 1) * D, t0:t0 + TT].rearrange("(j p) t -> p j t", p=128)),
                        waits=waits + [("cc", dram["cc_wait"])], inc=f"lx{buf}", dma=True)
                else:
                    xv = sc.op("pool", lambda g, buf=buf, ti=ti: g.dma_start(
                        out=x_t[buf][:].rearrange("p (j t) -> p j t", j=8),
                        in_=xfm[:, ti * TT:(ti + 1) * TT].rearrange("(j p) t -> p j t", p=128)),
                        waits=waits, inc=f"lx{buf}", dma=True)
                for wi, w in enumerate(("q", "k", "v")):
                    bank = pproj[buf][wi]
                    bcol = bias_sb[:, wi * 2 + pair: wi * 2 + pair + 1]
                    for j in range(8):
                        wts = []
                        if j == 0:
                            wts = [("ld", ld_w), (f"lx{buf}", xv)]
                            if ti > 0:
                                wts.append(("dve", dve_tile_done[ti - 1]))
                            if pair == 1 and ti == 0 and wi == 0:
                                wts += pair_end
                        mm = sc.op("pe", lambda t, bank=bank, w=w, j=j, buf=buf, pair=pair: t.matmul(
                            bank, w_sb[w][:, j * 256 + pair * 128: j * 256 + pair * 128 + 128],
                            x_t[buf][:, j * TT:(j + 1) * TT],
                            start=(j == 0), stop=(j == 7)),
                            waits=wts, inc="pe" if j == 7 else None)
                    if w == "q":
                        for hh in range(2):
                            dst = q65[hh][0:64, ti * TT:(ti + 1) * TT]
                            sc.op("dve", lambda v, bank=bank, bcol=bcol, hh=hh, dst=dst: v.tensor_scalar(
                                out=dst, in0=bank[hh * 64:(hh + 1) * 64, :],
                                scalar1=bcol[hh * 64:(hh + 1) * 64, :], scalar2=float(sign),
                                op0=ALU.add, op1=ALU.mult),
                                waits=[("pe", mm), ("ld", ld_w)], inc="dve")
                            dstb = q_blk[hh * 64:(hh + 1) * 64, ti * 2 * TT: (ti + 1) * 2 * TT] \
                                .rearrange("p (g c) -> p g c", c=128)[:, :, hh * 64:(hh + 1) * 64]
                            sc.op("dve", lambda v, bank=bank, bcol=bcol, hh=hh, dstb=dstb: v.tensor_scalar(
                                out=dstb,
                                in0=bank[hh * 64:(hh + 1) * 64, :].rearrange("p (g c) -> p g c", c=64),
                                scalar1=bcol[hh * 64:(hh + 1) * 64, :], scalar2=float(sign),
                                op0=ALU.add, op1=ALU.mult),
                                inc="dve")
                    elif w == "k":
                        for hh in range(2):
                            dst = k65[hh][0:64, ti * TT:(ti + 1) * TT]
                            sc.op("dve", lambda v, bank=bank, bcol=bcol, hh=hh, dst=dst: v.tensor_scalar_add(
                                out=dst, in0=bank[hh * 64:(hh + 1) * 64, :],
                                scalar1=bcol[hh * 64:(hh + 1) * 64, :]),
                                waits=[("pe", mm), ("ld", ld_w)], inc="dve")
                        sc.op("dve", lambda v, bank=bank, bcol=bcol, ti=ti: v.tensor_scalar_add(
                            out=k_fm[:, ti * TT:(ti + 1) * TT], in0=bank, scalar1=bcol),
                            inc="dve")
                    else:
                        sc.op("dve", lambda v, bank=bank, bcol=bcol: v.tensor_scalar_add(
                            out=v_stage[:], in0=bank, scalar1=bcol),
                            waits=[("pe", mm), ("ld", ld_w)], inc="dve")
                        for sti in range(4):
                            c = ti * 4 + sti
                            tv = sc.op("pe", lambda t, sti=sti: t.transpose(
                                ptrf, v_stage[:, sti * 128:(sti + 1) * 128], identf[:]),
                                waits=[("dve", sc.c("dve")), ("pool", init_done)], inc="pe")
                            for hh in range(2):
                                sc.op("dve", lambda v, hh=hh, c=c: v.tensor_copy(
                                    out=vt[:, (hh * NCHUNK + c) * 65:(hh * NCHUNK + c) * 65 + 64],
                                    in_=ptrf[:, hh * 64:(hh + 1) * 64]),
                                    waits=[("pe", tv)], inc="dve")
                pe_tile_done[ti] = sc.c("pe")
                dve_tile_done[ti] = sc.c("dve")
            proj_dve = sc.c("dve")

            for hh in range(2):
                r0 = sc.op("dve", lambda v, hh=hh: v.reduce_sum(
                    out=ksum[:, hh:hh + 1], in_=k65[hh][0:64, :], axis=AXL.X),
                    waits=[("dve", proj_dve)], inc="dve")
                sc.op("dve", lambda v, hh=hh: v.tensor_copy(
                    out=ksumb[:, hh:hh + 1], in_=ksum[:, hh:hh + 1]),
                    waits=[("dve", r0)], inc="dve")
                sc.op("dve", lambda v, hh=hh: v.tensor_copy(
                    out=ksum_pair[hh * 64:(hh + 1) * 64, 0:1], in_=ksum[:, hh:hh + 1]),
                    inc="dve")
            ksum_done = sc.c("dve")

            for g in range(NG):
                w = []
                if g == 0:
                    w = [("dve", ksum_done), ("act", a_g)]
                    if pair == 1:
                        w += pair_end
                mm = sc.op("pe", lambda t, g=g: t.matmul(
                    pmucol[:, g:g + 1], q_blk[:, g * 128:(g + 1) * 128], ksum_pair[:],
                    start=True, stop=True),
                    waits=w, inc="pe" if g == NG - 1 else None)
            negmu_ev = sc.op("act", lambda a: a.activation(
                out=negmu_i[:], in_=pmucol[:, 0:NG], func=AF.Copy, scale=-1.0 / T),
                waits=[("pe", mm)], inc="act")

            murow_ev = {}
            for hh in range(2):
                for tt in range(NQT):
                    idx = hh * NQT + tt
                    bankr = prow[idx % 2][0:1, 0:512]
                    w = [("act", negmu_ev)] if idx < 2 else [("act", murow_ev[idx - 2])]
                    mm = sc.op("pe", lambda t, hh=hh, tt=tt, bankr=bankr: t.matmul(
                        bankr, ksumb[:, hh:hh + 1], q65[hh][0:64, tt * 512:(tt + 1) * 512],
                        start=True, stop=True),
                        waits=w, inc="pe")
                    murow_ev[idx] = sc.op("act", lambda a, hh=hh, tt=tt, bankr=bankr: a.activation(
                        out=q65[hh][64:65, tt * 512:(tt + 1) * 512], in_=bankr,
                        func=AF.Copy, scale=-1.0 / T),
                        waits=[("pe", mm)], inc="act")
            murow_act = sc.c("act")

            last_dve_read = None
            last_act_read = None
            red_done = {}
            last_act_stat = None
            for g in range(NG):
                gpar = g % 2
                for kq in range(8):
                    par = kq % 2
                    for kk in range(2):
                        w = []
                        if kk == 0:
                            if par == 0 and last_dve_read is not None:
                                w.append(("dve", last_dve_read))
                            if par == 1 and last_act_read is not None:
                                w.append(("act", last_act_read))
                            if g == 0 and kq == 0:
                                w.append(("act", murow_act))
                        mm = sc.op("pe", lambda t, g=g, par=par, kq=kq, kk=kk: t.matmul(
                            psF[par][:, kk * 512:(kk + 1) * 512],
                            q_blk[:, g * 128:(g + 1) * 128],
                            k_fm[:, (kq * 2 + kk) * 512:(kq * 2 + kk) * 512 + 512],
                            start=True, stop=True),
                            waits=w, inc="pe" if kk == 1 else None)
                    if par == 0:
                        w = [("pe", mm)]
                        if g >= 2 and kq == 0:
                            w.append(("dve", red_done[g - 2]))
                        last_dve_read = sc.op("dve", lambda v, par=par, g=g: v.tensor_scalar(
                            out=uabs_d[:], in0=psF[par],
                            scalar1=negmu_i[:, g:g + 1], scalar2=None,
                            op0=ALU.add, op1=ALU.bypass),
                            waits=w, inc="dve")
                        sc.op("dve", lambda v, gpar=gpar, kq=kq: v.reduce_sum(
                            out=acc8[gpar][:, kq:kq + 1], in_=uabs_d[:],
                            axis=AXL.X, apply_absolute_value=True),
                            inc="dve")
                    else:
                        w = [("pe", mm)]
                        if g >= 2 and kq == 1:
                            w.append(("dve", red_done[g - 2]))
                        last_act_read = sc.op("act", lambda a, par=par, g=g, gpar=gpar, kq=kq: a.activation(
                            out=uabs[:], in_=psF[par],
                            func=AF.Abs, bias=negmu_i[:, g:g + 1], scale=1.0,
                            accum_out=acc8[gpar][:, kq:kq + 1]),
                            waits=w, inc="act")
                        last_act_stat = last_act_read
                w = [("act", last_act_stat), ("dve", sc.c("dve"))]
                red_done[g] = sc.op("dve", lambda v, gpar=gpar, g=g: v.reduce_sum(
                    out=absum_i[:, g:g + 1], in_=acc8[gpar][:], axis=AXL.X),
                    waits=w, inc="dve")
            stats_dve = sc.c("dve")
            stats_act = sc.c("act")

            w1 = sc.op("dve", lambda v: v.tensor_scalar(
                out=madt[:], in0=absum_i[:], scalar1=1.0 / T, scalar2=1e-6,
                op0=ALU.mult, op1=ALU.add), waits=[("dve", stats_dve)], inc="dve")
            sc.op("dve", lambda v: v.tensor_mul(out=m2t[:], in0=madt[:], in1=madt[:]),
                  waits=[("dve", w1)], inc="dve")
            sc.op("dve", lambda v: v.tensor_mul(out=madt[:], in0=m2t[:], in1=m2t[:]),
                  inc="dve")
            eps_done = sc.op("dve", lambda v: v.tensor_scalar(
                out=epsc_i[:], in0=madt[:], scalar1=g4inv[:, 0:1], scalar2=1e-6,
                op0=ALU.mult, op1=ALU.mult),
                waits=[("dve", g4_done)], inc="dve")

            relu_done = {}
            u_free = {}
            r_free = {}
            sq_done = {}
            bc_free = {}
            avmm = {}
            eps_ev = {}
            prow_free = {}
            ymult_done = 0
            for qt in range(NQT):
                for hh in range(2):
                    ro = hh * 64
                    for jj in range(8):
                        g = qt * 8 + jj
                        w = []
                        if jj == 0:
                            w = [("dve", eps_done)]
                            if hh in prow_free:
                                w.append(prow_free[hh])
                        tv = sc.op("pe", lambda t, hh=hh, g=g, jj=jj, ro=ro: t.transpose(
                            prow[hh][0:1, jj * 64:(jj + 1) * 64],
                            epsc_i[ro:ro + 64, g:g + 1], identf[ro:ro + 64, ro:ro + 64]),
                            waits=w, inc="pe" if jj == 7 else None)
                    eps_ev[hh] = sc.op("act", lambda a, hh=hh: a.copy(
                        out=epsrow[hh][:], in_=prow[hh][0:1, 0:512]),
                        waits=[("pe", tv)], inc="act")

                kcmax = 4 * qt + 4

                def sched_av(kc, kcmax=kcmax, qt=qt):
                    buf3 = kc % 3
                    for hh in range(2):
                        w = [("dve", sq_done[(kc, "r")])]
                        if kc == 0:
                            w.append(("dve", ymult_done))
                        avmm[hh] = sc.op("pe", lambda t, hh=hh, kc=kc, buf3=buf3, kcmax=kcmax: t.matmul(
                            py[hh],
                            vt[:, (hh * NCHUNK + kc) * 65:(hh * NCHUNK + kc) * 65 + 65],
                            r_t[buf3][:, hh * 512:(hh + 1) * 512],
                            start=(kc == 0), stop=(kc == kcmax - 1)),
                            waits=(w if hh == 0 else []), inc="pe")
                    r_free[buf3] = avmm[1]

                for kc in range(kcmax):
                    buf = kc % 2
                    buf3 = kc % 3
                    w = []
                    if (buf,) in relu_done:
                        w.append(("act", relu_done[(buf,)]))
                    elif qt == 0 and kc < 2:
                        w.append(("dve", stats_dve))
                        w.append(("act", stats_act))
                    for hh in range(2):
                        mm = sc.op("pe", lambda t, buf=buf, hh=hh, kc=kc, qt=qt: t.matmul(
                            psc2[buf][:, hh * 512:(hh + 1) * 512],
                            k65[hh][:, kc * 128:(kc + 1) * 128],
                            q65[hh][:, qt * 512:(qt + 1) * 512],
                            start=True, stop=True),
                            waits=(w if hh == 0 else []), inc="pe")
                    if kc >= 2:
                        sched_av(kc - 2)
                    rv = sc.op("act", lambda a, buf=buf, buf3=buf3: a.activation(
                        out=u_t[buf3][:], in_=psc2[buf][:], func=AF.Relu),
                        waits=[("pe", mm)] +
                              ([("dve", u_free[buf3])] if buf3 in u_free else []),
                        inc="act")
                    relu_done[(buf,)] = rv
                    p1 = sc.op("pool", lambda g, buf3=buf3: g.tensor_mul(
                        out=u_t[buf3][:], in0=u_t[buf3][:], in1=u_t[buf3][:]),
                        waits=[("act", rv)], inc="pool")
                    w = [("pool", p1)]
                    if buf3 in r_free:
                        w.append(("pe", r_free[buf3]))
                    d2 = sc.op("dve", lambda v, buf3=buf3: v.tensor_mul(
                        out=r_t[buf3][:], in0=u_t[buf3][:], in1=u_t[buf3][:]),
                        waits=w, inc="dve")
                    u_free[buf3] = d2
                    if kc >= 4 * qt:
                        for hh in range(2):
                            d2 = sc.op("dve", lambda v, buf3=buf3, kc=kc, qt=qt, hh=hh: v.tensor_mul(
                                out=r_t[buf3][:, hh * 512:(hh + 1) * 512],
                                in0=r_t[buf3][:, hh * 512:(hh + 1) * 512],
                                in1=dmask[kc - 4 * qt][:]),
                                inc="dve")
                    sq_done[(kc, "r")] = d2
                sched_av(kcmax - 2)
                sched_av(kcmax - 1)
                for hh in range(2):
                    d1 = sc.op("dve", lambda v, hh=hh: v.tensor_add(
                        out=denrow[hh][:], in0=py[hh][64:65, 0:512], in1=epsrow[hh][:]),
                        waits=[("act", eps_ev[hh]), ("pe", avmm[hh])], inc="dve")
                    d2 = sc.op("dve", lambda v, hh=hh: v.reciprocal(
                        out=rden[hh][:], in_=denrow[hh][:]),
                        waits=[("dve", d1)], inc="dve")
                    bmm = sc.op("pe", lambda t, hh=hh: t.matmul(
                        prow[hh][0:64, 0:512], onesf[0:1, 0:64], rden[hh][:],
                        start=True, stop=True),
                        waits=[("dve", d2), ("act", eps_ev[hh])], inc="pe")
                    bev = sc.op("act", lambda a, hh=hh: a.copy(
                        out=bcast_sb[hh][:], in_=prow[hh][0:64, 0:512]),
                        waits=[("pe", bmm)] +
                              ([("dve", bc_free[hh])] if hh in bc_free else []),
                        inc="act")
                    prow_free[hh] = ("act", bev)
                    rot = qt % 2
                    gq = pair * NQT + qt
                    wy = [("act", bev)]
                    if gq >= 2 and hh == 0:
                        wy.append(("st1", yst[gq - 2]))
                    ym = sc.op("dve", lambda v, hh=hh, rot=rot: v.tensor_mul(
                        out=y_t[rot][hh * 64:(hh + 1) * 64, :],
                        in0=py[hh][0:64, 0:512], in1=bcast_sb[hh][:]),
                        waits=wy, inc="dve")
                    bc_free[hh] = ym
                    ymult_done = ym
                rot = qt % 2
                gq = pair * NQT + qt
                if not fused:
                    yst[gq] = sc.op("sync", lambda e, qt=qt, rot=rot, pair=pair: e.dma_start(
                        out=dram["yfmC"][pair * 128:(pair + 1) * 128, qt * 512:(qt + 1) * 512],
                        in_=y_t[rot][:]),
                        waits=[("dve", ymult_done)], inc="st1", dma=True)
                else:
                    qq = qt // 4
                    c0 = (qt % 4) * 512
                    for tgt in (qq, qq + 4):
                        yst[gq] = sc.op("sync", lambda e, tgt=tgt, c0=c0, rot=rot, pair=pair: e.dma_start(
                            out=dram["a2a3_in"][tgt * 256 + pair * 128: tgt * 256 + (pair + 1) * 128,
                                                c0:c0 + 512],
                            in_=y_t[rot][:]),
                            waits=[("dve", ymult_done)] if tgt == qq else [], inc="st1", dma=True)
            pair_end = [("act", sc.c("act")), ("dve", sc.c("dve")), ("pool", sc.c("pool"))]

        if not fused:
            sc.op("sync", lambda e: None, waits=[("st1", sc.c("st1"))])
        else:
            def _a2a(g, a2a_in=dram["a2a3_in"], a2a_out=dram["a2a3_out"]):
                with allow_ici_all_to_all():
                    return g.collective_compute(
                        "AllToAll", ALU.bypass, replica_groups=[list(range(8))],
                        ins=[a2a_in.opt()], outs=[a2a_out.opt()])
            sc.op("pool", _a2a, waits=[("st1", sc.c("st1"))], inc="cc")
        sc.flush(block)





def build_D(stage=4):
    nc = bass.Bass("TRN2", target_bir_lowering=False)
    dram = {}
    dram["y1fm_sl"] = nc.dram_tensor("y1fm_sl", [D, TPC], BF16, kind="ExternalInput").ap()
    dram["x2_sl"] = nc.dram_tensor("x2_sl", [TPC, D], F32, kind="ExternalInput").ap()
    dram["wproj"] = nc.dram_tensor("wproj", [D, D], F32, kind="ExternalInput").ap()
    dram["bproj"] = nc.dram_tensor("bproj", [1, D], F32, kind="ExternalInput").ap()
    dram["wLT"] = nc.dram_tensor("wLT", [D, D], F32, kind="ExternalInput").ap()
    dram["bL"] = nc.dram_tensor("bL", [1, D], F32, kind="ExternalInput").ap()
    dram["ldT"] = nc.dram_tensor("ldT", [100, 4], F32, kind="ExternalInput").ap()
    dram["da"] = nc.dram_tensor("da", [100, D], F32, kind="ExternalInput").ap()
    dram["luT"] = nc.dram_tensor("luT", [4, 50], F32, kind="ExternalInput").ap()
    dram["uaT"] = nc.dram_tensor("uaT", [50, D], F32, kind="ExternalInput").ap()
    dram["x4o"] = nc.dram_tensor("x4o", [TPC, D], F32, kind="ExternalOutput").ap()
    dram["x3_scr"] = nc.dram_tensor("x3_scr", [TPC, D], F32).ap()
    dram["x3fm_scr"] = nc.dram_tensor("x3fm_scr", [D, TPC], BF16).ap()
    es0 = ExitStack()
    with es0:
        sc = Sched(nc, es0)
        block = es0.enter_context(nc.Block())
        phase_D(nc, sc, block, dram, fused=False, stage=stage)
    return nc


def phase_D(nc, sc, block, dram, fused, stage=4):
    bp_d = dram["bproj"]
    wl_d = dram["wLT"]
    bl_d = dram["bL"]
    ldT_d = dram["ldT"]
    da_d = dram["da"]
    luT_d = dram["luT"]
    uaT_d = dram["uaT"]
    x4o = dram["x4o"]
    x3_dram = dram["x3_scr"]
    x3fm_dram = dram["x3fm_scr"]
    if fused:
        yfm = dram["a2a3_out"]
        wp_d = dram["wp2"]
        NCJ = 16
        x_sl = dram["x2_scr"]
    else:
        yfm = dram["y1fm_sl"]
        wp_d = dram["wproj"]
        NCJ = 8
        x_sl = dram["x2_sl"]

    es = ExitStack()
    with es:
        def sb(name, shape, dt=F32):
            return es.enter_context(nc.sbuf_tensor("D_" + name, shape, dt))

        yfm_t = sb("yfm_t", [128, NCJ * 128], BF16)
        wp_sb = sb("wp_sb", [128, NCJ * D], BF16)
        weff = sb("weff", [128, 8 * D], BF16)
        bp_bc = sb("bp_bc", [128, D], F32)
        bl_bc = sb("bl_bc", [128, D], F32)
        bp_sb = sb("bp_sb", [1, D], F32)
        bl_sb = sb("bl_sb", [1, D], F32)
        xres = sb("xres", [128, D], F32)
        pre = sb("pre", [128, D], F32)
        mean = sb("mean", [128, 1], F32)
        nmean = sb("nmean", [128, 1], F32)
        var = sb("var", [128, 1], F32)
        rstd = sb("rstd", [128, 1], F32)
        xout = sb("xout", [128, D], F32)
        xout_bf = sb("xout_bf", [128, D], BF16)
        ident = sb("ident", [128, 128], BF16)
        onesf = sb("onesf", [1, 128], F32)
        x3fm_t = sb("x3fm_t", [128, 8 * 128], BF16)
        ldT_sb = sb("ldT_sb", [100, 4], BF16)
        da_sb = sb("da_sb", [100, D], BF16)
        luT_sb = sb("luT_sb", [4, 50], BF16)
        uaT_sb = sb("uaT_sb", [50, D], BF16)
        a1_sb = sb("a1_sb", [4, D], BF16)
        a2_sb = sb("a2_sb", [50, D], BF16)

        pb = [es.enter_context(nc.psum_tensor(f"D_pbD{i}", [128, 512], F32)) for i in range(4)]
        ptr = es.enter_context(nc.psum_tensor("D_ptrD", [128, 128], BF16))

        make_ident(sc, ident)
        sc.op("pool", lambda g: g.memset(onesf[:], 1.0), inc="pool")
        init_done = sc.c("pool")

        for j in range(NCJ):
            sc.op("pool", lambda g, j=j: g.dma_start(
                out=wp_sb[:, j * D:(j + 1) * D], in_=wp_d[j * 128:(j + 1) * 128, :]),
                inc="ld", dma=True)
        for nm, dst, srcd in (("ldT", ldT_sb, ldT_d), ("da", da_sb, da_d),
                              ("luT", luT_sb, luT_d), ("uaT", uaT_sb, uaT_d)):
            sc.op("pool", lambda g, dst=dst, srcd=srcd: g.dma_start(out=dst[:], in_=srcd),
                  inc="ld", dma=True)
        bv = sc.op("pool", lambda g: g.dma_start(out=bp_sb[:], in_=bp_d), inc="ld", dma=True)
        bv = sc.op("pool", lambda g: g.dma_start(out=bl_sb[:], in_=bl_d), inc="ld", dma=True)
        ld_w = sc.c("ld")

        evs = []
        for (row, bc) in ((bp_sb, bp_bc), (bl_sb, bl_bc)):
            for oj in range(2):
                mm = sc.op("pe", lambda t, row=row, oj=oj: t.matmul(
                    pb[3][:], onesf[:], row[:, oj * 512:(oj + 1) * 512],
                    start=True, stop=True),
                    waits=[("ld", ld_w), ("pool", init_done)] +
                          ([("act", evs[-1])] if evs else []),
                    inc="pe")
                evs.append(sc.op("act", lambda a, bc=bc, oj=oj: a.copy(
                    out=bc[:, oj * 512:(oj + 1) * 512], in_=pb[3][:]),
                    waits=[("pe", mm)], inc="act"))
        bias_done = evs[-1]

        prev = bias_done
        for oj in range(2 if stage >= 0 else 0):
            mm = sc.op("pe", lambda t, oj=oj: t.matmul(
                pb[3][0:4, :], ldT_sb[:], da_sb[:, oj * 512:(oj + 1) * 512],
                start=True, stop=True),
                waits=[("act", prev)], inc="pe")
            prev = sc.op("act", lambda a, oj=oj: a.copy(
                out=a1_sb[:, oj * 512:(oj + 1) * 512], in_=pb[3][0:4, :]),
                waits=[("pe", mm)], inc="act")
        for oj in range(2 if stage >= 1 else 0):
            mm = sc.op("pe", lambda t, oj=oj: t.matmul(
                pb[3][0:50, :], luT_sb[:], a1_sb[:, oj * 512:(oj + 1) * 512],
                start=True, stop=True),
                waits=[("act", prev)], inc="pe")
            prev = sc.op("act", lambda a, oj=oj: a.copy(
                out=a2_sb[:, oj * 512:(oj + 1) * 512], in_=pb[3][0:50, :]),
                waits=[("pe", mm)], inc="act")
        wl_stage = sb("wl_stage", [128, D], BF16)
        wdone = prev
        dv = None
        for j in range(8 if stage >= 2 else 0):
            lv = sc.op("pool", lambda g, j=j: g.dma_start(
                out=wl_stage[:], in_=wl_d[j * 128:(j + 1) * 128, :]),
                waits=[("dve", sc.c("dve"))], inc="lst", dma=True)
            for oj in range(2):
                mm = sc.op("pe", lambda t, j=j, oj=oj: t.matmul(
                    pb[2][:], a2_sb[:, j * 128:(j + 1) * 128],
                    uaT_sb[:, oj * 512:(oj + 1) * 512],
                    start=True, stop=True),
                    waits=[("act", wdone)] + ([("dve", dv)] if dv else []), inc="pe")
                dv = sc.op("dve", lambda v, j=j, oj=oj: v.tensor_add(
                    out=weff[:, j * D + oj * 512: j * D + (oj + 1) * 512],
                    in0=pb[2][:], in1=wl_stage[:, oj * 512:(oj + 1) * 512]),
                    waits=[("pe", mm), ("lst", lv)], inc="dve")
        weff_done = sc.c("dve")

        NT1 = TPC // 128
        def ln_tile(tt, src_pre, out_dram, fm_dram, fm_tile, extra_store=None):
            v2 = sc.op("dve", lambda v: v.reduce_sum(out=mean[:, 0:1], in_=src_pre, axis=AXL.X),
                       inc="dve")
            v3 = sc.op("dve", lambda v: v.tensor_scalar_mul(
                out=nmean[:, 0:1], in0=mean[:, 0:1], scalar1=-1.0 / D),
                waits=[("dve", v2)], inc="dve")
            a1 = sc.op("act", lambda a: a.activation(
                out=xout[:], in_=src_pre, func=AF.Square, bias=nmean[:, 0:1], scale=1.0,
                accum_out=var[:, 0:1]),
                waits=[("dve", v3), ("st", sc.c("st")), ("st1", sc.c("st1"))], inc="act")
            v4 = sc.op("dve", lambda v: v.tensor_scalar(
                out=var[:, 0:1], in0=var[:, 0:1], scalar1=1.0 / D, scalar2=1e-5,
                op0=ALU.mult, op1=ALU.add),
                waits=[("act", a1)], inc="dve")
            a2 = sc.op("act", lambda a: a.activation(
                out=rstd[:, 0:1], in_=var[:, 0:1], func=AF.Sqrt),
                waits=[("dve", v4)], inc="act")
            v5 = sc.op("dve", lambda v: v.reciprocal(out=rstd[:, 0:1], in_=rstd[:, 0:1]),
                       waits=[("act", a2)], inc="dve")
            v6 = sc.op("dve", lambda v: v.tensor_scalar(
                out=xout[:], in0=src_pre, scalar1=nmean[:, 0:1], scalar2=rstd[:, 0:1],
                op0=ALU.add, op1=ALU.mult),
                waits=[("dve", v5)], inc="dve")
            stsem = "st1" if fm_dram is not None else "st"
            if fm_dram is not None:
                v7 = sc.op("dve", lambda v: v.tensor_copy(out=xout_bf[:], in_=xout[:]),
                           inc="dve")
                for sj in range(8):
                    tv = sc.op("pe", lambda t, sj=sj: t.transpose(
                        ptr[:], xout_bf[:, sj * 128:(sj + 1) * 128], ident[:]),
                        waits=[("dve", sc.c("dve")), ("pool", init_done)], inc="pe")
                    sc.op("dve", lambda v, sj=sj: v.tensor_copy(
                        out=fm_tile[:, sj * 128:(sj + 1) * 128], in_=ptr[:]),
                        waits=[("pe", tv)], inc="dve")
                vlast = sc.c("dve")
                sc.op("sync", lambda e, tt=tt: e.dma_start(
                    out=fm_dram[:, tt * 128:(tt + 1) * 128].rearrange("(j p) t -> p j t", p=128),
                    in_=fm_tile[:].rearrange("p (j t) -> p j t", j=8)),
                    waits=[("dve", vlast)], inc=stsem, dma=True)
            else:
                vlast = v6
            sc.op("sync", lambda e, tt=tt: e.dma_start(
                out=out_dram[tt * 128:(tt + 1) * 128, :], in_=xout[:]),
                waits=[("dve", vlast)], inc=stsem, dma=True)

        for tt in range(NT1 if stage >= 3 else 0):
            xv = sc.op("sync", lambda e, tt=tt: e.dma_start(
                out=xres[:], in_=x_sl[tt * 128:(tt + 1) * 128, :]),
                waits=([("dve", sc.c("dve"))] if tt else []), inc="ld2", dma=True)
            ccw = [("cc", dram["cc_wait"])] if fused else []
            yv = sc.op("sync", lambda e, tt=tt: e.dma_start(
                out=yfm_t[:].rearrange("p (j t) -> p j t", j=NCJ),
                in_=yfm[:, tt * 128:(tt + 1) * 128].rearrange("(j p) t -> p j t", p=128)),
                waits=([("pe", sc.c("pe"))] if tt else []) + ccw, inc="ld2", dma=True)
            mms = []
            for oj in range(2):
                for j in range(NCJ):
                    w = []
                    if j == 0:
                        w = [("ld", ld_w), ("ld2", max(xv, yv))]
                        if tt > 0:
                            w.append(("dve", sc.c("dve")))
                    mm = sc.op("pe", lambda t, oj=oj, j=j: t.matmul(
                        pb[oj][:], yfm_t[:, j * 128:(j + 1) * 128],
                        wp_sb[:, j * D + oj * 512: j * D + (oj + 1) * 512],
                        start=(j == 0), stop=(j == NCJ - 1)),
                        waits=w, inc="pe" if j == NCJ - 1 else None)
                mms.append(mm)
            for oj in range(2):
                sc.op("dve", lambda v, oj=oj: v.tensor_add(
                    out=pre[:, oj * 512:(oj + 1) * 512], in0=pb[oj][:],
                    in1=bp_bc[:, oj * 512:(oj + 1) * 512]),
                    waits=[("pe", mms[oj]), ("act", bias_done)], inc="dve")
            sc.op("dve", lambda v: v.tensor_add(out=pre[:], in0=pre[:], in1=xres[:]),
                  waits=[("ld2", max(xv, yv))], inc="dve")
            ln_tile(tt, pre[:], x3_dram, x3fm_dram, x3fm_t)
        d1_st = sc.c("st1")

        for tt in range(NT1 if stage >= 4 else 0):
            xv = sc.op("sync", lambda e, tt=tt: e.dma_start(
                out=xres[:], in_=x3_dram[tt * 128:(tt + 1) * 128, :]),
                waits=[("st1", d1_st), ("dve", sc.c("dve"))], inc="ld2", dma=True)
            yv = sc.op("sync", lambda e, tt=tt: e.dma_start(
                out=yfm_t[:, 0:1024].rearrange("p (j t) -> p j t", j=8),
                in_=x3fm_dram[:, tt * 128:(tt + 1) * 128].rearrange("(j p) t -> p j t", p=128)),
                waits=[("st1", d1_st), ("pe", sc.c("pe"))], inc="ld2", dma=True)
            mms = []
            for oj in range(2):
                for j in range(8):
                    w = []
                    if j == 0:
                        w = [("dve", weff_done), ("ld2", max(xv, yv))]
                        if tt > 0:
                            w.append(("dve", sc.c("dve")))
                    mm = sc.op("pe", lambda t, oj=oj, j=j: t.matmul(
                        pb[oj][:], yfm_t[:, j * 128:(j + 1) * 128],
                        weff[:, j * D + oj * 512: j * D + (oj + 1) * 512],
                        start=(j == 0), stop=(j == 7)),
                        waits=w, inc="pe" if j == 7 else None)
                mms.append(mm)
            for oj in range(2):
                sc.op("dve", lambda v, oj=oj: v.tensor_add(
                    out=pre[:, oj * 512:(oj + 1) * 512], in0=pb[oj][:],
                    in1=bl_bc[:, oj * 512:(oj + 1) * 512]),
                    waits=[("pe", mms[oj]), ("act", bias_done)], inc="dve")
            sc.op("dve", lambda v: v.tensor_add(out=pre[:], in0=pre[:], in1=xres[:]),
                  waits=[("ld2", max(xv, yv))], inc="dve")
            ln_tile(tt, pre[:], x4o, None, None)
        sc.op("sync", lambda e: None, waits=[("st", sc.c("st"))])
        sc.flush(block)


HEAD_OF_CORE = [(c // 4, 4 * (c % 4)) for c in range(8)]


def kernel_unfused(**inp):
    x = np.asarray(inp["L0_attn_x"])
    Wq = np.asarray(inp["L0_attn_Wq"])
    Wk = np.asarray(inp["L0_attn_Wk"])
    Wv = np.asarray(inp["L0_attn_Wv"])
    bq = np.asarray(inp["L0_attn_bq"])
    bk = np.asarray(inp["L0_attn_bk"])
    bv = np.asarray(inp["L0_attn_bv"])

    nc = build_A()
    in_maps = []
    for c in range(8):
        b, h0 = HEAD_OF_CORE[c]
        sl = slice(h0 * HD, (h0 + HPC) * HD)
        wo2 = np.concatenate([woT, zeros], 0) if b == 0 else np.concatenate([zeros, woT], 0)
        wp2 = np.concatenate([wproj, zeros], 0) if b == 0 else np.concatenate([zeros, wproj], 0)
        in_maps.append({
            "xfm": np.ascontiguousarray(x[b].T),
            "wqT": np.ascontiguousarray(Wq[sl, :].T),
            "wkT": np.ascontiguousarray(Wk[sl, :].T),
            "wvT": np.ascontiguousarray(Wv[sl, :].T),
            "bqkv": np.ascontiguousarray(np.stack([bq[sl], bk[sl], bv[sl]])),
        })
    res = run_bass_kernel_spmd(nc, in_maps, core_ids=list(range(8)))
    y_fm = [np.concatenate([np.asarray(res.results[4 * b_ + i]["yfm"]) for i in range(4)], axis=0)
            for b_ in range(B)]

    nc = build_B(debug=False)
    in_maps = []
    for c in range(8):
        b, q = c // 4, c % 4
        ts = slice(q * TPC, (q + 1) * TPC)
        in_maps.append({
            "yfm_sl": np.ascontiguousarray(y_fm[b][:, ts]),
            "x_sl": np.ascontiguousarray(x[b][ts, :]),
            "woT": np.ascontiguousarray(np.asarray(inp["L0_attn_Wo"]).T),
            "bo": np.ascontiguousarray(np.asarray(inp["L0_attn_bo"])[None, :]),
            "wgT": np.ascontiguousarray(np.asarray(inp["L0_ffn_w_gate"]).T),
            "wuT": np.ascontiguousarray(np.asarray(inp["L0_ffn_w_up"]).T),
            "wdT": np.ascontiguousarray(np.asarray(inp["L0_ffn_w_down"]).T),
        })
    res = run_bass_kernel_spmd(nc, in_maps, core_ids=list(range(8)))
    x2 = np.zeros((B, T, D), np.float32)
    for c in range(8):
        b, q = c // 4, c % 4
        ts = slice(q * TPC, (q + 1) * TPC)
        x2[b, ts] = np.asarray(res.results[c]["x2o"])

    W_attn = np.asarray(inp["L1_attn_W_attn"])
    b_attn = np.asarray(inp["L1_attn_b_attn"])
    gain = np.asarray(inp["L1_attn_score_gain"]).reshape(1, 1)
    sign = 1.0 if float(gain[0, 0]) >= 0 else -1.0
    nc = build_C(sign=sign)
    in_maps = []
    for c in range(8):
        b, h0 = HEAD_OF_CORE[c]
        cs = slice(h0 * HD, (h0 + HPC) * HD)
        in_maps.append({
            "x2fm": np.ascontiguousarray(x2[b].T),
            "wqC": np.ascontiguousarray(W_attn[:, cs]),
            "wkC": np.ascontiguousarray(W_attn[:, 1024 + h0 * HD: 1024 + (h0 + HPC) * HD]),
            "wvC": np.ascontiguousarray(W_attn[:, 2048 + h0 * HD: 2048 + (h0 + HPC) * HD]),
            "bqkvC": np.ascontiguousarray(np.stack([
                b_attn[cs], b_attn[1024 + h0 * HD: 1024 + (h0 + HPC) * HD],
                b_attn[2048 + h0 * HD: 2048 + (h0 + HPC) * HD]])),
            "gain": np.ascontiguousarray(gain.astype(np.float32)),
        })
    res = run_bass_kernel_spmd(nc, in_maps, core_ids=list(range(8)))
    y1_fm = [np.concatenate([np.asarray(res.results[4 * b_ + i]["yfm"]) for i in range(4)], axis=0)
             for b_ in range(B)]

    nc = build_D()
    in_maps = []
    for c in range(8):
        b, q = c // 4, c % 4
        ts = slice(q * TPC, (q + 1) * TPC)
        in_maps.append({
            "y1fm_sl": np.ascontiguousarray(y1_fm[b][:, ts]),
            "x2_sl": np.ascontiguousarray(x2[b][ts, :]),
            "wproj": np.ascontiguousarray(np.asarray(inp["L1_attn_W_proj"])),
            "bproj": np.ascontiguousarray(np.asarray(inp["L1_attn_b_proj"])[None, :]),
            "wLT": np.ascontiguousarray(np.asarray(inp["L1_ffn_W"]).T),
            "bL": np.ascontiguousarray(np.asarray(inp["L1_ffn_b"])[None, :]),
            "ldT": np.ascontiguousarray(np.asarray(inp["L1_ffn_lora_down"]).T),
            "da": np.ascontiguousarray(np.asarray(inp["L1_ffn_down_aux"])),
            "luT": np.ascontiguousarray(np.asarray(inp["L1_ffn_lora_up"]).T),
            "uaT": np.ascontiguousarray(np.asarray(inp["L1_ffn_up_aux"]).T),
        })
    res = run_bass_kernel_spmd(nc, in_maps, core_ids=list(range(8)))
    out = np.zeros((B, T, D), np.float32)
    for c in range(8):
        b, q = c // 4, c % 4
        ts = slice(q * TPC, (q + 1) * TPC)
        out[b, ts] = np.asarray(res.results[c]["x4o"])
    return out


def build_full(sign=1.0):
    nc = bass.Bass("TRN2", target_bir_lowering=False)
    dram = {}
    dram["xfm"] = nc.dram_tensor("xfm", [D, T], F32, kind="ExternalInput").ap()
    for w in ("q", "k", "v"):
        dram[f"w{w}T"] = nc.dram_tensor(f"w{w}T", [D, HPC * HD], F32, kind="ExternalInput").ap()
    dram["bqkv"] = nc.dram_tensor("bqkv", [3, HPC * HD], F32, kind="ExternalInput").ap()
    dram["x_sl"] = nc.dram_tensor("x_sl", [TPC, D], F32, kind="ExternalInput").ap()
    dram["wo2T"] = nc.dram_tensor("wo2T", [2 * D, D], F32, kind="ExternalInput").ap()
    dram["bo"] = nc.dram_tensor("bo", [1, D], F32, kind="ExternalInput").ap()
    dram["wgT"] = nc.dram_tensor("wgT", [D, 3072], I32, kind="ExternalInput").ap()
    dram["wuT"] = nc.dram_tensor("wuT", [D, 3072], I32, kind="ExternalInput").ap()
    dram["wdT"] = nc.dram_tensor("wdT", [3072, D], I32, kind="ExternalInput").ap()
    for w in ("q", "k", "v"):
        dram[f"w{w}C"] = nc.dram_tensor(f"w{w}C", [D, HPC * HD], F32, kind="ExternalInput").ap()
    dram["bqkvC"] = nc.dram_tensor("bqkvC", [3, HPC * HD], F32, kind="ExternalInput").ap()
    dram["gain"] = nc.dram_tensor("gain", [1, 1], F32, kind="ExternalInput").ap()
    dram["wp2"] = nc.dram_tensor("wp2", [2 * D, D], F32, kind="ExternalInput").ap()
    dram["bproj"] = nc.dram_tensor("bproj", [1, D], F32, kind="ExternalInput").ap()
    dram["wLT"] = nc.dram_tensor("wLT", [D, D], F32, kind="ExternalInput").ap()
    dram["bL"] = nc.dram_tensor("bL", [1, D], F32, kind="ExternalInput").ap()
    dram["ldT"] = nc.dram_tensor("ldT", [100, 4], F32, kind="ExternalInput").ap()
    dram["da"] = nc.dram_tensor("da", [100, D], F32, kind="ExternalInput").ap()
    dram["luT"] = nc.dram_tensor("luT", [4, 50], F32, kind="ExternalInput").ap()
    dram["uaT"] = nc.dram_tensor("uaT", [50, D], F32, kind="ExternalInput").ap()
    dram["x4o"] = nc.dram_tensor("x4o", [TPC, D], F32, kind="ExternalOutput").ap()
    for n, sh, dt in (("x1_scr", [TPC, D], F32), ("x1fm_scr", [D, TPC], BF16),
                      ("x2_scr", [TPC, D], F32), ("x3_scr", [TPC, D], F32),
                      ("x3fm_scr", [D, TPC], BF16),
                      ("a2a1_in", [2048, TPC], BF16), ("a2a1_out", [2048, TPC], BF16),
                      ("ag2_in", [D, TPC], BF16), ("ag2_out", [4 * D, TPC], BF16),
                      ("a2a3_in", [2048, TPC], BF16), ("a2a3_out", [2048, TPC], BF16)):
        dram[n] = nc.dram_tensor(n, sh, dt).ap()
    es0 = ExitStack()
    with es0:
        sc = Sched(nc, es0)
        block = es0.enter_context(nc.Block())
        phase_A(nc, sc, block, dram, fused=True)
        sc.barrier()
        dram["cc_wait"] = 1
        phase_B(nc, sc, block, dram, fused=True)
        sc.barrier()
        dram["cc_wait"] = 2
        phase_C(nc, sc, block, dram, sign, fused=True)
        sc.barrier()
        dram["cc_wait"] = 3
        phase_D(nc, sc, block, dram, fused=True)
    return nc


def kernel(**inp):
    inp = {k: np.asarray(v) for k, v in inp.items()}
    x = inp["L0_attn_x"]
    gain = inp["L1_attn_score_gain"].reshape(1, 1).astype(np.float32)
    sign = 1.0 if float(gain[0, 0]) >= 0 else -1.0
    W_attn = inp["L1_attn_W_attn"]
    b_attn = inp["L1_attn_b_attn"]
    woT = np.ascontiguousarray(inp["L0_attn_Wo"].T)
    wproj = np.ascontiguousarray(inp["L1_attn_W_proj"])
    zeros = np.zeros((D, D), np.float32)
    nc = build_full(sign=sign)
    in_maps = []
    for c in range(8):
        b, h0 = HEAD_OF_CORE[c]
        q = c % 4
        sl = slice(h0 * HD, (h0 + HPC) * HD)
        ts = slice(q * TPC, (q + 1) * TPC)
        wo2 = np.concatenate([woT, zeros], 0) if b == 0 else np.concatenate([zeros, woT], 0)
        wp2 = np.concatenate([wproj, zeros], 0) if b == 0 else np.concatenate([zeros, wproj], 0)
        in_maps.append({
            "xfm": np.ascontiguousarray(x[b].T),
            "wqT": np.ascontiguousarray(inp["L0_attn_Wq"][sl, :].T),
            "wkT": np.ascontiguousarray(inp["L0_attn_Wk"][sl, :].T),
            "wvT": np.ascontiguousarray(inp["L0_attn_Wv"][sl, :].T),
            "bqkv": np.ascontiguousarray(np.stack([
                inp["L0_attn_bq"][sl], inp["L0_attn_bk"][sl], inp["L0_attn_bv"][sl]])),
            "x_sl": np.ascontiguousarray(x[b][ts, :]),
            "wo2T": np.ascontiguousarray(wo2),
            "bo": np.ascontiguousarray(inp["L0_attn_bo"][None, :]),
            "wgT": np.ascontiguousarray(inp["L0_ffn_w_gate"].T),
            "wuT": np.ascontiguousarray(inp["L0_ffn_w_up"].T),
            "wdT": np.ascontiguousarray(inp["L0_ffn_w_down"].T),
            "wqC": np.ascontiguousarray(W_attn[:, sl]),
            "wkC": np.ascontiguousarray(W_attn[:, 1024 + h0 * HD: 1024 + (h0 + HPC) * HD]),
            "wvC": np.ascontiguousarray(W_attn[:, 2048 + h0 * HD: 2048 + (h0 + HPC) * HD]),
            "bqkvC": np.ascontiguousarray(np.stack([
                b_attn[sl], b_attn[1024 + h0 * HD: 1024 + (h0 + HPC) * HD],
                b_attn[2048 + h0 * HD: 2048 + (h0 + HPC) * HD]])),
            "gain": np.ascontiguousarray(gain),
            "wp2": np.ascontiguousarray(wp2),
            "bproj": np.ascontiguousarray(inp["L1_attn_b_proj"][None, :]),
            "wLT": np.ascontiguousarray(inp["L1_ffn_W"].T),
            "bL": np.ascontiguousarray(inp["L1_ffn_b"][None, :]),
            "ldT": np.ascontiguousarray(inp["L1_ffn_lora_down"].T),
            "da": np.ascontiguousarray(inp["L1_ffn_down_aux"]),
            "luT": np.ascontiguousarray(inp["L1_ffn_lora_up"].T),
            "uaT": np.ascontiguousarray(inp["L1_ffn_up_aux"].T),
        })
    res = run_bass_kernel_spmd(nc, in_maps, core_ids=list(range(8)))
    out = np.zeros((B, T, D), np.float32)
    for c in range(8):
        b, _ = HEAD_OF_CORE[c]
        q = c % 4
        out[b, q * TPC:(q + 1) * TPC] = np.asarray(res.results[c]["x4o"])
    return out


# version 09
# speedup vs baseline: 1.1347x; 1.1347x over previous
import numpy as np
import ml_dtypes
from contextlib import ExitStack

import concourse.bass as bass
import concourse.mybir as mybir
from concourse.bass_utils import run_bass_kernel_spmd
from concourse.aot_env import allow_ici_all_to_all

F32 = mybir.dt.float32
BF16 = mybir.dt.bfloat16
I32 = mybir.dt.int32
AF = mybir.ActivationFunctionType
ALU = mybir.AluOpType
AXL = mybir.AxisListType

B = 2
C_PAIRS = 2
T = 8192
D = 1024
NH = 16
HD = 64
HPC = 4
TPC = 2048
CHUNK = 128
NCHUNK = T // CHUNK


class Sem:
    def __init__(self, h):
        self.h = h
        self.c = 0


class Sched:
    ENGS = ("pe", "act", "dve", "pool", "sync")

    def __init__(self, nc, es):
        self.nc = nc
        self.q = {e: [] for e in ("pe", "act", "dve", "pool", "sync")}
        self.sem = {n: Sem(es.enter_context(nc.semaphore(f"s_{n}")))
                    for n in ("ld", "ld2", "lx0", "lx1", "lst", "st1", "pe", "act",
                              "dve", "pool", "sync", "st", "cc")}
        self.pending = {}

    def op(self, eng, fn, waits=(), inc=None, dma=False):
        s = self.sem[inc] if inc else None
        n = 16 if dma else 1
        val = None
        if s is not None:
            s.c += n
            val = s.c
        if self.pending.get(eng):
            waits = tuple(self.pending.pop(eng)) + tuple(waits)
        waits = tuple((self.sem[w], v) for (w, v) in waits)

        def g(e, fn=fn, waits=waits, s=s, n=n):
            for (ss, v) in waits:
                e.wait_ge(ss.h, v)
            ins = fn(e)
            if s is not None:
                ins.then_inc(s.h, n)
        self.q[eng].append(g)
        return val

    def c(self, name):
        return self.sem[name].c

    def barrier(self):
        snap = [(n, s.c) for n, s in self.sem.items() if s.c > 0]
        for e in self.ENGS:
            self.pending[e] = list(snap)

    def flush(self, block):
        self.emit(block)
        self.q = {e: [] for e in self.ENGS}

    def emit(self, block):
        @block.tensor
        def _(eng):
            for f in self.q["pe"]:
                f(eng)

        @block.scalar
        def _(eng):
            for f in self.q["act"]:
                f(eng)

        @block.vector
        def _(eng):
            for f in self.q["dve"]:
                f(eng)

        @block.gpsimd
        def _(eng):
            for f in self.q["pool"]:
                f(eng)

        @block.sync
        def _(eng):
            for f in self.q["sync"]:
                f(eng)


def make_ident(sc, ident):
    v = sc.op("pool", lambda g: g.memset(ident[:], 1.0), inc="pool")
    return sc.op("pool", lambda g: g.affine_select(
        out=ident[:], in_=ident[:], pattern=[[-1, 128]],
        compare_op=ALU.is_equal, fill=0.0, base=0, channel_multiplier=1),
        waits=[("pool", v)], inc="pool")


def make_triu(sc, triu):
    v = sc.op("pool", lambda g: g.memset(triu[:], 1.0), inc="pool")
    return sc.op("pool", lambda g: g.affine_select(
        out=triu[:], in_=triu[:], pattern=[[1, 128]],
        compare_op=ALU.is_ge, fill=0.0, base=0, channel_multiplier=-1),
        waits=[("pool", v)], inc="pool")


def build_A(debug=False):
    nc = bass.Bass("TRN2", target_bir_lowering=False)
    dram = {}
    dram["xfm"] = nc.dram_tensor("xfm", [D, T], F32, kind="ExternalInput").ap()
    for w in ("q", "k", "v"):
        dram[f"w{w}T"] = nc.dram_tensor(f"w{w}T", [D, HPC * HD], F32, kind="ExternalInput").ap()
    dram["bqkv"] = nc.dram_tensor("bqkv", [3, HPC * HD], F32, kind="ExternalInput").ap()
    dram["yfm"] = nc.dram_tensor("yfm", [HPC * HD, T], BF16, kind="ExternalOutput").ap()
    es0 = ExitStack()
    with es0:
        sc = Sched(nc, es0)
        block = es0.enter_context(nc.Block())
        phase_A(nc, sc, block, dram, fused=False)
    return nc


def phase_A(nc, sc, block, dram, fused):
    xfm = dram["xfm"]
    wT = {w: dram[f"w{w}T"] for w in ("q", "k", "v")}
    bqkv = dram["bqkv"]
    TT = 512
    NTT = T // TT
    es = ExitStack()
    with es:
        def sb(name, shape, dt=F32):
            return es.enter_context(nc.sbuf_tensor("A_" + name, shape, dt))

        def psum(name, shape, dt=F32):
            return es.enter_context(nc.psum_tensor("A_" + name, shape, dt))

        w_sb = {w: sb(f"w{w}", [128, 8 * 256], BF16) for w in ("q", "k", "v")}
        bias_sb = sb("bias", [128, 6], F32)
        x_t = [sb(f"x_t{i}", [128, 8 * TT], BF16) for i in range(2)]
        q_fm = sb("q_fm", [128, 2 * T], BF16)
        k_fm = sb("k_fm", [128, 2 * T], BF16)
        kt = sb("kt", [128, HPC * NCHUNK * HD], BF16)
        vt = sb("vt", [128, HPC * NCHUNK * 65], BF16)
        v_stage = sb("v_stage", [128, TT], BF16)
        phi_m = sb("phi_m", [128, TT], F32)
        phi_e = sb("phi_e", [128, TT], F32)
        phi_r = sb("phi_r", [128, TT], F32)
        masked = [sb(f"masked{i}", [128, 128], BF16) for i in range(2)]
        triu = sb("triu", [128, 128], BF16)
        ident = sb("ident", [128, 128], BF16)
        S_arena = sb("S_arena", [128, 2 * 65], BF16)
        ones1 = sb("ones1", [1, 64], F32)
        rden3 = sb("rden3", [1, 384], F32)
        bcast_sb = sb("bcast_sb", [64, 128], F32)
        y_fm = sb("y_fm", [128, 2 * T], BF16)

        pp = [psum(f"pp{i}", [128, TT]) for i in range(6)]
        ptr = psum("ptr", [128, 128], BF16)
        psc_bank = psum("psc", [128, 512])
        psc = [psc_bank[:, 0:128], psc_bank[:, 128:256]]
        pnum3 = [pp[0][0:65, 0:128], pp[0][0:65, 128:256], pp[0][0:65, 256:384]]
        pS_h = [pp[1], pp[3], pp[4], pp[5]]
        pbc = pp[2][0:64, 0:128]

        make_ident(sc, ident)
        make_triu(sc, triu)
        sc.op("pool", lambda g: g.memset(ones1[:], 1.0), inc="pool")
        sc.op("pool", lambda g: g.memset(vt[:], 1.0), inc="pool")
        sc.op("pool", lambda g: g.memset(S_arena[:], 0.0), inc="pool")
        init_done = sc.c("pool")

        for w in ("q", "k", "v"):
            for j in range(8):
                sc.op("pool", lambda g, w=w, j=j: g.dma_start(
                    out=w_sb[w][:, j * 256:(j + 1) * 256],
                    in_=wT[w][j * 128:(j + 1) * 128, :]), inc="ld", dma=True)
        for wi in range(3):
            for p in range(2):
                sc.op("pool", lambda g, wi=wi, p=p: g.dma_start(
                    out=bias_sb[:, wi * 2 + p: wi * 2 + p + 1],
                    in_=bqkv[wi:wi + 1, p * 128:(p + 1) * 128].rearrange("a e -> e a")),
                    inc="ld", dma=True)
        ld_w = sc.c("ld")

        pe_tile_done = {}
        dve_tile_done = {}
        for ti in range(NTT):
            buf = ti % 2
            waits = [("pe", pe_tile_done[ti - 2])] if ti >= 2 else []
            xv = sc.op("pool", lambda g, buf=buf, ti=ti: g.dma_start(
                out=x_t[buf][:].rearrange("p (j t) -> p j t", j=8),
                in_=xfm[:, ti * TT:(ti + 1) * TT].rearrange("(j p) t -> p j t", p=128)),
                waits=waits, inc=f"lx{buf}", dma=True)

            for wi, w in enumerate(("q", "k", "v")):
                for p in range(2):
                    bank = pp[wi * 2 + p]
                    bcol = bias_sb[:, wi * 2 + p: wi * 2 + p + 1]
                    for j in range(8):
                        wts = []
                        if j == 0:
                            wts = [("ld", ld_w), (f"lx{buf}", xv)]
                            if ti > 0:
                                wts.append(("dve", dve_tile_done[ti - 1]))
                        mm = sc.op("pe", lambda t, bank=bank, w=w, j=j, p=p, buf=buf: t.matmul(
                            bank[:],
                            w_sb[w][:, j * 256 + p * 128: j * 256 + (p + 1) * 128],
                            x_t[buf][:, j * TT:(j + 1) * TT],
                            start=(j == 0), stop=(j == 7)),
                            waits=wts, inc="pe" if j == 7 else None)
                    if w in ("q", "k"):
                        dst = q_fm if w == "q" else k_fm
                        dslc = dst[:, p * T + ti * TT: p * T + (ti + 1) * TT]
                        v1 = sc.op("dve", lambda v, bank=bank, bcol=bcol: v.tensor_scalar(
                            out=phi_m[:], in0=bank[:], scalar1=bcol, scalar2=0.0,
                            op0=ALU.add, op1=ALU.min),
                            waits=[("pe", mm), ("ld", ld_w)], inc="dve")
                        a1 = sc.op("act", lambda a: a.activation(
                            out=phi_e[:], in_=phi_m[:], func=AF.Exp),
                            waits=[("dve", v1)], inc="act")
                        sc.op("dve", lambda v, bank=bank, bcol=bcol: v.tensor_scalar(
                            out=phi_r[:], in0=bank[:], scalar1=bcol, scalar2=0.0,
                            op0=ALU.add, op1=ALU.max), inc="dve")
                        v3 = sc.op("dve", lambda v, dslc=dslc: v.tensor_add(
                            out=dslc, in0=phi_e[:], in1=phi_r[:]),
                            waits=[("act", a1)], inc="dve")
                        if w == "k":
                            for sti in range(4):
                                c = ti * 4 + sti
                                src = dst[:, p * T + c * 128:p * T + (c + 1) * 128]
                                tv = sc.op("pe", lambda t, src=src: t.transpose(
                                    ptr[:], src, ident[:]),
                                    waits=[("dve", sc.c("dve")), ("pool", init_done)],
                                    inc="pe")
                                for hh in range(2):
                                    h = 2 * p + hh
                                    sc.op("dve", lambda v, hh=hh, h=h, c=c: v.tensor_copy(
                                        out=kt[:, (h * NCHUNK + c) * 64:(h * NCHUNK + c) * 64 + 64],
                                        in_=ptr[:, hh * 64:(hh + 1) * 64]),
                                        waits=[("pe", tv)], inc="dve")
                    else:
                        vv = sc.op("dve", lambda v, bank=bank, bcol=bcol: v.tensor_scalar_add(
                            out=v_stage[:], in0=bank[:], scalar1=bcol),
                            waits=[("pe", mm), ("ld", ld_w)], inc="dve")
                        for sti in range(4):
                            c = ti * 4 + sti
                            tv = sc.op("pe", lambda t, sti=sti: t.transpose(
                                ptr[:], v_stage[:, sti * 128:(sti + 1) * 128], ident[:]),
                                waits=[("dve", sc.c("dve")), ("pool", init_done)],
                                inc="pe")
                            for hh in range(2):
                                h = 2 * p + hh
                                sc.op("dve", lambda v, hh=hh, h=h, c=c: v.tensor_copy(
                                    out=vt[:, (h * NCHUNK + c) * 65:(h * NCHUNK + c) * 65 + 64],
                                    in_=ptr[:, hh * 64:(hh + 1) * 64]),
                                    waits=[("pe", tv)], inc="dve")
            pe_tile_done[ti] = sc.c("pe")
            dve_tile_done[ti] = sc.c("dve")
        phase1_dve = sc.c("dve")
        phase1_pe = sc.c("pe")

        NJ = NCHUNK * HPC
        mask_done = [0] * NJ
        num1_done = [0] * NJ
        supd_done = [0] * NJ
        ymult_done = [0] * NJ
        recip_done = [0] * NJ
        sevac_done = [0] * NJ
        bc_done = [0] * NJ
        bcevac_done = [0] * NJ

        def jparts(j):
            i, h = j // HPC, j % HPC
            return i, h, h % 2, h // 2

        for j in range(NJ + 2):
            if j >= 2:
                jj = j - 2
                bcw = [("dve", recip_done[jj])]
                if jj > 0:
                    bcw.append(("act", bcevac_done[jj - 1]))
                bc_done[jj] = sc.op("pe", lambda t, jj=jj: t.matmul(
                    pbc, ones1[:], rden3[:, (jj % 3) * 128:(jj % 3) * 128 + 128],
                    start=True, stop=True),
                    waits=bcw, inc="pe")
            if j < NJ:
                i, h, par, pair = jparts(j)
                rowoff = par * 64
                qk_off = pair * T + i * 128
                wts = [("dve", phase1_dve)]
                if j >= 2:
                    wts.append(("dve", mask_done[j - 2]))
                sc.op("pe", lambda t, rowoff=rowoff, qk_off=qk_off, buf=j % 2: t.matmul(
                    psc[buf][:],
                    k_fm[rowoff:rowoff + 64, qk_off:qk_off + 128],
                    q_fm[rowoff:rowoff + 64, qk_off:qk_off + 128],
                    start=True, stop=True, tile_position=(rowoff, 0)),
                    waits=wts, inc="pe")
                sc_done = sc.c("pe")
            if 1 <= j <= NJ:
                jj = j - 1
                i, h, par, pair = jparts(jj)
                rowoff = par * 64
                qk_off = pair * T + i * 128
                Sap = S_arena[rowoff:rowoff + 64, pair * 65: pair * 65 + 65]
                pSap = pS_h[h][rowoff:rowoff + 64, 0:65]
                vslc = vt[:, (h * NCHUNK + i) * 65:(h * NCHUNK + i) * 65 + 65]
                kslc = kt[:, (h * NCHUNK + i) * 64:(h * NCHUNK + i) * 64 + 64]
                nb = pnum3[jj % 3]
                nwts = [("dve", mask_done[jj])]
                if jj >= 3:
                    nwts.append(("dve", ymult_done[jj - 3]))
                num1_done[jj] = sc.op("pe", lambda t, nb=nb, vslc=vslc, buf=jj % 2: t.matmul(
                    nb, vslc, masked[buf][:], start=True, stop=False),
                    waits=nwts, inc="pe")
                n2w = []
                if jj >= HPC:
                    n2w = [("act", sevac_done[jj - HPC])]
                sc.op("pe", lambda t, nb=nb, Sap=Sap, rowoff=rowoff, qk_off=qk_off: t.matmul(
                    nb, Sap, q_fm[rowoff:rowoff + 64, qk_off:qk_off + 128],
                    start=False, stop=True, tile_position=(rowoff, 0)),
                    waits=n2w, inc="pe")
                supd_done[jj] = sc.op("pe", lambda t, pSap=pSap, kslc=kslc, vslc=vslc, i=i, rowoff=rowoff: t.matmul(
                    pSap, kslc, vslc, start=(i == 0), stop=(i == NCHUNK - 1),
                    tile_position=(0, rowoff)),
                    inc="pe")
                sevac_done[jj] = sc.op("act", lambda a, Sap=Sap, pSap=pSap: a.copy(
                    out=Sap, in_=pSap),
                    waits=[("pe", supd_done[jj])], inc="act")
                recip_done[jj] = sc.op("dve", lambda v, nb=nb, jj=jj: v.reciprocal(
                    out=rden3[:, (jj % 3) * 128:(jj % 3) * 128 + 128],
                    in_=nb[64:65, 0:128]),
                    waits=[("pe", supd_done[jj])], inc="dve")
            if j < NJ:
                mwts = [("pe", sc_done)]
                if j >= 2:
                    mwts.append(("pe", num1_done[j - 2]))
                mask_done[j] = sc.op("dve", lambda v, buf=j % 2: v.tensor_mul(
                    out=masked[buf][:], in0=psc[buf][:], in1=triu[:]),
                    waits=mwts, inc="dve")
            if j >= 2:
                jj = j - 2
                i, h, par, pair = jparts(jj)
                bw = [("pe", bc_done[jj])]
                if jj >= 1:
                    bw.append(("dve", ymult_done[jj - 1]))
                bcevac_done[jj] = sc.op("act", lambda a: a.copy(
                    out=bcast_sb[:], in_=pbc),
                    waits=bw, inc="act")
                ydst = y_fm[(h % 2) * 64:(h % 2) * 64 + 64,
                            (h // 2) * T + i * 128:(h // 2) * T + (i + 1) * 128]
                ymult_done[jj] = sc.op("dve", lambda v, nb=pnum3[jj % 3], ydst=ydst: v.tensor_mul(
                    out=ydst, in0=nb[0:64, 0:128], in1=bcast_sb[:]),
                    waits=[("act", bcevac_done[jj])], inc="dve")
        yl = sc.c("dve")

        if not fused:
            for p in range(2):
                sc.op("sync", lambda e, p=p: e.dma_start(
                    out=dram["yfm"][p * 128:(p + 1) * 128, :], in_=y_fm[:, p * T:(p + 1) * T]),
                    waits=[("dve", yl)], inc="st", dma=True)
            sc.op("sync", lambda e: None, waits=[("st", sc.c("st"))])
        else:
            a2a_in, a2a_out = dram["a2a1_in"], dram["a2a1_out"]
            for jj in range(8):
                for p in range(2):
                    sc.op("sync", lambda e, jj=jj, p=p: e.dma_start(
                        out=a2a_in[jj * 256 + p * 128: jj * 256 + (p + 1) * 128, :],
                        in_=y_fm[:, p * T + (jj % 4) * TPC: p * T + (jj % 4) * TPC + TPC]),
                        waits=[("dve", yl)], inc="st1", dma=True)
            def _a2a(g, a2a_in=a2a_in, a2a_out=a2a_out):
                with allow_ici_all_to_all():
                    return g.collective_compute(
                        "AllToAll", ALU.bypass, replica_groups=[list(range(8))],
                        ins=[a2a_in.opt()], outs=[a2a_out.opt()])
            sc.op("pool", _a2a, waits=[("st1", sc.c("st1"))], inc="cc")
        sc.flush(block)




def build_B(debug=False):
    nc = bass.Bass("TRN2", target_bir_lowering=False)
    dram = {}
    dram["yfm_sl"] = nc.dram_tensor("yfm_sl", [D, TPC], BF16, kind="ExternalInput").ap()
    dram["x_sl"] = nc.dram_tensor("x_sl", [TPC, D], F32, kind="ExternalInput").ap()
    dram["woT"] = nc.dram_tensor("woT", [D, D], F32, kind="ExternalInput").ap()
    dram["bo"] = nc.dram_tensor("bo", [1, D], F32, kind="ExternalInput").ap()
    dram["wgT"] = nc.dram_tensor("wgT", [D, 3072], I32, kind="ExternalInput").ap()
    dram["wuT"] = nc.dram_tensor("wuT", [D, 3072], I32, kind="ExternalInput").ap()
    dram["wdT"] = nc.dram_tensor("wdT", [3072, D], I32, kind="ExternalInput").ap()
    dram["x2o"] = nc.dram_tensor("x2o", [TPC, D], F32, kind="ExternalOutput").ap()
    dram["x1_scr"] = nc.dram_tensor("x1_scr", [TPC, D], F32).ap()
    dram["x1fm_scr"] = nc.dram_tensor("x1fm_scr", [D, TPC], BF16).ap()
    es0 = ExitStack()
    with es0:
        sc = Sched(nc, es0)
        block = es0.enter_context(nc.Block())
        phase_B(nc, sc, block, dram, fused=False)
    return nc


def phase_B(nc, sc, block, dram, fused):
    x_sl = dram["x_sl"]
    woT_d = dram.get("woT")
    bo_d = dram["bo"]
    wgT_d = dram["wgT"]
    wuT_d = dram["wuT"]
    wdT_d = dram["wdT"]
    x1_dram = dram["x1_scr"]
    x1fm_dram = dram["x1fm_scr"]
    if fused:
        yfm = dram["a2a1_out"]
        wo2_d = dram["wo2T"]
        NCJ = 16
        x2_dram = dram["x2_scr"]
        x2fm_dram = dram["ag2_in"]
    else:
        yfm = dram["yfm_sl"]
        wo2_d = dram["woT"]
        NCJ = 8
        x2_dram = None
        x2fm_dram = None
    es = ExitStack()
    with es:
        def sb(name, shape, dt=F32):
            return es.enter_context(nc.sbuf_tensor("B_" + name, shape, dt))

        def psum(name, shape, dt=F32):
            return es.enter_context(nc.psum_tensor("B_" + name, shape, dt))

        bo_bc = sb("bo_bc", [128, D], F32)
        bo_sb = sb("bo_sb", [1, D], F32)
        xres = sb("xres", [128, D], F32)
        pre = sb("pre", [128, D], F32)
        mean = sb("mean", [128, 1], F32)
        nmean = sb("nmean", [128, 1], F32)
        var = sb("var", [128, 1], F32)
        rstd = sb("rstd", [128, 1], F32)
        xout = sb("xout", [128, D], F32)
        xout_bf = sb("xout_bf", [128, D], BF16)
        ident = sb("ident", [128, 128], BF16)
        ones1 = sb("ones1", [1, 128], F32)
        x1fm_t = sb("x1fm_t", [128, 8 * 128], BF16)

        pb = [psum(f"pb{i}", [128, 512]) for i in range(6)]
        ptr = psum("ptr", [128, 128], BF16)

        make_ident(sc, ident)
        sc.op("pool", lambda g: g.memset(ones1[:], 1.0), inc="pool")
        init_done = sc.c("pool")

        es1 = ExitStack()
        with es1:
            yfm_t = es1.enter_context(nc.sbuf_tensor("B_yfm_t", [128, NCJ * 128], BF16))
            woT_sb = es1.enter_context(nc.sbuf_tensor("B_woT_sb", [128, NCJ * D], BF16))
            for j in range(NCJ):
                sc.op("pool", lambda g, j=j: g.dma_start(
                    out=woT_sb[:, j * D:(j + 1) * D],
                    in_=wo2_d[j * 128:(j + 1) * 128, :]), inc="ld", dma=True)
            ld_w = sc.c("ld")

            bv = sc.op("pool", lambda g: g.dma_start(out=bo_sb[:], in_=bo_d), inc="ld", dma=True)
            for oj in range(2):
                mm = sc.op("pe", lambda t, oj=oj: t.matmul(
                    pb[4][:, 0:512], ones1[:], bo_sb[:, oj * 512:(oj + 1) * 512],
                    start=True, stop=True),
                    waits=[("ld", bv), ("pool", init_done)] + ([("act", sc.c("act"))] if oj else []),
                    inc="pe")
                sc.op("act", lambda a, oj=oj: a.copy(
                    out=bo_bc[:, oj * 512:(oj + 1) * 512], in_=pb[4][:, 0:512]),
                    waits=[("pe", mm)], inc="act")
            bo_done = sc.c("act")

            NT1 = TPC // 128
            for tt in range(NT1):
                xv = sc.op("sync", lambda e, tt=tt: e.dma_start(
                    out=xres[:], in_=x_sl[tt * 128:(tt + 1) * 128, :]),
                    waits=([("dve", sc.c("dve"))] if tt else []), inc="ld2", dma=True)
                ccw = [("cc", dram["cc_wait"])] if fused else []
                yv = sc.op("sync", lambda e, tt=tt: e.dma_start(
                    out=yfm_t[:].rearrange("p (j t) -> p j t", j=NCJ),
                    in_=yfm[:, tt * 128:(tt + 1) * 128].rearrange("(j p) t -> p j t", p=128)),
                    waits=([("pe", sc.c("pe"))] if tt else []) + ccw, inc="ld2", dma=True)
                mms = []
                for oj in range(2):
                    for j in range(NCJ):
                        w = []
                        if j == 0:
                            w = [("ld", ld_w), ("ld2", max(xv, yv))]
                            if tt > 0:
                                w.append(("dve", sc.c("dve")))
                        mm = sc.op("pe", lambda t, oj=oj, j=j: t.matmul(
                            pb[oj][:],
                            yfm_t[:, j * 128:(j + 1) * 128],
                            woT_sb[:, j * D + oj * 512: j * D + (oj + 1) * 512],
                            start=(j == 0), stop=(j == NCJ - 1)),
                            waits=w, inc="pe" if j == NCJ - 1 else None)
                    mms.append(mm)
                for oj in range(2):
                    sc.op("dve", lambda v, oj=oj: v.tensor_add(
                        out=pre[:, oj * 512:(oj + 1) * 512], in0=pb[oj][:],
                        in1=bo_bc[:, oj * 512:(oj + 1) * 512]),
                        waits=[("pe", mms[oj]), ("act", bo_done)], inc="dve")
                v1 = sc.op("dve", lambda v: v.tensor_add(out=pre[:], in0=pre[:], in1=xres[:]),
                           waits=[("ld2", max(xv, yv))], inc="dve")
                v2 = sc.op("dve", lambda v: v.reduce_sum(out=mean[:, 0:1], in_=pre[:], axis=AXL.X),
                           inc="dve")
                v3 = sc.op("dve", lambda v: v.tensor_scalar_mul(
                    out=nmean[:, 0:1], in0=mean[:, 0:1], scalar1=-1.0 / D),
                    waits=[("dve", v2)], inc="dve")
                a1 = sc.op("act", lambda a: a.activation(
                    out=xout[:], in_=pre[:], func=AF.Square, bias=nmean[:, 0:1], scale=1.0,
                    accum_out=var[:, 0:1]),
                    waits=[("dve", v3), ("st1", sc.c("st1")), ("st", sc.c("st"))], inc="act")
                v4 = sc.op("dve", lambda v: v.tensor_scalar(
                    out=var[:, 0:1], in0=var[:, 0:1], scalar1=1.0 / D, scalar2=1e-5,
                    op0=ALU.mult, op1=ALU.add),
                    waits=[("act", a1)], inc="dve")
                a2 = sc.op("act", lambda a: a.activation(
                    out=rstd[:, 0:1], in_=var[:, 0:1], func=AF.Sqrt),
                    waits=[("dve", v4)], inc="act")
                v5 = sc.op("dve", lambda v: v.reciprocal(out=rstd[:, 0:1], in_=rstd[:, 0:1]),
                           waits=[("act", a2)], inc="dve")
                v6 = sc.op("dve", lambda v: v.tensor_scalar(
                    out=xout[:], in0=pre[:], scalar1=nmean[:, 0:1], scalar2=rstd[:, 0:1],
                    op0=ALU.add, op1=ALU.mult),
                    waits=[("dve", v5)], inc="dve")
                v7 = sc.op("dve", lambda v: v.tensor_copy(out=xout_bf[:], in_=xout[:]),
                           inc="dve")
                for sj in range(8):
                    tv = sc.op("pe", lambda t, sj=sj: t.transpose(
                        ptr[:], xout_bf[:, sj * 128:(sj + 1) * 128], ident[:]),
                        waits=[("dve", v7), ("pool", init_done)] +
                              ([("dve", sc.c("dve"))] if sj else []), inc="pe")
                    sc.op("dve", lambda v, sj=sj: v.tensor_copy(
                        out=x1fm_t[:, sj * 128:(sj + 1) * 128], in_=ptr[:]),
                        waits=[("pe", tv)], inc="dve")
                vlast = sc.c("dve")
                sc.op("sync", lambda e, tt=tt: e.dma_start(
                    out=x1_dram[tt * 128:(tt + 1) * 128, :], in_=xout[:]),
                    waits=[("dve", vlast)], inc="st1", dma=True)
                sc.op("sync", lambda e, tt=tt: e.dma_start(
                    out=x1fm_dram[:, tt * 128:(tt + 1) * 128].rearrange("(j p) t -> p j t", p=128),
                    in_=x1fm_t[:].rearrange("p (j t) -> p j t", j=8)),
                    waits=[("dve", vlast)], inc="st1", dma=True)

            b1_st = sc.c("st1")
            sc.flush(block)
        es2 = ExitStack()
        es2.__enter__()
        wg_sb = es2.enter_context(nc.sbuf_tensor("B_wg_sb", [128, 8 * 3072], BF16))
        wu_sb = es2.enter_context(nc.sbuf_tensor("B_wu_sb", [128, 8 * 3072], BF16))
        wd_sb = es2.enter_context(nc.sbuf_tensor("B_wd_sb", [128, 24 * D], BF16))
        stage_i = es2.enter_context(nc.sbuf_tensor("B_stage_i", [128, 512], I32))
        h_fm = es2.enter_context(nc.sbuf_tensor("B_h_fm", [128, 24 * 512], BF16))
        silu_t = es2.enter_context(nc.sbuf_tensor("B_silu_t", [128, 512], F32))
        x1fm_blk = es2.enter_context(nc.sbuf_tensor("B_x1fm_blk", [128, 8 * 512], BF16))
        for (wsb, wdr, nch, wid) in ((wg_sb, wgT_d, 8, 3072), (wu_sb, wuT_d, 8, 3072),
                                     (wd_sb, wdT_d, 24, D)):
            for j in range(nch):
                for cc2 in range(wid // 512):
                    lv = sc.op("pool", lambda g, wdr=wdr, j=j, cc2=cc2: g.dma_start(
                        out=stage_i[:], in_=wdr[j * 128:(j + 1) * 128, cc2 * 512:(cc2 + 1) * 512]),
                        waits=[("dve", sc.c("dve"))], inc="lst", dma=True)
                    sc.op("dve", lambda v, wsb=wsb, j=j, wid=wid, cc2=cc2: v.tensor_scalar_mul(
                        out=wsb[:, j * wid + cc2 * 512: j * wid + (cc2 + 1) * 512], in0=stage_i[:],
                        scalar1=1.0 / 127.0),
                        waits=[("lst", lv)], inc="dve")
        dq_done = sc.c("dve")

        NB = TPC // 512
        for bb in range(NB):
            xv = sc.op("sync", lambda e, bb=bb: e.dma_start(
                out=x1fm_blk[:].rearrange("p (j t) -> p j t", j=8),
                in_=x1fm_dram[:, bb * 512:(bb + 1) * 512].rearrange("(j p) t -> p j t", p=128)),
                waits=[("st1", b1_st)] + ([("dve", sc.c("dve"))] if bb else []),
                inc="ld2", dma=True)
            for hj in range(24):
                gmm = umm = None
                for j in range(8):
                    w = []
                    if j == 0:
                        w = [("ld2", xv), ("dve", dq_done)]
                        if hj >= 2:
                            w.append(("dve", sc.c("dve")))
                    gmm = sc.op("pe", lambda t, hj=hj, j=j: t.matmul(
                        pb[hj % 2][:],
                        wg_sb[:, j * 3072 + hj * 128: j * 3072 + (hj + 1) * 128],
                        x1fm_blk[:, j * 512:(j + 1) * 512],
                        start=(j == 0), stop=(j == 7)),
                        waits=w, inc="pe" if j == 7 else None)
                for j in range(8):
                    umm = sc.op("pe", lambda t, hj=hj, j=j: t.matmul(
                        pb[2 + hj % 2][:],
                        wu_sb[:, j * 3072 + hj * 128: j * 3072 + (hj + 1) * 128],
                        x1fm_blk[:, j * 512:(j + 1) * 512],
                        start=(j == 0), stop=(j == 7)),
                        inc="pe" if j == 7 else None)
                a1 = sc.op("act", lambda a, hj=hj: a.activation(
                    out=silu_t[:], in_=pb[hj % 2][:], func=AF.Silu),
                    waits=[("pe", gmm)] + ([("dve", sc.c("dve"))] if hj >= 1 else []),
                    inc="act")
                sc.op("dve", lambda v, hj=hj: v.tensor_mul(
                    out=h_fm[:, hj * 512:(hj + 1) * 512], in0=pb[2 + hj % 2][:],
                    in1=silu_t[:]),
                    waits=[("act", a1), ("pe", umm)], inc="dve")
            hdone = sc.c("dve")
            for st_i in range(4):
                tt = bb * 4 + st_i
                xv2 = sc.op("sync", lambda e, tt=tt: e.dma_start(
                    out=xres[:], in_=x1_dram[tt * 128:(tt + 1) * 128, :]),
                    waits=[("st1", b1_st), ("dve", sc.c("dve"))], inc="ld2", dma=True)
                dmm = []
                for oj in range(2):
                    for hj in range(24):
                        w = []
                        if hj == 0:
                            w = [("dve", hdone)]
                            w.append(("dve", sc.c("dve")))
                        mm = sc.op("pe", lambda t, oj=oj, hj=hj, st_i=st_i: t.matmul(
                            pb[4 + oj][:],
                            h_fm[:, hj * 512 + st_i * 128: hj * 512 + (st_i + 1) * 128],
                            wd_sb[:, hj * D + oj * 512: hj * D + (oj + 1) * 512],
                            start=(hj == 0), stop=(hj == 23)),
                            waits=w, inc="pe" if hj == 23 else None)
                    dmm.append(mm)
                for oj in range(2):
                    sc.op("dve", lambda v, oj=oj: v.tensor_add(
                        out=pre[:, oj * 512:(oj + 1) * 512], in0=pb[4 + oj][:],
                        in1=xres[:, oj * 512:(oj + 1) * 512]),
                        waits=[("pe", dmm[oj]), ("ld2", xv2)], inc="dve")
                v2 = sc.op("dve", lambda v: v.reduce_sum(out=mean[:, 0:1], in_=pre[:], axis=AXL.X),
                           inc="dve")
                v3 = sc.op("dve", lambda v: v.tensor_scalar_mul(
                    out=nmean[:, 0:1], in0=mean[:, 0:1], scalar1=-1.0 / D),
                    waits=[("dve", v2)], inc="dve")
                a1 = sc.op("act", lambda a: a.activation(
                    out=xout[:], in_=pre[:], func=AF.Square, bias=nmean[:, 0:1], scale=1.0,
                    accum_out=var[:, 0:1]),
                    waits=[("dve", v3), ("st", sc.c("st")), ("st1", sc.c("st1"))], inc="act")
                v4 = sc.op("dve", lambda v: v.tensor_scalar(
                    out=var[:, 0:1], in0=var[:, 0:1], scalar1=1.0 / D, scalar2=1e-5,
                    op0=ALU.mult, op1=ALU.add),
                    waits=[("act", a1)], inc="dve")
                a2 = sc.op("act", lambda a: a.activation(
                    out=rstd[:, 0:1], in_=var[:, 0:1], func=AF.Sqrt),
                    waits=[("dve", v4)], inc="act")
                v5 = sc.op("dve", lambda v: v.reciprocal(out=rstd[:, 0:1], in_=rstd[:, 0:1]),
                           waits=[("act", a2)], inc="dve")
                v6 = sc.op("dve", lambda v: v.tensor_scalar(
                    out=xout[:], in0=pre[:], scalar1=nmean[:, 0:1], scalar2=rstd[:, 0:1],
                    op0=ALU.add, op1=ALU.mult),
                    waits=[("dve", v5)], inc="dve")
                if not fused:
                    sc.op("sync", lambda e, tt=tt: e.dma_start(
                        out=dram["x2o"][tt * 128:(tt + 1) * 128, :], in_=xout[:]),
                        waits=[("dve", v6)], inc="st", dma=True)
                else:
                    sc.op("sync", lambda e, tt=tt: e.dma_start(
                        out=x2_dram[tt * 128:(tt + 1) * 128, :], in_=xout[:]),
                        waits=[("dve", v6)], inc="st1", dma=True)
                    v7 = sc.op("dve", lambda v: v.tensor_copy(out=xout_bf[:], in_=xout[:]),
                               waits=[("dve", v6)], inc="dve")
                    for sj in range(8):
                        tv = sc.op("pe", lambda t, sj=sj: t.transpose(
                            ptr[:], xout_bf[:, sj * 128:(sj + 1) * 128], ident[:]),
                            waits=[("dve", sc.c("dve"))], inc="pe")
                        sc.op("dve", lambda v, sj=sj: v.tensor_copy(
                            out=x1fm_t[:, sj * 128:(sj + 1) * 128], in_=ptr[:]),
                            waits=[("pe", tv)], inc="dve")
                    vl2 = sc.c("dve")
                    sc.op("sync", lambda e, tt=tt: e.dma_start(
                        out=x2fm_dram[:, tt * 128:(tt + 1) * 128].rearrange("(j p) t -> p j t", p=128),
                        in_=x1fm_t[:].rearrange("p (j t) -> p j t", j=8)),
                        waits=[("dve", vl2)], inc="st1", dma=True)
        if not fused:
            sc.op("sync", lambda e: None, waits=[("st", sc.c("st"))])
        else:
            sc.op("pool", lambda g, ai=dram["ag2_in"], ao=dram["ag2_out"]: g.collective_compute(
                "AllGather", ALU.bypass, replica_groups=[[0, 1, 2, 3], [4, 5, 6, 7]],
                ins=[ai.opt()], outs=[ao.opt()]),
                waits=[("st1", sc.c("st1"))], inc="cc")
        sc.flush(block)
        es2.__exit__(None, None, None)




def build_C(sign=1.0):
    nc = bass.Bass("TRN2", target_bir_lowering=False)
    dram = {}
    dram["x2fm"] = nc.dram_tensor("x2fm", [D, T], F32, kind="ExternalInput").ap()
    for w in ("q", "k", "v"):
        dram[f"w{w}C"] = nc.dram_tensor(f"w{w}C", [D, HPC * HD], F32, kind="ExternalInput").ap()
    dram["bqkvC"] = nc.dram_tensor("bqkvC", [3, HPC * HD], F32, kind="ExternalInput").ap()
    dram["gain"] = nc.dram_tensor("gain", [1, 1], F32, kind="ExternalInput").ap()
    dram["yfmC"] = nc.dram_tensor("yfm", [HPC * HD, T], BF16, kind="ExternalOutput").ap()
    es0 = ExitStack()
    with es0:
        sc = Sched(nc, es0)
        block = es0.enter_context(nc.Block())
        phase_C(nc, sc, block, dram, sign, fused=False)
    return nc


def phase_C(nc, sc, block, dram, sign, fused):
    wC = {w: dram[f"w{w}C"] for w in ("q", "k", "v")}
    bqkv = dram["bqkvC"]
    gain_d = dram["gain"]
    xfm = None if fused else dram["x2fm"]
    ag2_out = dram.get("ag2_out")

    TT = 512
    NTT = T // TT
    NG = T // 64
    NQT = T // 512
    es = ExitStack()
    with es:
        def sb(name, shape, dt=F32):
            return es.enter_context(nc.sbuf_tensor("C_" + name, shape, dt))

        bias_sb = sb("bias", [128, 6], F32)
        w_sb = {w: sb("w" + w, [128, 8 * 256], BF16) for w in ("q", "k", "v")}
        x_t = [sb(f"x_t{i}", [128, 8 * TT], BF16) for i in range(2)]
        q65 = [sb(f"q65_{i}", [65, T], BF16) for i in range(2)]
        k65 = [sb(f"k65_{i}", [65, T], BF16) for i in range(2)]
        k_fm = sb("k_fm", [128, T], BF16)
        q_blk = sb("q_blk", [128, 2 * T], BF16)
        vt = sb("vt", [128, 2 * NCHUNK * 65], BF16)
        ident = sb("ident", [128, 128], BF16)
        identf = sb("identf", [128, 128], F32)
        onesf = sb("onesf", [1, 128], F32)
        ksum = sb("ksum", [64, 2], F32)
        ksumb = sb("ksumb", [64, 2], BF16)
        ksum_pair = sb("ksum_pair", [128, 1], BF16)
        negmu_i = sb("negmu_i", [128, NG], F32)
        absum_i = sb("absum_i", [128, NG], F32)
        epsc_i = sb("epsc_i", [128, NG], F32)
        acc8 = [sb(f"acc8_{i}", [128, 8], F32) for i in range(2)]
        uabs = sb("uabs", [128, 1024], BF16)
        uabs_d = sb("uabs_d", [128, 1024], BF16)
        v_stage = sb("v_stage", [128, 512], F32)
        gaincol = sb("gaincol", [128, 1], F32)
        g4inv = sb("g4inv", [128, 1], F32)
        scr = sb("scr", [128, 2], F32)
        madt = sb("madt", [128, NG], F32)
        m2t = sb("m2t", [128, NG], F32)
        u_t = [sb(f"u_t{i}", [128, 1024], BF16) for i in range(3)]
        r_t = [sb(f"r_t{i}", [128, 1024], BF16) for i in range(3)]
        dmask = [sb(f"dmask{i}", [128, 512], BF16) for i in range(4)]
        epsrow = [sb(f"epsrow{i}", [1, 512], F32) for i in range(2)]
        denrow = [sb(f"denrow{i}", [1, 512], F32) for i in range(2)]
        rden = [sb(f"rdenC{i}", [1, 512], F32) for i in range(2)]
        bcast_sb = [sb(f"bcastC{i}", [64, 512], F32) for i in range(2)]
        y_t = [sb(f"y_t{i}", [128, 512], BF16) for i in range(2)]

        pa = es.enter_context(nc.psum_tensor("C_pa", [128, 2048], F32))
        pa2 = es.enter_context(nc.psum_tensor("C_pa2", [128, 2048], F32))
        ptrf = pa2[:, 1536:1664]
        pproj = {0: [pa[:, 0:512], pa[:, 512:1024], pa[:, 1024:1536]],
                 1: [pa2[:, 0:512], pa2[:, 512:1024], pa2[:, 1024:1536]]}
        psF = [pa[:, 0:1024], pa[:, 1024:2048]]
        psc2 = [pa[:, 0:1024], pa[:, 1024:2048]]
        py = [pa2[0:65, 0:512], pa2[0:65, 512:1024]]
        prow = [pa2[:, 1024:1536], pa2[:, 1536:2048]]
        pmucol = pa2[:, 1536:1664]
        pgain = pa2[:, 2047:2048]

        make_ident(sc, ident)
        v = sc.op("pool", lambda g: g.memset(identf[:], 1.0), inc="pool")
        sc.op("pool", lambda g: g.affine_select(
            out=identf[:], in_=identf[:], pattern=[[-1, 128]],
            compare_op=ALU.is_equal, fill=0.0, base=0, channel_multiplier=1),
            waits=[("pool", v)], inc="pool")
        sc.op("pool", lambda g: g.memset(onesf[:], 1.0), inc="pool")
        sc.op("pool", lambda g: g.memset(vt[:], 1.0), inc="pool")
        sc.op("pool", lambda g: g.memset(q_blk[:], 0.0), inc="pool")
        for i in range(2):
            sc.op("pool", lambda g, i=i: g.memset(k65[i][:], 1.0), inc="pool")
        for j in range(4):
            v = sc.op("pool", lambda g, j=j: g.memset(dmask[j][:], 1.0), inc="pool")
            v = sc.op("pool", lambda g, j=j: g.affine_select(
                out=dmask[j][:], in_=dmask[j][:], pattern=[[1, 512]],
                compare_op=ALU.is_ge, fill=0.0, base=-128 * j, channel_multiplier=-1),
                waits=[("pool", v)], inc="pool")
        init_done = sc.c("pool")

        for w in ("q", "k", "v"):
            for j in range(8):
                sc.op("pool", lambda g, w=w, j=j: g.dma_start(
                    out=w_sb[w][:, j * 256:(j + 1) * 256],
                    in_=wC[w][j * 128:(j + 1) * 128, :]), inc="ld", dma=True)
        for wi in range(3):
            for hh in range(2):
                sc.op("pool", lambda g, wi=wi, hh=hh: g.dma_start(
                    out=bias_sb[:, wi * 2 + hh: wi * 2 + hh + 1],
                    in_=bqkv[wi:wi + 1, hh * 128:(hh + 1) * 128].rearrange("a e -> e a")),
                    inc="ld", dma=True)
        gv = sc.op("pool", lambda g: g.dma_start(out=scr[0:1, 0:1], in_=gain_d),
                   inc="ld", dma=True)
        ld_w = sc.c("ld")

        gmm = sc.op("pe", lambda t: t.matmul(
            pgain, onesf[:], scr[0:1, 0:1], start=True, stop=True),
            waits=[("ld", gv), ("pool", init_done)], inc="pe")
        a_g = sc.op("act", lambda a: a.copy(out=gaincol[:, 0:1], in_=pgain),
                    waits=[("pe", gmm)], inc="act")
        v1 = sc.op("dve", lambda v: v.tensor_mul(out=scr[:, 0:1], in0=gaincol[:, 0:1],
                                                 in1=gaincol[:, 0:1]),
                   waits=[("act", a_g)], inc="dve")
        v2 = sc.op("dve", lambda v: v.tensor_mul(out=scr[:, 1:2], in0=scr[:, 0:1],
                                                 in1=scr[:, 0:1]),
                   waits=[("dve", v1)], inc="dve")
        g4_done = sc.op("dve", lambda v: v.reciprocal(out=g4inv[:, 0:1], in_=scr[:, 1:2]),
                        waits=[("dve", v2)], inc="dve")

        pair_end = None
        yst = {}
        for pair in range(C_PAIRS):
            if pair == 1:
                sc.op("pe", lambda t: t.nop(), waits=pair_end, inc="pe")
            pe_tile_done = {}
            dve_tile_done = {}
            for ti in range(NTT):
                buf = ti % 2
                if ti >= 2:
                    waits = [("pe", pe_tile_done[ti - 2])]
                elif pair == 1:
                    waits = [("pe", prev_proj_pe)]
                else:
                    waits = []
                if fused:
                    rr = ti // 4
                    t0 = (ti % 4) * TT
                    xv = sc.op("pool", lambda g, buf=buf, rr=rr, t0=t0: g.dma_start(
                        out=x_t[buf][:].rearrange("p (j t) -> p j t", j=8),
                        in_=ag2_out[rr * D:(rr + 1) * D, t0:t0 + TT].rearrange("(j p) t -> p j t", p=128)),
                        waits=waits + [("cc", dram["cc_wait"])], inc=f"lx{buf}", dma=True)
                else:
                    xv = sc.op("pool", lambda g, buf=buf, ti=ti: g.dma_start(
                        out=x_t[buf][:].rearrange("p (j t) -> p j t", j=8),
                        in_=xfm[:, ti * TT:(ti + 1) * TT].rearrange("(j p) t -> p j t", p=128)),
                        waits=waits, inc=f"lx{buf}", dma=True)
                for wi, w in enumerate(("q", "k", "v")):
                    bank = pproj[buf][wi]
                    bcol = bias_sb[:, wi * 2 + pair: wi * 2 + pair + 1]
                    for j in range(8):
                        wts = []
                        if j == 0:
                            wts = [("ld", ld_w), (f"lx{buf}", xv)]
                            if ti > 0:
                                wts.append(("dve", dve_tile_done[ti - 1]))
                        mm = sc.op("pe", lambda t, bank=bank, w=w, j=j, buf=buf, pair=pair: t.matmul(
                            bank, w_sb[w][:, j * 256 + pair * 128: j * 256 + pair * 128 + 128],
                            x_t[buf][:, j * TT:(j + 1) * TT],
                            start=(j == 0), stop=(j == 7)),
                            waits=wts, inc="pe" if j == 7 else None)
                    if w == "q":
                        for hh in range(2):
                            dst = q65[hh][0:64, ti * TT:(ti + 1) * TT]
                            sc.op("dve", lambda v, bank=bank, bcol=bcol, hh=hh, dst=dst: v.tensor_scalar(
                                out=dst, in0=bank[hh * 64:(hh + 1) * 64, :],
                                scalar1=bcol[hh * 64:(hh + 1) * 64, :], scalar2=float(sign),
                                op0=ALU.add, op1=ALU.mult),
                                waits=[("pe", mm), ("ld", ld_w)], inc="dve")
                            dstb = q_blk[hh * 64:(hh + 1) * 64, ti * 2 * TT: (ti + 1) * 2 * TT] \
                                .rearrange("p (g c) -> p g c", c=128)[:, :, hh * 64:(hh + 1) * 64]
                            sc.op("dve", lambda v, bank=bank, bcol=bcol, hh=hh, dstb=dstb: v.tensor_scalar(
                                out=dstb,
                                in0=bank[hh * 64:(hh + 1) * 64, :].rearrange("p (g c) -> p g c", c=64),
                                scalar1=bcol[hh * 64:(hh + 1) * 64, :], scalar2=float(sign),
                                op0=ALU.add, op1=ALU.mult),
                                inc="dve")
                    elif w == "k":
                        for hh in range(2):
                            dst = k65[hh][0:64, ti * TT:(ti + 1) * TT]
                            sc.op("dve", lambda v, bank=bank, bcol=bcol, hh=hh, dst=dst: v.tensor_scalar_add(
                                out=dst, in0=bank[hh * 64:(hh + 1) * 64, :],
                                scalar1=bcol[hh * 64:(hh + 1) * 64, :]),
                                waits=[("pe", mm), ("ld", ld_w)], inc="dve")
                        sc.op("dve", lambda v, bank=bank, bcol=bcol, ti=ti: v.tensor_scalar_add(
                            out=k_fm[:, ti * TT:(ti + 1) * TT], in0=bank, scalar1=bcol),
                            inc="dve")
                    else:
                        sc.op("dve", lambda v, bank=bank, bcol=bcol: v.tensor_scalar_add(
                            out=v_stage[:], in0=bank, scalar1=bcol),
                            waits=[("pe", mm), ("ld", ld_w)], inc="dve")
                        for sti in range(4):
                            c = ti * 4 + sti
                            tv = sc.op("pe", lambda t, sti=sti: t.transpose(
                                ptrf, v_stage[:, sti * 128:(sti + 1) * 128], identf[:]),
                                waits=[("dve", sc.c("dve")), ("pool", init_done)], inc="pe")
                            for hh in range(2):
                                sc.op("dve", lambda v, hh=hh, c=c: v.tensor_copy(
                                    out=vt[:, (hh * NCHUNK + c) * 65:(hh * NCHUNK + c) * 65 + 64],
                                    in_=ptrf[:, hh * 64:(hh + 1) * 64]),
                                    waits=[("pe", tv)], inc="dve")
                pe_tile_done[ti] = sc.c("pe")
                dve_tile_done[ti] = sc.c("dve")
            proj_dve = sc.c("dve")
            prev_proj_pe = sc.c("pe")

            for hh in range(2):
                r0 = sc.op("dve", lambda v, hh=hh: v.reduce_sum(
                    out=ksum[:, hh:hh + 1], in_=k65[hh][0:64, :], axis=AXL.X),
                    waits=[("dve", proj_dve)], inc="dve")
                sc.op("dve", lambda v, hh=hh: v.tensor_copy(
                    out=ksumb[:, hh:hh + 1], in_=ksum[:, hh:hh + 1]),
                    waits=[("dve", r0)], inc="dve")
                sc.op("dve", lambda v, hh=hh: v.tensor_copy(
                    out=ksum_pair[hh * 64:(hh + 1) * 64, 0:1], in_=ksum[:, hh:hh + 1]),
                    inc="dve")
            ksum_done = sc.c("dve")

            for g in range(NG):
                w = []
                if g == 0:
                    w = [("dve", ksum_done), ("act", a_g)]
                mm = sc.op("pe", lambda t, g=g: t.matmul(
                    pmucol[:, g:g + 1], q_blk[:, g * 128:(g + 1) * 128], ksum_pair[:],
                    start=True, stop=True),
                    waits=w, inc="pe" if g == NG - 1 else None)
            negmu_ev = sc.op("act", lambda a: a.activation(
                out=negmu_i[:], in_=pmucol[:, 0:NG], func=AF.Copy, scale=-1.0 / T),
                waits=[("pe", mm)], inc="act")

            murow_ev = {}
            for hh in range(2):
                for tt in range(NQT):
                    idx = hh * NQT + tt
                    bankr = prow[idx % 2][0:1, 0:512]
                    w = [("act", negmu_ev)] if idx < 2 else [("act", murow_ev[idx - 2])]
                    mm = sc.op("pe", lambda t, hh=hh, tt=tt, bankr=bankr: t.matmul(
                        bankr, ksumb[:, hh:hh + 1], q65[hh][0:64, tt * 512:(tt + 1) * 512],
                        start=True, stop=True),
                        waits=w, inc="pe")
                    murow_ev[idx] = sc.op("act", lambda a, hh=hh, tt=tt, bankr=bankr: a.activation(
                        out=q65[hh][64:65, tt * 512:(tt + 1) * 512], in_=bankr,
                        func=AF.Copy, scale=-1.0 / T),
                        waits=[("pe", mm)], inc="act")
            murow_act = sc.c("act")

            DVE_PAT = (0, 3, 6, 9, 12)
            last_read = {0: None, 1: None}
            red_done = {}
            for g in range(NG):
                gpar = g % 2
                first_act = True
                first_dve = True
                for kq in range(8):
                    t_i = g * 8 + kq
                    buf = t_i % 2
                    use_dve = (t_i % 16) in DVE_PAT
                    for kk in range(2):
                        w = []
                        if kk == 0:
                            if last_read[buf] is not None:
                                w.append(last_read[buf])
                            if g == 0 and kq == 0:
                                w.append(("act", murow_act))
                        mm = sc.op("pe", lambda t, g=g, buf=buf, kq=kq, kk=kk: t.matmul(
                            psF[buf][:, kk * 512:(kk + 1) * 512],
                            q_blk[:, g * 128:(g + 1) * 128],
                            k_fm[:, (kq * 2 + kk) * 512:(kq * 2 + kk) * 512 + 512],
                            start=True, stop=True),
                            waits=w, inc="pe" if kk == 1 else None)
                    if use_dve:
                        w = [("pe", mm)]
                        if g >= 2 and first_dve:
                            w.append(("dve", red_done[g - 2]))
                        first_dve = False
                        rd = sc.op("dve", lambda v, buf=buf, g=g: v.tensor_scalar(
                            out=uabs_d[:], in0=psF[buf],
                            scalar1=negmu_i[:, g:g + 1], scalar2=None,
                            op0=ALU.add, op1=ALU.bypass),
                            waits=w, inc="dve")
                        last_read[buf] = ("dve", rd)
                        sc.op("dve", lambda v, gpar=gpar, kq=kq: v.reduce_sum(
                            out=acc8[gpar][:, kq:kq + 1], in_=uabs_d[:],
                            axis=AXL.X, apply_absolute_value=True),
                            inc="dve")
                    else:
                        w = [("pe", mm)]
                        if g >= 2 and first_act:
                            w.append(("dve", red_done[g - 2]))
                        first_act = False
                        rd = sc.op("act", lambda a, buf=buf, g=g, gpar=gpar, kq=kq: a.activation(
                            out=uabs[:], in_=psF[buf],
                            func=AF.Abs, bias=negmu_i[:, g:g + 1], scale=1.0,
                            accum_out=acc8[gpar][:, kq:kq + 1]),
                            waits=w, inc="act")
                        last_read[buf] = ("act", rd)
                w = [("act", sc.c("act")), ("dve", sc.c("dve"))]
                red_done[g] = sc.op("dve", lambda v, gpar=gpar, g=g: v.reduce_sum(
                    out=absum_i[:, g:g + 1], in_=acc8[gpar][:], axis=AXL.X),
                    waits=w, inc="dve")
            stats_dve = sc.c("dve")
            stats_act = sc.c("act")

            w1 = sc.op("dve", lambda v: v.tensor_scalar(
                out=madt[:], in0=absum_i[:], scalar1=1.0 / T, scalar2=1e-6,
                op0=ALU.mult, op1=ALU.add), waits=[("dve", stats_dve)], inc="dve")
            sc.op("dve", lambda v: v.tensor_mul(out=m2t[:], in0=madt[:], in1=madt[:]),
                  waits=[("dve", w1)], inc="dve")
            sc.op("dve", lambda v: v.tensor_mul(out=madt[:], in0=m2t[:], in1=m2t[:]),
                  inc="dve")
            eps_done = sc.op("dve", lambda v: v.tensor_scalar(
                out=epsc_i[:], in0=madt[:], scalar1=g4inv[:, 0:1], scalar2=1e-6,
                op0=ALU.mult, op1=ALU.mult),
                waits=[("dve", g4_done)], inc="dve")

            relu_done = {}
            u_free = {}
            r_free = {}
            sq_done = {}
            bc_free = {}
            avmm = {}
            eps_ev = {}
            eps_free = {}
            prow_free = {}
            ymult_done = 0
            for qt in range(NQT):
                for hh in range(2):
                    ro = hh * 64
                    for jj in range(8):
                        g = qt * 8 + jj
                        w = []
                        if jj == 0:
                            w = [("dve", eps_done)]
                            if hh in prow_free:
                                w.append(prow_free[hh])
                        tv = sc.op("pe", lambda t, hh=hh, g=g, jj=jj, ro=ro: t.transpose(
                            prow[hh][0:1, jj * 64:(jj + 1) * 64],
                            epsc_i[ro:ro + 64, g:g + 1], identf[ro:ro + 64, ro:ro + 64]),
                            waits=w, inc="pe" if jj == 7 else None)
                    eps_ev[hh] = sc.op("act", lambda a, hh=hh: a.copy(
                        out=epsrow[hh][:], in_=prow[hh][0:1, 0:512]),
                        waits=[("pe", tv)] +
                              ([("dve", eps_free[hh])] if hh in eps_free else []),
                        inc="act")

                kcmax = 4 * qt + 4

                def sched_av(kc, kcmax=kcmax, qt=qt):
                    buf3 = kc % 3
                    for hh in range(2):
                        w = [("dve", sq_done[(kc, "r")])]
                        if kc == 0:
                            w.append(("dve", ymult_done))
                        avmm[hh] = sc.op("pe", lambda t, hh=hh, kc=kc, buf3=buf3, kcmax=kcmax: t.matmul(
                            py[hh],
                            vt[:, (hh * NCHUNK + kc) * 65:(hh * NCHUNK + kc) * 65 + 65],
                            r_t[buf3][:, hh * 512:(hh + 1) * 512],
                            start=(kc == 0), stop=(kc == kcmax - 1)),
                            waits=(w if hh == 0 else []), inc="pe")
                    r_free[buf3] = avmm[1]

                for kc in range(kcmax):
                    buf = kc % 2
                    buf3 = kc % 3
                    w = []
                    if (buf,) in relu_done:
                        w.append(("act", relu_done[(buf,)]))
                    elif qt == 0 and kc < 2:
                        w.append(("dve", stats_dve))
                        w.append(("act", stats_act))
                    for hh in range(2):
                        mm = sc.op("pe", lambda t, buf=buf, hh=hh, kc=kc, qt=qt: t.matmul(
                            psc2[buf][:, hh * 512:(hh + 1) * 512],
                            k65[hh][:, kc * 128:(kc + 1) * 128],
                            q65[hh][:, qt * 512:(qt + 1) * 512],
                            start=True, stop=True),
                            waits=(w if hh == 0 else []), inc="pe")
                    if kc >= 2:
                        sched_av(kc - 2)
                    rv = sc.op("act", lambda a, buf=buf, buf3=buf3: a.activation(
                        out=u_t[buf3][:], in_=psc2[buf][:], func=AF.Relu),
                        waits=[("pe", mm)] +
                              ([("dve", u_free[buf3])] if buf3 in u_free else []),
                        inc="act")
                    relu_done[(buf,)] = rv
                    d1 = sc.op("dve", lambda v, buf3=buf3: v.tensor_mul(
                        out=u_t[buf3][:], in0=u_t[buf3][:], in1=u_t[buf3][:]),
                        waits=[("act", rv)], inc="dve")
                    w = []
                    if buf3 in r_free:
                        w.append(("pe", r_free[buf3]))
                    d2 = sc.op("dve", lambda v, buf3=buf3: v.tensor_mul(
                        out=r_t[buf3][:], in0=u_t[buf3][:], in1=u_t[buf3][:]),
                        waits=w, inc="dve")
                    u_free[buf3] = d2
                    if kc >= 4 * qt:
                        for hh in range(2):
                            d2 = sc.op("dve", lambda v, buf3=buf3, kc=kc, qt=qt, hh=hh: v.tensor_mul(
                                out=r_t[buf3][:, hh * 512:(hh + 1) * 512],
                                in0=r_t[buf3][:, hh * 512:(hh + 1) * 512],
                                in1=dmask[kc - 4 * qt][:]),
                                inc="dve")
                    sq_done[(kc, "r")] = d2
                sched_av(kcmax - 2)
                sched_av(kcmax - 1)
                for hh in range(2):
                    d1 = sc.op("dve", lambda v, hh=hh: v.tensor_add(
                        out=denrow[hh][:], in0=py[hh][64:65, 0:512], in1=epsrow[hh][:]),
                        waits=[("act", eps_ev[hh]), ("pe", avmm[hh])], inc="dve")
                    eps_free[hh] = d1
                    bmm = sc.op("pe", lambda t, hh=hh: t.matmul(
                        prow[hh][0:64, 0:512], onesf[0:1, 0:64], denrow[hh][:],
                        start=True, stop=True),
                        waits=[("dve", d1), ("act", eps_ev[hh])], inc="pe")
                    bev = sc.op("act", lambda a, hh=hh: a.copy(
                        out=bcast_sb[hh][:], in_=prow[hh][0:64, 0:512]),
                        waits=[("pe", bmm)] +
                              ([("dve", bc_free[hh])] if hh in bc_free else []),
                        inc="act")
                    prow_free[hh] = ("act", bev)
                    d2 = sc.op("dve", lambda v, hh=hh: v.reciprocal(
                        out=bcast_sb[hh][:], in_=bcast_sb[hh][:]),
                        waits=[("act", bev)], inc="dve")
                    rot = qt % 2
                    gq = pair * NQT + qt
                    wy = [("dve", d2)]
                    if gq >= 2 and hh == 0:
                        wy.append(("st1", yst[gq - 2]))
                    ym = sc.op("dve", lambda v, hh=hh, rot=rot: v.tensor_mul(
                        out=y_t[rot][hh * 64:(hh + 1) * 64, :],
                        in0=py[hh][0:64, 0:512], in1=bcast_sb[hh][:]),
                        waits=wy, inc="dve")
                    bc_free[hh] = ym
                    ymult_done = ym
                rot = qt % 2
                gq = pair * NQT + qt
                if not fused:
                    yst[gq] = sc.op("sync", lambda e, qt=qt, rot=rot, pair=pair: e.dma_start(
                        out=dram["yfmC"][pair * 128:(pair + 1) * 128, qt * 512:(qt + 1) * 512],
                        in_=y_t[rot][:]),
                        waits=[("dve", ymult_done)], inc="st1", dma=True)
                else:
                    qq = qt // 4
                    c0 = (qt % 4) * 512
                    for tgt in (qq, qq + 4):
                        yst[gq] = sc.op("sync", lambda e, tgt=tgt, c0=c0, rot=rot, pair=pair: e.dma_start(
                            out=dram["a2a3_in"][tgt * 256 + pair * 128: tgt * 256 + (pair + 1) * 128,
                                                c0:c0 + 512],
                            in_=y_t[rot][:]),
                            waits=[("dve", ymult_done)] if tgt == qq else [], inc="st1", dma=True)
            pair_end = [("act", sc.c("act")), ("dve", sc.c("dve")), ("pool", sc.c("pool")),
                        ("st1", sc.c("st1"))]

        if not fused:
            sc.op("sync", lambda e: None, waits=[("st1", sc.c("st1"))])
        else:
            def _a2a(g, a2a_in=dram["a2a3_in"], a2a_out=dram["a2a3_out"]):
                with allow_ici_all_to_all():
                    return g.collective_compute(
                        "AllToAll", ALU.bypass, replica_groups=[list(range(8))],
                        ins=[a2a_in.opt()], outs=[a2a_out.opt()])
            sc.op("pool", _a2a, waits=[("st1", sc.c("st1"))], inc="cc")
        sc.flush(block)





def build_D(stage=4):
    nc = bass.Bass("TRN2", target_bir_lowering=False)
    dram = {}
    dram["y1fm_sl"] = nc.dram_tensor("y1fm_sl", [D, TPC], BF16, kind="ExternalInput").ap()
    dram["x2_sl"] = nc.dram_tensor("x2_sl", [TPC, D], F32, kind="ExternalInput").ap()
    dram["wproj"] = nc.dram_tensor("wproj", [D, D], F32, kind="ExternalInput").ap()
    dram["bproj"] = nc.dram_tensor("bproj", [1, D], F32, kind="ExternalInput").ap()
    dram["wLT"] = nc.dram_tensor("wLT", [D, D], F32, kind="ExternalInput").ap()
    dram["bL"] = nc.dram_tensor("bL", [1, D], F32, kind="ExternalInput").ap()
    dram["ldT"] = nc.dram_tensor("ldT", [100, 4], F32, kind="ExternalInput").ap()
    dram["da"] = nc.dram_tensor("da", [100, D], F32, kind="ExternalInput").ap()
    dram["luT"] = nc.dram_tensor("luT", [4, 50], F32, kind="ExternalInput").ap()
    dram["uaT"] = nc.dram_tensor("uaT", [50, D], F32, kind="ExternalInput").ap()
    dram["x4o"] = nc.dram_tensor("x4o", [TPC, D], F32, kind="ExternalOutput").ap()
    dram["x3_scr"] = nc.dram_tensor("x3_scr", [TPC, D], F32).ap()
    dram["x3fm_scr"] = nc.dram_tensor("x3fm_scr", [D, TPC], BF16).ap()
    es0 = ExitStack()
    with es0:
        sc = Sched(nc, es0)
        block = es0.enter_context(nc.Block())
        phase_D(nc, sc, block, dram, fused=False, stage=stage)
    return nc


def phase_D(nc, sc, block, dram, fused, stage=4):
    bp_d = dram["bproj"]
    wl_d = dram["wLT"]
    bl_d = dram["bL"]
    ldT_d = dram["ldT"]
    da_d = dram["da"]
    luT_d = dram["luT"]
    uaT_d = dram["uaT"]
    x4o = dram["x4o"]
    x3_dram = dram["x3_scr"]
    x3fm_dram = dram["x3fm_scr"]
    if fused:
        yfm = dram["a2a3_out"]
        wp_d = dram["wp2"]
        NCJ = 16
        x_sl = dram["x2_scr"]
    else:
        yfm = dram["y1fm_sl"]
        wp_d = dram["wproj"]
        NCJ = 8
        x_sl = dram["x2_sl"]

    es = ExitStack()
    with es:
        def sb(name, shape, dt=F32):
            return es.enter_context(nc.sbuf_tensor("D_" + name, shape, dt))

        yfm_t = sb("yfm_t", [128, NCJ * 128], BF16)
        wp_sb = sb("wp_sb", [128, NCJ * D], BF16)
        weff = sb("weff", [128, 8 * D], BF16)
        bp_bc = sb("bp_bc", [128, D], F32)
        bl_bc = sb("bl_bc", [128, D], F32)
        bp_sb = sb("bp_sb", [1, D], F32)
        bl_sb = sb("bl_sb", [1, D], F32)
        xres = sb("xres", [128, D], F32)
        pre = sb("pre", [128, D], F32)
        mean = sb("mean", [128, 1], F32)
        nmean = sb("nmean", [128, 1], F32)
        var = sb("var", [128, 1], F32)
        rstd = sb("rstd", [128, 1], F32)
        xout = sb("xout", [128, D], F32)
        xout_bf = sb("xout_bf", [128, D], BF16)
        ident = sb("ident", [128, 128], BF16)
        onesf = sb("onesf", [1, 128], F32)
        x3fm_t = sb("x3fm_t", [128, 8 * 128], BF16)
        ldT_sb = sb("ldT_sb", [100, 4], BF16)
        da_sb = sb("da_sb", [100, D], BF16)
        luT_sb = sb("luT_sb", [4, 50], BF16)
        uaT_sb = sb("uaT_sb", [50, D], BF16)
        a1_sb = sb("a1_sb", [4, D], BF16)
        a2_sb = sb("a2_sb", [50, D], BF16)

        pb = [es.enter_context(nc.psum_tensor(f"D_pbD{i}", [128, 512], F32)) for i in range(4)]
        ptr = es.enter_context(nc.psum_tensor("D_ptrD", [128, 128], BF16))

        make_ident(sc, ident)
        sc.op("pool", lambda g: g.memset(onesf[:], 1.0), inc="pool")
        init_done = sc.c("pool")

        for j in range(NCJ):
            sc.op("pool", lambda g, j=j: g.dma_start(
                out=wp_sb[:, j * D:(j + 1) * D], in_=wp_d[j * 128:(j + 1) * 128, :]),
                inc="ld", dma=True)
        for nm, dst, srcd in (("ldT", ldT_sb, ldT_d), ("da", da_sb, da_d),
                              ("luT", luT_sb, luT_d), ("uaT", uaT_sb, uaT_d)):
            sc.op("pool", lambda g, dst=dst, srcd=srcd: g.dma_start(out=dst[:], in_=srcd),
                  inc="ld", dma=True)
        bv = sc.op("pool", lambda g: g.dma_start(out=bp_sb[:], in_=bp_d), inc="ld", dma=True)
        bv = sc.op("pool", lambda g: g.dma_start(out=bl_sb[:], in_=bl_d), inc="ld", dma=True)
        ld_w = sc.c("ld")

        evs = []
        for (row, bc) in ((bp_sb, bp_bc), (bl_sb, bl_bc)):
            for oj in range(2):
                mm = sc.op("pe", lambda t, row=row, oj=oj: t.matmul(
                    pb[3][:], onesf[:], row[:, oj * 512:(oj + 1) * 512],
                    start=True, stop=True),
                    waits=[("ld", ld_w), ("pool", init_done)] +
                          ([("act", evs[-1])] if evs else []),
                    inc="pe")
                evs.append(sc.op("act", lambda a, bc=bc, oj=oj: a.copy(
                    out=bc[:, oj * 512:(oj + 1) * 512], in_=pb[3][:]),
                    waits=[("pe", mm)], inc="act"))
        bias_done = evs[-1]

        prev = bias_done
        for oj in range(2 if stage >= 0 else 0):
            mm = sc.op("pe", lambda t, oj=oj: t.matmul(
                pb[3][0:4, :], ldT_sb[:], da_sb[:, oj * 512:(oj + 1) * 512],
                start=True, stop=True),
                waits=[("act", prev)], inc="pe")
            prev = sc.op("act", lambda a, oj=oj: a.copy(
                out=a1_sb[:, oj * 512:(oj + 1) * 512], in_=pb[3][0:4, :]),
                waits=[("pe", mm)], inc="act")
        for oj in range(2 if stage >= 1 else 0):
            mm = sc.op("pe", lambda t, oj=oj: t.matmul(
                pb[3][0:50, :], luT_sb[:], a1_sb[:, oj * 512:(oj + 1) * 512],
                start=True, stop=True),
                waits=[("act", prev)], inc="pe")
            prev = sc.op("act", lambda a, oj=oj: a.copy(
                out=a2_sb[:, oj * 512:(oj + 1) * 512], in_=pb[3][0:50, :]),
                waits=[("pe", mm)], inc="act")
        wl_stage = sb("wl_stage", [128, D], BF16)
        wdone = prev
        dv = None
        for j in range(8 if stage >= 2 else 0):
            lv = sc.op("pool", lambda g, j=j: g.dma_start(
                out=wl_stage[:], in_=wl_d[j * 128:(j + 1) * 128, :]),
                waits=[("dve", sc.c("dve"))], inc="lst", dma=True)
            for oj in range(2):
                mm = sc.op("pe", lambda t, j=j, oj=oj: t.matmul(
                    pb[2][:], a2_sb[:, j * 128:(j + 1) * 128],
                    uaT_sb[:, oj * 512:(oj + 1) * 512],
                    start=True, stop=True),
                    waits=[("act", wdone)] + ([("dve", dv)] if dv else []), inc="pe")
                dv = sc.op("dve", lambda v, j=j, oj=oj: v.tensor_add(
                    out=weff[:, j * D + oj * 512: j * D + (oj + 1) * 512],
                    in0=pb[2][:], in1=wl_stage[:, oj * 512:(oj + 1) * 512]),
                    waits=[("pe", mm), ("lst", lv)], inc="dve")
        weff_done = sc.c("dve")

        NT1 = TPC // 128
        def ln_tile(tt, src_pre, out_dram, fm_dram, fm_tile, extra_store=None):
            v2 = sc.op("dve", lambda v: v.reduce_sum(out=mean[:, 0:1], in_=src_pre, axis=AXL.X),
                       inc="dve")
            v3 = sc.op("dve", lambda v: v.tensor_scalar_mul(
                out=nmean[:, 0:1], in0=mean[:, 0:1], scalar1=-1.0 / D),
                waits=[("dve", v2)], inc="dve")
            a1 = sc.op("act", lambda a: a.activation(
                out=xout[:], in_=src_pre, func=AF.Square, bias=nmean[:, 0:1], scale=1.0,
                accum_out=var[:, 0:1]),
                waits=[("dve", v3), ("st", sc.c("st")), ("st1", sc.c("st1"))], inc="act")
            v4 = sc.op("dve", lambda v: v.tensor_scalar(
                out=var[:, 0:1], in0=var[:, 0:1], scalar1=1.0 / D, scalar2=1e-5,
                op0=ALU.mult, op1=ALU.add),
                waits=[("act", a1)], inc="dve")
            a2 = sc.op("act", lambda a: a.activation(
                out=rstd[:, 0:1], in_=var[:, 0:1], func=AF.Sqrt),
                waits=[("dve", v4)], inc="act")
            v5 = sc.op("dve", lambda v: v.reciprocal(out=rstd[:, 0:1], in_=rstd[:, 0:1]),
                       waits=[("act", a2)], inc="dve")
            v6 = sc.op("dve", lambda v: v.tensor_scalar(
                out=xout[:], in0=src_pre, scalar1=nmean[:, 0:1], scalar2=rstd[:, 0:1],
                op0=ALU.add, op1=ALU.mult),
                waits=[("dve", v5)], inc="dve")
            stsem = "st1" if fm_dram is not None else "st"
            if fm_dram is not None:
                v7 = sc.op("dve", lambda v: v.tensor_copy(out=xout_bf[:], in_=xout[:]),
                           inc="dve")
                for sj in range(8):
                    tv = sc.op("pe", lambda t, sj=sj: t.transpose(
                        ptr[:], xout_bf[:, sj * 128:(sj + 1) * 128], ident[:]),
                        waits=[("dve", sc.c("dve")), ("pool", init_done)], inc="pe")
                    sc.op("dve", lambda v, sj=sj: v.tensor_copy(
                        out=fm_tile[:, sj * 128:(sj + 1) * 128], in_=ptr[:]),
                        waits=[("pe", tv)], inc="dve")
                vlast = sc.c("dve")
                sc.op("sync", lambda e, tt=tt: e.dma_start(
                    out=fm_dram[:, tt * 128:(tt + 1) * 128].rearrange("(j p) t -> p j t", p=128),
                    in_=fm_tile[:].rearrange("p (j t) -> p j t", j=8)),
                    waits=[("dve", vlast)], inc=stsem, dma=True)
            else:
                vlast = v6
            sc.op("sync", lambda e, tt=tt: e.dma_start(
                out=out_dram[tt * 128:(tt + 1) * 128, :], in_=xout[:]),
                waits=[("dve", vlast)], inc=stsem, dma=True)

        for tt in range(NT1 if stage >= 3 else 0):
            xv = sc.op("sync", lambda e, tt=tt: e.dma_start(
                out=xres[:], in_=x_sl[tt * 128:(tt + 1) * 128, :]),
                waits=([("dve", sc.c("dve"))] if tt else []), inc="ld2", dma=True)
            ccw = [("cc", dram["cc_wait"])] if fused else []
            yv = sc.op("sync", lambda e, tt=tt: e.dma_start(
                out=yfm_t[:].rearrange("p (j t) -> p j t", j=NCJ),
                in_=yfm[:, tt * 128:(tt + 1) * 128].rearrange("(j p) t -> p j t", p=128)),
                waits=([("pe", sc.c("pe"))] if tt else []) + ccw, inc="ld2", dma=True)
            mms = []
            for oj in range(2):
                for j in range(NCJ):
                    w = []
                    if j == 0:
                        w = [("ld", ld_w), ("ld2", max(xv, yv))]
                        if tt > 0:
                            w.append(("dve", sc.c("dve")))
                    mm = sc.op("pe", lambda t, oj=oj, j=j: t.matmul(
                        pb[oj][:], yfm_t[:, j * 128:(j + 1) * 128],
                        wp_sb[:, j * D + oj * 512: j * D + (oj + 1) * 512],
                        start=(j == 0), stop=(j == NCJ - 1)),
                        waits=w, inc="pe" if j == NCJ - 1 else None)
                mms.append(mm)
            for oj in range(2):
                sc.op("dve", lambda v, oj=oj: v.tensor_add(
                    out=pre[:, oj * 512:(oj + 1) * 512], in0=pb[oj][:],
                    in1=bp_bc[:, oj * 512:(oj + 1) * 512]),
                    waits=[("pe", mms[oj]), ("act", bias_done)], inc="dve")
            sc.op("dve", lambda v: v.tensor_add(out=pre[:], in0=pre[:], in1=xres[:]),
                  waits=[("ld2", max(xv, yv))], inc="dve")
            ln_tile(tt, pre[:], x3_dram, x3fm_dram, x3fm_t)
        d1_st = sc.c("st1")

        for tt in range(NT1 if stage >= 4 else 0):
            xv = sc.op("sync", lambda e, tt=tt: e.dma_start(
                out=xres[:], in_=x3_dram[tt * 128:(tt + 1) * 128, :]),
                waits=[("st1", d1_st), ("dve", sc.c("dve"))], inc="ld2", dma=True)
            yv = sc.op("sync", lambda e, tt=tt: e.dma_start(
                out=yfm_t[:, 0:1024].rearrange("p (j t) -> p j t", j=8),
                in_=x3fm_dram[:, tt * 128:(tt + 1) * 128].rearrange("(j p) t -> p j t", p=128)),
                waits=[("st1", d1_st), ("pe", sc.c("pe"))], inc="ld2", dma=True)
            mms = []
            for oj in range(2):
                for j in range(8):
                    w = []
                    if j == 0:
                        w = [("dve", weff_done), ("ld2", max(xv, yv))]
                        if tt > 0:
                            w.append(("dve", sc.c("dve")))
                    mm = sc.op("pe", lambda t, oj=oj, j=j: t.matmul(
                        pb[oj][:], yfm_t[:, j * 128:(j + 1) * 128],
                        weff[:, j * D + oj * 512: j * D + (oj + 1) * 512],
                        start=(j == 0), stop=(j == 7)),
                        waits=w, inc="pe" if j == 7 else None)
                mms.append(mm)
            for oj in range(2):
                sc.op("dve", lambda v, oj=oj: v.tensor_add(
                    out=pre[:, oj * 512:(oj + 1) * 512], in0=pb[oj][:],
                    in1=bl_bc[:, oj * 512:(oj + 1) * 512]),
                    waits=[("pe", mms[oj]), ("act", bias_done)], inc="dve")
            sc.op("dve", lambda v: v.tensor_add(out=pre[:], in0=pre[:], in1=xres[:]),
                  waits=[("ld2", max(xv, yv))], inc="dve")
            ln_tile(tt, pre[:], x4o, None, None)
        sc.op("sync", lambda e: None, waits=[("st", sc.c("st"))])
        sc.flush(block)


HEAD_OF_CORE = [(c // 4, 4 * (c % 4)) for c in range(8)]


def kernel_unfused(**inp):
    x = np.asarray(inp["L0_attn_x"])
    Wq = np.asarray(inp["L0_attn_Wq"])
    Wk = np.asarray(inp["L0_attn_Wk"])
    Wv = np.asarray(inp["L0_attn_Wv"])
    bq = np.asarray(inp["L0_attn_bq"])
    bk = np.asarray(inp["L0_attn_bk"])
    bv = np.asarray(inp["L0_attn_bv"])

    nc = build_A()
    in_maps = []
    for c in range(8):
        b, h0 = HEAD_OF_CORE[c]
        sl = slice(h0 * HD, (h0 + HPC) * HD)
        wo2 = np.concatenate([woT, zeros], 0) if b == 0 else np.concatenate([zeros, woT], 0)
        wp2 = np.concatenate([wproj, zeros], 0) if b == 0 else np.concatenate([zeros, wproj], 0)
        in_maps.append({
            "xfm": np.ascontiguousarray(x[b].T),
            "wqT": np.ascontiguousarray(Wq[sl, :].T),
            "wkT": np.ascontiguousarray(Wk[sl, :].T),
            "wvT": np.ascontiguousarray(Wv[sl, :].T),
            "bqkv": np.ascontiguousarray(np.stack([bq[sl], bk[sl], bv[sl]])),
        })
    res = run_bass_kernel_spmd(nc, in_maps, core_ids=list(range(8)))
    y_fm = [np.concatenate([np.asarray(res.results[4 * b_ + i]["yfm"]) for i in range(4)], axis=0)
            for b_ in range(B)]

    nc = build_B(debug=False)
    in_maps = []
    for c in range(8):
        b, q = c // 4, c % 4
        ts = slice(q * TPC, (q + 1) * TPC)
        in_maps.append({
            "yfm_sl": np.ascontiguousarray(y_fm[b][:, ts]),
            "x_sl": np.ascontiguousarray(x[b][ts, :]),
            "woT": np.ascontiguousarray(np.asarray(inp["L0_attn_Wo"]).T),
            "bo": np.ascontiguousarray(np.asarray(inp["L0_attn_bo"])[None, :]),
            "wgT": np.ascontiguousarray(np.asarray(inp["L0_ffn_w_gate"]).T),
            "wuT": np.ascontiguousarray(np.asarray(inp["L0_ffn_w_up"]).T),
            "wdT": np.ascontiguousarray(np.asarray(inp["L0_ffn_w_down"]).T),
        })
    res = run_bass_kernel_spmd(nc, in_maps, core_ids=list(range(8)))
    x2 = np.zeros((B, T, D), np.float32)
    for c in range(8):
        b, q = c // 4, c % 4
        ts = slice(q * TPC, (q + 1) * TPC)
        x2[b, ts] = np.asarray(res.results[c]["x2o"])

    W_attn = np.asarray(inp["L1_attn_W_attn"])
    b_attn = np.asarray(inp["L1_attn_b_attn"])
    gain = np.asarray(inp["L1_attn_score_gain"]).reshape(1, 1)
    sign = 1.0 if float(gain[0, 0]) >= 0 else -1.0
    nc = build_C(sign=sign)
    in_maps = []
    for c in range(8):
        b, h0 = HEAD_OF_CORE[c]
        cs = slice(h0 * HD, (h0 + HPC) * HD)
        in_maps.append({
            "x2fm": np.ascontiguousarray(x2[b].T),
            "wqC": np.ascontiguousarray(W_attn[:, cs]),
            "wkC": np.ascontiguousarray(W_attn[:, 1024 + h0 * HD: 1024 + (h0 + HPC) * HD]),
            "wvC": np.ascontiguousarray(W_attn[:, 2048 + h0 * HD: 2048 + (h0 + HPC) * HD]),
            "bqkvC": np.ascontiguousarray(np.stack([
                b_attn[cs], b_attn[1024 + h0 * HD: 1024 + (h0 + HPC) * HD],
                b_attn[2048 + h0 * HD: 2048 + (h0 + HPC) * HD]])),
            "gain": np.ascontiguousarray(gain.astype(np.float32)),
        })
    res = run_bass_kernel_spmd(nc, in_maps, core_ids=list(range(8)))
    y1_fm = [np.concatenate([np.asarray(res.results[4 * b_ + i]["yfm"]) for i in range(4)], axis=0)
             for b_ in range(B)]

    nc = build_D()
    in_maps = []
    for c in range(8):
        b, q = c // 4, c % 4
        ts = slice(q * TPC, (q + 1) * TPC)
        in_maps.append({
            "y1fm_sl": np.ascontiguousarray(y1_fm[b][:, ts]),
            "x2_sl": np.ascontiguousarray(x2[b][ts, :]),
            "wproj": np.ascontiguousarray(np.asarray(inp["L1_attn_W_proj"])),
            "bproj": np.ascontiguousarray(np.asarray(inp["L1_attn_b_proj"])[None, :]),
            "wLT": np.ascontiguousarray(np.asarray(inp["L1_ffn_W"]).T),
            "bL": np.ascontiguousarray(np.asarray(inp["L1_ffn_b"])[None, :]),
            "ldT": np.ascontiguousarray(np.asarray(inp["L1_ffn_lora_down"]).T),
            "da": np.ascontiguousarray(np.asarray(inp["L1_ffn_down_aux"])),
            "luT": np.ascontiguousarray(np.asarray(inp["L1_ffn_lora_up"]).T),
            "uaT": np.ascontiguousarray(np.asarray(inp["L1_ffn_up_aux"]).T),
        })
    res = run_bass_kernel_spmd(nc, in_maps, core_ids=list(range(8)))
    out = np.zeros((B, T, D), np.float32)
    for c in range(8):
        b, q = c // 4, c % 4
        ts = slice(q * TPC, (q + 1) * TPC)
        out[b, ts] = np.asarray(res.results[c]["x4o"])
    return out


def build_full(sign=1.0):
    nc = bass.Bass("TRN2", target_bir_lowering=False)
    dram = {}
    dram["xfm"] = nc.dram_tensor("xfm", [D, T], F32, kind="ExternalInput").ap()
    for w in ("q", "k", "v"):
        dram[f"w{w}T"] = nc.dram_tensor(f"w{w}T", [D, HPC * HD], F32, kind="ExternalInput").ap()
    dram["bqkv"] = nc.dram_tensor("bqkv", [3, HPC * HD], F32, kind="ExternalInput").ap()
    dram["x_sl"] = nc.dram_tensor("x_sl", [TPC, D], F32, kind="ExternalInput").ap()
    dram["wo2T"] = nc.dram_tensor("wo2T", [2 * D, D], F32, kind="ExternalInput").ap()
    dram["bo"] = nc.dram_tensor("bo", [1, D], F32, kind="ExternalInput").ap()
    dram["wgT"] = nc.dram_tensor("wgT", [D, 3072], I32, kind="ExternalInput").ap()
    dram["wuT"] = nc.dram_tensor("wuT", [D, 3072], I32, kind="ExternalInput").ap()
    dram["wdT"] = nc.dram_tensor("wdT", [3072, D], I32, kind="ExternalInput").ap()
    for w in ("q", "k", "v"):
        dram[f"w{w}C"] = nc.dram_tensor(f"w{w}C", [D, HPC * HD], F32, kind="ExternalInput").ap()
    dram["bqkvC"] = nc.dram_tensor("bqkvC", [3, HPC * HD], F32, kind="ExternalInput").ap()
    dram["gain"] = nc.dram_tensor("gain", [1, 1], F32, kind="ExternalInput").ap()
    dram["wp2"] = nc.dram_tensor("wp2", [2 * D, D], F32, kind="ExternalInput").ap()
    dram["bproj"] = nc.dram_tensor("bproj", [1, D], F32, kind="ExternalInput").ap()
    dram["wLT"] = nc.dram_tensor("wLT", [D, D], F32, kind="ExternalInput").ap()
    dram["bL"] = nc.dram_tensor("bL", [1, D], F32, kind="ExternalInput").ap()
    dram["ldT"] = nc.dram_tensor("ldT", [100, 4], F32, kind="ExternalInput").ap()
    dram["da"] = nc.dram_tensor("da", [100, D], F32, kind="ExternalInput").ap()
    dram["luT"] = nc.dram_tensor("luT", [4, 50], F32, kind="ExternalInput").ap()
    dram["uaT"] = nc.dram_tensor("uaT", [50, D], F32, kind="ExternalInput").ap()
    dram["x4o"] = nc.dram_tensor("x4o", [TPC, D], F32, kind="ExternalOutput").ap()
    for n, sh, dt in (("x1_scr", [TPC, D], F32), ("x1fm_scr", [D, TPC], BF16),
                      ("x2_scr", [TPC, D], F32), ("x3_scr", [TPC, D], F32),
                      ("x3fm_scr", [D, TPC], BF16),
                      ("a2a1_in", [2048, TPC], BF16), ("a2a1_out", [2048, TPC], BF16),
                      ("ag2_in", [D, TPC], BF16), ("ag2_out", [4 * D, TPC], BF16),
                      ("a2a3_in", [2048, TPC], BF16), ("a2a3_out", [2048, TPC], BF16)):
        dram[n] = nc.dram_tensor(n, sh, dt).ap()
    es0 = ExitStack()
    with es0:
        sc = Sched(nc, es0)
        block = es0.enter_context(nc.Block())
        phase_A(nc, sc, block, dram, fused=True)
        sc.barrier()
        dram["cc_wait"] = 1
        phase_B(nc, sc, block, dram, fused=True)
        sc.barrier()
        dram["cc_wait"] = 2
        phase_C(nc, sc, block, dram, sign, fused=True)
        sc.barrier()
        dram["cc_wait"] = 3
        phase_D(nc, sc, block, dram, fused=True)
    return nc


def kernel(**inp):
    inp = {k: np.asarray(v) for k, v in inp.items()}
    x = inp["L0_attn_x"]
    gain = inp["L1_attn_score_gain"].reshape(1, 1).astype(np.float32)
    sign = 1.0 if float(gain[0, 0]) >= 0 else -1.0
    W_attn = inp["L1_attn_W_attn"]
    b_attn = inp["L1_attn_b_attn"]
    woT = np.ascontiguousarray(inp["L0_attn_Wo"].T)
    wproj = np.ascontiguousarray(inp["L1_attn_W_proj"])
    zeros = np.zeros((D, D), np.float32)
    nc = build_full(sign=sign)
    in_maps = []
    for c in range(8):
        b, h0 = HEAD_OF_CORE[c]
        q = c % 4
        sl = slice(h0 * HD, (h0 + HPC) * HD)
        ts = slice(q * TPC, (q + 1) * TPC)
        wo2 = np.concatenate([woT, zeros], 0) if b == 0 else np.concatenate([zeros, woT], 0)
        wp2 = np.concatenate([wproj, zeros], 0) if b == 0 else np.concatenate([zeros, wproj], 0)
        in_maps.append({
            "xfm": np.ascontiguousarray(x[b].T),
            "wqT": np.ascontiguousarray(inp["L0_attn_Wq"][sl, :].T),
            "wkT": np.ascontiguousarray(inp["L0_attn_Wk"][sl, :].T),
            "wvT": np.ascontiguousarray(inp["L0_attn_Wv"][sl, :].T),
            "bqkv": np.ascontiguousarray(np.stack([
                inp["L0_attn_bq"][sl], inp["L0_attn_bk"][sl], inp["L0_attn_bv"][sl]])),
            "x_sl": np.ascontiguousarray(x[b][ts, :]),
            "wo2T": np.ascontiguousarray(wo2),
            "bo": np.ascontiguousarray(inp["L0_attn_bo"][None, :]),
            "wgT": np.ascontiguousarray(inp["L0_ffn_w_gate"].T),
            "wuT": np.ascontiguousarray(inp["L0_ffn_w_up"].T),
            "wdT": np.ascontiguousarray(inp["L0_ffn_w_down"].T),
            "wqC": np.ascontiguousarray(W_attn[:, sl]),
            "wkC": np.ascontiguousarray(W_attn[:, 1024 + h0 * HD: 1024 + (h0 + HPC) * HD]),
            "wvC": np.ascontiguousarray(W_attn[:, 2048 + h0 * HD: 2048 + (h0 + HPC) * HD]),
            "bqkvC": np.ascontiguousarray(np.stack([
                b_attn[sl], b_attn[1024 + h0 * HD: 1024 + (h0 + HPC) * HD],
                b_attn[2048 + h0 * HD: 2048 + (h0 + HPC) * HD]])),
            "gain": np.ascontiguousarray(gain),
            "wp2": np.ascontiguousarray(wp2),
            "bproj": np.ascontiguousarray(inp["L1_attn_b_proj"][None, :]),
            "wLT": np.ascontiguousarray(inp["L1_ffn_W"].T),
            "bL": np.ascontiguousarray(inp["L1_ffn_b"][None, :]),
            "ldT": np.ascontiguousarray(inp["L1_ffn_lora_down"].T),
            "da": np.ascontiguousarray(inp["L1_ffn_down_aux"]),
            "luT": np.ascontiguousarray(inp["L1_ffn_lora_up"].T),
            "uaT": np.ascontiguousarray(inp["L1_ffn_up_aux"].T),
        })
    res = run_bass_kernel_spmd(nc, in_maps, core_ids=list(range(8)))
    out = np.zeros((B, T, D), np.float32)
    for c in range(8):
        b, _ = HEAD_OF_CORE[c]
        q = c % 4
        out[b, q * TPC:(q + 1) * TPC] = np.asarray(res.results[c]["x4o"])
    return out


# version 11
# speedup vs baseline: 1.1518x; 1.0151x over previous
import numpy as np
import ml_dtypes
from contextlib import ExitStack

import concourse.bass as bass
import concourse.mybir as mybir
from concourse.bass_utils import run_bass_kernel_spmd
from concourse.aot_env import allow_ici_all_to_all

F32 = mybir.dt.float32
BF16 = mybir.dt.bfloat16
I32 = mybir.dt.int32
AF = mybir.ActivationFunctionType
ALU = mybir.AluOpType
AXL = mybir.AxisListType

B = 2
C_PAIRS = 2
T = 8192
D = 1024
NH = 16
HD = 64
HPC = 4
TPC = 2048
CHUNK = 128
NCHUNK = T // CHUNK


class Sem:
    def __init__(self, h):
        self.h = h
        self.c = 0


class Sched:
    ENGS = ("pe", "act", "dve", "pool", "sync")

    def __init__(self, nc, es):
        self.nc = nc
        self.q = {e: [] for e in ("pe", "act", "dve", "pool", "sync")}
        self.sem = {n: Sem(es.enter_context(nc.semaphore(f"s_{n}")))
                    for n in ("ld", "ld2", "lx0", "lx1", "lst", "st1", "pe", "act",
                              "dve", "pool", "sync", "st", "cc")}
        self.pending = {}

    def op(self, eng, fn, waits=(), inc=None, dma=False):
        s = self.sem[inc] if inc else None
        n = 16 if dma else 1
        val = None
        if s is not None:
            s.c += n
            val = s.c
        if self.pending.get(eng):
            waits = tuple(self.pending.pop(eng)) + tuple(waits)
        waits = tuple((self.sem[w], v) for (w, v) in waits)

        def g(e, fn=fn, waits=waits, s=s, n=n):
            for (ss, v) in waits:
                e.wait_ge(ss.h, v)
            ins = fn(e)
            if s is not None:
                ins.then_inc(s.h, n)
        self.q[eng].append(g)
        return val

    def c(self, name):
        return self.sem[name].c

    def barrier(self):
        snap = [(n, s.c) for n, s in self.sem.items() if s.c > 0]
        for e in self.ENGS:
            self.pending[e] = list(snap)

    def flush(self, block):
        self.emit(block)
        self.q = {e: [] for e in self.ENGS}

    def emit(self, block):
        @block.tensor
        def _(eng):
            for f in self.q["pe"]:
                f(eng)

        @block.scalar
        def _(eng):
            for f in self.q["act"]:
                f(eng)

        @block.vector
        def _(eng):
            for f in self.q["dve"]:
                f(eng)

        @block.gpsimd
        def _(eng):
            for f in self.q["pool"]:
                f(eng)

        @block.sync
        def _(eng):
            for f in self.q["sync"]:
                f(eng)


def make_ident(sc, ident):
    v = sc.op("pool", lambda g: g.memset(ident[:], 1.0), inc="pool")
    return sc.op("pool", lambda g: g.affine_select(
        out=ident[:], in_=ident[:], pattern=[[-1, 128]],
        compare_op=ALU.is_equal, fill=0.0, base=0, channel_multiplier=1),
        waits=[("pool", v)], inc="pool")


def make_triu(sc, triu):
    v = sc.op("pool", lambda g: g.memset(triu[:], 1.0), inc="pool")
    return sc.op("pool", lambda g: g.affine_select(
        out=triu[:], in_=triu[:], pattern=[[1, 128]],
        compare_op=ALU.is_ge, fill=0.0, base=0, channel_multiplier=-1),
        waits=[("pool", v)], inc="pool")


def build_A(debug=False):
    nc = bass.Bass("TRN2", target_bir_lowering=False)
    dram = {}
    dram["xfm"] = nc.dram_tensor("xfm", [D, T], F32, kind="ExternalInput").ap()
    for w in ("q", "k", "v"):
        dram[f"w{w}T"] = nc.dram_tensor(f"w{w}T", [D, HPC * HD], F32, kind="ExternalInput").ap()
    dram["bqkv"] = nc.dram_tensor("bqkv", [3, HPC * HD], F32, kind="ExternalInput").ap()
    dram["yfm"] = nc.dram_tensor("yfm", [HPC * HD, T], BF16, kind="ExternalOutput").ap()
    es0 = ExitStack()
    with es0:
        sc = Sched(nc, es0)
        block = es0.enter_context(nc.Block())
        phase_A(nc, sc, block, dram, fused=False)
    return nc


def phase_A(nc, sc, block, dram, fused):
    xfm = dram["xfm"]
    wT = {w: dram[f"w{w}T"] for w in ("q", "k", "v")}
    bqkv = dram["bqkv"]
    TT = 512
    NTT = T // TT
    es = ExitStack()
    with es:
        def sb(name, shape, dt=F32):
            return es.enter_context(nc.sbuf_tensor("A_" + name, shape, dt))

        def psum(name, shape, dt=F32):
            return es.enter_context(nc.psum_tensor("A_" + name, shape, dt))

        w_sb = {w: sb(f"w{w}", [128, 8 * 256], BF16) for w in ("q", "k", "v")}
        bias_sb = sb("bias", [128, 6], F32)
        x_t = [sb(f"x_t{i}", [128, 8 * TT], BF16) for i in range(2)]
        q_fm = sb("q_fm", [128, 2 * T], BF16)
        k_fm = sb("k_fm", [128, 2 * T], BF16)
        kt = sb("kt", [128, HPC * NCHUNK * HD], BF16)
        vt = sb("vt", [128, HPC * NCHUNK * 65], BF16)
        v_stage = sb("v_stage", [128, TT], BF16)
        phi_m = sb("phi_m", [128, TT], F32)
        phi_e = sb("phi_e", [128, TT], F32)
        phi_r = sb("phi_r", [128, TT], F32)
        masked = [sb(f"masked{i}", [128, 128], BF16) for i in range(2)]
        triu = sb("triu", [128, 128], BF16)
        ident = sb("ident", [128, 128], BF16)
        S_arena = sb("S_arena", [128, 2 * 65], BF16)
        ones1 = sb("ones1", [1, 64], F32)
        rden3 = sb("rden3", [1, 384], F32)
        bcast_sb = sb("bcast_sb", [64, 128], F32)
        y_fm = sb("y_fm", [128, 2 * T], BF16)

        pp = [psum(f"pp{i}", [128, TT]) for i in range(6)]
        ptr = psum("ptr", [128, 128], BF16)
        psc_bank = psum("psc", [128, 512])
        psc = [psc_bank[:, 0:128], psc_bank[:, 128:256]]
        pnum3 = [pp[0][0:65, 0:128], pp[0][0:65, 128:256], pp[0][0:65, 256:384]]
        pS_h = [pp[1], pp[3], pp[4], pp[5]]
        pbc = pp[2][0:64, 0:128]

        make_ident(sc, ident)
        make_triu(sc, triu)
        sc.op("pool", lambda g: g.memset(ones1[:], 1.0), inc="pool")
        sc.op("pool", lambda g: g.memset(vt[:], 1.0), inc="pool")
        sc.op("pool", lambda g: g.memset(S_arena[:], 0.0), inc="pool")
        init_done = sc.c("pool")

        for w in ("q", "k", "v"):
            for j in range(8):
                sc.op("pool", lambda g, w=w, j=j: g.dma_start(
                    out=w_sb[w][:, j * 256:(j + 1) * 256],
                    in_=wT[w][j * 128:(j + 1) * 128, :]), inc="ld", dma=True)
        for wi in range(3):
            for p in range(2):
                sc.op("pool", lambda g, wi=wi, p=p: g.dma_start(
                    out=bias_sb[:, wi * 2 + p: wi * 2 + p + 1],
                    in_=bqkv[wi:wi + 1, p * 128:(p + 1) * 128].rearrange("a e -> e a")),
                    inc="ld", dma=True)
        ld_w = sc.c("ld")

        pe_tile_done = {}
        dve_tile_done = {}
        for ti in range(NTT):
            buf = ti % 2
            waits = [("pe", pe_tile_done[ti - 2])] if ti >= 2 else []
            xv = sc.op("pool", lambda g, buf=buf, ti=ti: g.dma_start(
                out=x_t[buf][:].rearrange("p (j t) -> p j t", j=8),
                in_=xfm[:, ti * TT:(ti + 1) * TT].rearrange("(j p) t -> p j t", p=128)),
                waits=waits, inc=f"lx{buf}", dma=True)

            for wi, w in enumerate(("q", "k", "v")):
                for p in range(2):
                    bank = pp[wi * 2 + p]
                    bcol = bias_sb[:, wi * 2 + p: wi * 2 + p + 1]
                    for j in range(8):
                        wts = []
                        if j == 0:
                            wts = [("ld", ld_w), (f"lx{buf}", xv)]
                            if ti > 0:
                                wts.append(("dve", dve_tile_done[ti - 1]))
                        mm = sc.op("pe", lambda t, bank=bank, w=w, j=j, p=p, buf=buf: t.matmul(
                            bank[:],
                            w_sb[w][:, j * 256 + p * 128: j * 256 + (p + 1) * 128],
                            x_t[buf][:, j * TT:(j + 1) * TT],
                            start=(j == 0), stop=(j == 7)),
                            waits=wts, inc="pe" if j == 7 else None)
                    if w in ("q", "k"):
                        dst = q_fm if w == "q" else k_fm
                        dslc = dst[:, p * T + ti * TT: p * T + (ti + 1) * TT]
                        v1 = sc.op("dve", lambda v, bank=bank, bcol=bcol: v.tensor_scalar(
                            out=phi_m[:], in0=bank[:], scalar1=bcol, scalar2=0.0,
                            op0=ALU.add, op1=ALU.min),
                            waits=[("pe", mm), ("ld", ld_w)], inc="dve")
                        a1 = sc.op("act", lambda a: a.activation(
                            out=phi_e[:], in_=phi_m[:], func=AF.Exp),
                            waits=[("dve", v1)], inc="act")
                        sc.op("dve", lambda v, bank=bank, bcol=bcol: v.tensor_scalar(
                            out=phi_r[:], in0=bank[:], scalar1=bcol, scalar2=0.0,
                            op0=ALU.add, op1=ALU.max), inc="dve")
                        v3 = sc.op("dve", lambda v, dslc=dslc: v.tensor_add(
                            out=dslc, in0=phi_e[:], in1=phi_r[:]),
                            waits=[("act", a1)], inc="dve")
                        if w == "k":
                            for sti in range(4):
                                c = ti * 4 + sti
                                src = dst[:, p * T + c * 128:p * T + (c + 1) * 128]
                                tv = sc.op("pe", lambda t, src=src: t.transpose(
                                    ptr[:], src, ident[:]),
                                    waits=[("dve", sc.c("dve")), ("pool", init_done)],
                                    inc="pe")
                                for hh in range(2):
                                    h = 2 * p + hh
                                    sc.op("dve", lambda v, hh=hh, h=h, c=c: v.tensor_copy(
                                        out=kt[:, (h * NCHUNK + c) * 64:(h * NCHUNK + c) * 64 + 64],
                                        in_=ptr[:, hh * 64:(hh + 1) * 64]),
                                        waits=[("pe", tv)], inc="dve")
                    else:
                        vv = sc.op("dve", lambda v, bank=bank, bcol=bcol: v.tensor_scalar_add(
                            out=v_stage[:], in0=bank[:], scalar1=bcol),
                            waits=[("pe", mm), ("ld", ld_w)], inc="dve")
                        for sti in range(4):
                            c = ti * 4 + sti
                            tv = sc.op("pe", lambda t, sti=sti: t.transpose(
                                ptr[:], v_stage[:, sti * 128:(sti + 1) * 128], ident[:]),
                                waits=[("dve", sc.c("dve")), ("pool", init_done)],
                                inc="pe")
                            for hh in range(2):
                                h = 2 * p + hh
                                sc.op("dve", lambda v, hh=hh, h=h, c=c: v.tensor_copy(
                                    out=vt[:, (h * NCHUNK + c) * 65:(h * NCHUNK + c) * 65 + 64],
                                    in_=ptr[:, hh * 64:(hh + 1) * 64]),
                                    waits=[("pe", tv)], inc="dve")
            pe_tile_done[ti] = sc.c("pe")
            dve_tile_done[ti] = sc.c("dve")
        phase1_dve = sc.c("dve")
        phase1_pe = sc.c("pe")

        NJ = NCHUNK * HPC
        mask_done = [0] * NJ
        num1_done = [0] * NJ
        supd_done = [0] * NJ
        ymult_done = [0] * NJ
        recip_done = [0] * NJ
        sevac_done = [0] * NJ
        bc_done = [0] * NJ
        bcevac_done = [0] * NJ

        def jparts(j):
            i, h = j // HPC, j % HPC
            return i, h, h % 2, h // 2

        for j in range(NJ + 2):
            if j >= 2:
                jj = j - 2
                bcw = [("dve", recip_done[jj])]
                if jj > 0:
                    bcw.append(("act", bcevac_done[jj - 1]))
                bc_done[jj] = sc.op("pe", lambda t, jj=jj: t.matmul(
                    pbc, ones1[:], rden3[:, (jj % 3) * 128:(jj % 3) * 128 + 128],
                    start=True, stop=True),
                    waits=bcw, inc="pe")
            if j < NJ:
                i, h, par, pair = jparts(j)
                rowoff = par * 64
                qk_off = pair * T + i * 128
                wts = [("dve", phase1_dve)]
                if j >= 2:
                    wts.append(("dve", mask_done[j - 2]))
                sc.op("pe", lambda t, rowoff=rowoff, qk_off=qk_off, buf=j % 2: t.matmul(
                    psc[buf][:],
                    k_fm[rowoff:rowoff + 64, qk_off:qk_off + 128],
                    q_fm[rowoff:rowoff + 64, qk_off:qk_off + 128],
                    start=True, stop=True, tile_position=(rowoff, 0)),
                    waits=wts, inc="pe")
                sc_done = sc.c("pe")
            if 1 <= j <= NJ:
                jj = j - 1
                i, h, par, pair = jparts(jj)
                rowoff = par * 64
                qk_off = pair * T + i * 128
                Sap = S_arena[rowoff:rowoff + 64, pair * 65: pair * 65 + 65]
                pSap = pS_h[h][rowoff:rowoff + 64, 0:65]
                vslc = vt[:, (h * NCHUNK + i) * 65:(h * NCHUNK + i) * 65 + 65]
                kslc = kt[:, (h * NCHUNK + i) * 64:(h * NCHUNK + i) * 64 + 64]
                nb = pnum3[jj % 3]
                nwts = [("dve", mask_done[jj])]
                if jj >= 3:
                    nwts.append(("dve", ymult_done[jj - 3]))
                num1_done[jj] = sc.op("pe", lambda t, nb=nb, vslc=vslc, buf=jj % 2: t.matmul(
                    nb, vslc, masked[buf][:], start=True, stop=False),
                    waits=nwts, inc="pe")
                n2w = []
                if jj >= HPC:
                    n2w = [("act", sevac_done[jj - HPC])]
                sc.op("pe", lambda t, nb=nb, Sap=Sap, rowoff=rowoff, qk_off=qk_off: t.matmul(
                    nb, Sap, q_fm[rowoff:rowoff + 64, qk_off:qk_off + 128],
                    start=False, stop=True, tile_position=(rowoff, 0)),
                    waits=n2w, inc="pe")
                supd_done[jj] = sc.op("pe", lambda t, pSap=pSap, kslc=kslc, vslc=vslc, i=i, rowoff=rowoff: t.matmul(
                    pSap, kslc, vslc, start=(i == 0), stop=(i == NCHUNK - 1),
                    tile_position=(0, rowoff)),
                    inc="pe")
                sevac_done[jj] = sc.op("act", lambda a, Sap=Sap, pSap=pSap: a.copy(
                    out=Sap, in_=pSap),
                    waits=[("pe", supd_done[jj])], inc="act")
                recip_done[jj] = sc.op("dve", lambda v, nb=nb, jj=jj: v.reciprocal(
                    out=rden3[:, (jj % 3) * 128:(jj % 3) * 128 + 128],
                    in_=nb[64:65, 0:128]),
                    waits=[("pe", supd_done[jj])], inc="dve")
            if j < NJ:
                mwts = [("pe", sc_done)]
                if j >= 2:
                    mwts.append(("pe", num1_done[j - 2]))
                mask_done[j] = sc.op("dve", lambda v, buf=j % 2: v.tensor_mul(
                    out=masked[buf][:], in0=psc[buf][:], in1=triu[:]),
                    waits=mwts, inc="dve")
            if j >= 2:
                jj = j - 2
                i, h, par, pair = jparts(jj)
                bw = [("pe", bc_done[jj])]
                if jj >= 1:
                    bw.append(("dve", ymult_done[jj - 1]))
                bcevac_done[jj] = sc.op("act", lambda a: a.copy(
                    out=bcast_sb[:], in_=pbc),
                    waits=bw, inc="act")
                ydst = y_fm[(h % 2) * 64:(h % 2) * 64 + 64,
                            (h // 2) * T + i * 128:(h // 2) * T + (i + 1) * 128]
                ymult_done[jj] = sc.op("dve", lambda v, nb=pnum3[jj % 3], ydst=ydst: v.tensor_mul(
                    out=ydst, in0=nb[0:64, 0:128], in1=bcast_sb[:]),
                    waits=[("act", bcevac_done[jj])], inc="dve")
        yl = sc.c("dve")

        if not fused:
            for p in range(2):
                sc.op("sync", lambda e, p=p: e.dma_start(
                    out=dram["yfm"][p * 128:(p + 1) * 128, :], in_=y_fm[:, p * T:(p + 1) * T]),
                    waits=[("dve", yl)], inc="st", dma=True)
            sc.op("sync", lambda e: None, waits=[("st", sc.c("st"))])
        else:
            a2a_in, a2a_out = dram["a2a1_in"], dram["a2a1_out"]
            for jj in range(8):
                for p in range(2):
                    sc.op("sync", lambda e, jj=jj, p=p: e.dma_start(
                        out=a2a_in[jj * 256 + p * 128: jj * 256 + (p + 1) * 128, :],
                        in_=y_fm[:, p * T + (jj % 4) * TPC: p * T + (jj % 4) * TPC + TPC]),
                        waits=[("dve", yl)], inc="st1", dma=True)
            def _a2a(g, a2a_in=a2a_in, a2a_out=a2a_out):
                with allow_ici_all_to_all():
                    return g.collective_compute(
                        "AllToAll", ALU.bypass, replica_groups=[list(range(8))],
                        ins=[a2a_in.opt()], outs=[a2a_out.opt()])
            sc.op("pool", _a2a, waits=[("st1", sc.c("st1"))], inc="cc")
        sc.flush(block)




def build_B(debug=False):
    nc = bass.Bass("TRN2", target_bir_lowering=False)
    dram = {}
    dram["yfm_sl"] = nc.dram_tensor("yfm_sl", [D, TPC], BF16, kind="ExternalInput").ap()
    dram["x_sl"] = nc.dram_tensor("x_sl", [TPC, D], F32, kind="ExternalInput").ap()
    dram["woT"] = nc.dram_tensor("woT", [D, D], F32, kind="ExternalInput").ap()
    dram["bo"] = nc.dram_tensor("bo", [1, D], F32, kind="ExternalInput").ap()
    dram["wgT"] = nc.dram_tensor("wgT", [D, 3072], I32, kind="ExternalInput").ap()
    dram["wuT"] = nc.dram_tensor("wuT", [D, 3072], I32, kind="ExternalInput").ap()
    dram["wdT"] = nc.dram_tensor("wdT", [3072, D], I32, kind="ExternalInput").ap()
    dram["x2o"] = nc.dram_tensor("x2o", [TPC, D], F32, kind="ExternalOutput").ap()
    dram["x1_scr"] = nc.dram_tensor("x1_scr", [TPC, D], F32).ap()
    dram["x1fm_scr"] = nc.dram_tensor("x1fm_scr", [D, TPC], BF16).ap()
    es0 = ExitStack()
    with es0:
        sc = Sched(nc, es0)
        block = es0.enter_context(nc.Block())
        phase_B(nc, sc, block, dram, fused=False)
    return nc


def phase_B(nc, sc, block, dram, fused):
    x_sl = dram["x_sl"]
    woT_d = dram.get("woT")
    bo_d = dram["bo"]
    wgT_d = dram["wgT"]
    wuT_d = dram["wuT"]
    wdT_d = dram["wdT"]
    x1_dram = dram["x1_scr"]
    x1fm_dram = dram["x1fm_scr"]
    if fused:
        yfm = dram["a2a1_out"]
        wo2_d = dram["wo2T"]
        NCJ = 16
        x2_dram = dram["x2_scr"]
        x2fm_dram = dram["ag2_in"]
    else:
        yfm = dram["yfm_sl"]
        wo2_d = dram["woT"]
        NCJ = 8
        x2_dram = None
        x2fm_dram = None
    es = ExitStack()
    with es:
        def sb(name, shape, dt=F32):
            return es.enter_context(nc.sbuf_tensor("B_" + name, shape, dt))

        def psum(name, shape, dt=F32):
            return es.enter_context(nc.psum_tensor("B_" + name, shape, dt))

        bo_bc = sb("bo_bc", [128, D], F32)
        bo_sb = sb("bo_sb", [1, D], F32)
        xres = sb("xres", [128, D], F32)
        pre = sb("pre", [128, D], F32)
        mean = sb("mean", [128, 1], F32)
        nmean = sb("nmean", [128, 1], F32)
        var = sb("var", [128, 1], F32)
        rstd = sb("rstd", [128, 1], F32)
        xout = sb("xout", [128, D], F32)
        xout_bf = sb("xout_bf", [128, D], BF16)
        ident = sb("ident", [128, 128], BF16)
        ones1 = sb("ones1", [1, 128], F32)
        x1fm_t = sb("x1fm_t", [128, 8 * 128], BF16)

        pb = [psum(f"pb{i}", [128, 512]) for i in range(6)]
        ptr = psum("ptr", [128, 128], BF16)

        make_ident(sc, ident)
        sc.op("pool", lambda g: g.memset(ones1[:], 1.0), inc="pool")
        init_done = sc.c("pool")

        es1 = ExitStack()
        with es1:
            yfm_t = es1.enter_context(nc.sbuf_tensor("B_yfm_t", [128, NCJ * 128], BF16))
            woT_sb = es1.enter_context(nc.sbuf_tensor("B_woT_sb", [128, NCJ * D], BF16))
            for j in range(NCJ):
                sc.op("pool", lambda g, j=j: g.dma_start(
                    out=woT_sb[:, j * D:(j + 1) * D],
                    in_=wo2_d[j * 128:(j + 1) * 128, :]), inc="ld", dma=True)
            ld_w = sc.c("ld")

            bv = sc.op("pool", lambda g: g.dma_start(out=bo_sb[:], in_=bo_d), inc="ld", dma=True)
            for oj in range(2):
                mm = sc.op("pe", lambda t, oj=oj: t.matmul(
                    pb[4][:, 0:512], ones1[:], bo_sb[:, oj * 512:(oj + 1) * 512],
                    start=True, stop=True),
                    waits=[("ld", bv), ("pool", init_done)] + ([("act", sc.c("act"))] if oj else []),
                    inc="pe")
                sc.op("act", lambda a, oj=oj: a.copy(
                    out=bo_bc[:, oj * 512:(oj + 1) * 512], in_=pb[4][:, 0:512]),
                    waits=[("pe", mm)], inc="act")
            bo_done = sc.c("act")

            NT1 = TPC // 128
            for tt in range(NT1):
                xv = sc.op("sync", lambda e, tt=tt: e.dma_start(
                    out=xres[:], in_=x_sl[tt * 128:(tt + 1) * 128, :]),
                    waits=([("dve", sc.c("dve"))] if tt else []), inc="ld2", dma=True)
                ccw = [("cc", dram["cc_wait"])] if fused else []
                yv = sc.op("sync", lambda e, tt=tt: e.dma_start(
                    out=yfm_t[:].rearrange("p (j t) -> p j t", j=NCJ),
                    in_=yfm[:, tt * 128:(tt + 1) * 128].rearrange("(j p) t -> p j t", p=128)),
                    waits=([("pe", sc.c("pe"))] if tt else []) + ccw, inc="ld2", dma=True)
                mms = []
                for oj in range(2):
                    for j in range(NCJ):
                        w = []
                        if j == 0:
                            w = [("ld", ld_w), ("ld2", max(xv, yv))]
                            if tt > 0:
                                w.append(("dve", sc.c("dve")))
                        mm = sc.op("pe", lambda t, oj=oj, j=j: t.matmul(
                            pb[oj][:],
                            yfm_t[:, j * 128:(j + 1) * 128],
                            woT_sb[:, j * D + oj * 512: j * D + (oj + 1) * 512],
                            start=(j == 0), stop=(j == NCJ - 1)),
                            waits=w, inc="pe" if j == NCJ - 1 else None)
                    mms.append(mm)
                for oj in range(2):
                    sc.op("dve", lambda v, oj=oj: v.tensor_add(
                        out=pre[:, oj * 512:(oj + 1) * 512], in0=pb[oj][:],
                        in1=bo_bc[:, oj * 512:(oj + 1) * 512]),
                        waits=[("pe", mms[oj]), ("act", bo_done)], inc="dve")
                v1 = sc.op("dve", lambda v: v.tensor_add(out=pre[:], in0=pre[:], in1=xres[:]),
                           waits=[("ld2", max(xv, yv))], inc="dve")
                v2 = sc.op("dve", lambda v: v.reduce_sum(out=mean[:, 0:1], in_=pre[:], axis=AXL.X),
                           inc="dve")
                v3 = sc.op("dve", lambda v: v.tensor_scalar_mul(
                    out=nmean[:, 0:1], in0=mean[:, 0:1], scalar1=-1.0 / D),
                    waits=[("dve", v2)], inc="dve")
                a1 = sc.op("act", lambda a: a.activation(
                    out=xout[:], in_=pre[:], func=AF.Square, bias=nmean[:, 0:1], scale=1.0,
                    accum_out=var[:, 0:1]),
                    waits=[("dve", v3), ("st1", sc.c("st1")), ("st", sc.c("st"))], inc="act")
                v4 = sc.op("dve", lambda v: v.tensor_scalar(
                    out=var[:, 0:1], in0=var[:, 0:1], scalar1=1.0 / D, scalar2=1e-5,
                    op0=ALU.mult, op1=ALU.add),
                    waits=[("act", a1)], inc="dve")
                a2 = sc.op("act", lambda a: a.activation(
                    out=rstd[:, 0:1], in_=var[:, 0:1], func=AF.Sqrt),
                    waits=[("dve", v4)], inc="act")
                v5 = sc.op("dve", lambda v: v.reciprocal(out=rstd[:, 0:1], in_=rstd[:, 0:1]),
                           waits=[("act", a2)], inc="dve")
                v6 = sc.op("dve", lambda v: v.tensor_scalar(
                    out=xout[:], in0=pre[:], scalar1=nmean[:, 0:1], scalar2=rstd[:, 0:1],
                    op0=ALU.add, op1=ALU.mult),
                    waits=[("dve", v5)], inc="dve")
                v7 = sc.op("dve", lambda v: v.tensor_copy(out=xout_bf[:], in_=xout[:]),
                           inc="dve")
                for sj in range(8):
                    tv = sc.op("pe", lambda t, sj=sj: t.transpose(
                        ptr[:], xout_bf[:, sj * 128:(sj + 1) * 128], ident[:]),
                        waits=[("dve", v7), ("pool", init_done)] +
                              ([("dve", sc.c("dve"))] if sj else []), inc="pe")
                    sc.op("dve", lambda v, sj=sj: v.tensor_copy(
                        out=x1fm_t[:, sj * 128:(sj + 1) * 128], in_=ptr[:]),
                        waits=[("pe", tv)], inc="dve")
                vlast = sc.c("dve")
                sc.op("sync", lambda e, tt=tt: e.dma_start(
                    out=x1_dram[tt * 128:(tt + 1) * 128, :], in_=xout[:]),
                    waits=[("dve", vlast)], inc="st1", dma=True)
                sc.op("sync", lambda e, tt=tt: e.dma_start(
                    out=x1fm_dram[:, tt * 128:(tt + 1) * 128].rearrange("(j p) t -> p j t", p=128),
                    in_=x1fm_t[:].rearrange("p (j t) -> p j t", j=8)),
                    waits=[("dve", vlast)], inc="st1", dma=True)

            b1_st = sc.c("st1")
            sc.flush(block)
        es2 = ExitStack()
        es2.__enter__()
        wg_sb = es2.enter_context(nc.sbuf_tensor("B_wg_sb", [128, 8 * 3072], BF16))
        wu_sb = es2.enter_context(nc.sbuf_tensor("B_wu_sb", [128, 8 * 3072], BF16))
        wd_sb = es2.enter_context(nc.sbuf_tensor("B_wd_sb", [128, 24 * D], BF16))
        stage_i = [es2.enter_context(nc.sbuf_tensor(f"B_stage_i{i}", [128, 512], I32))
                   for i in range(2)]
        h_fm = es2.enter_context(nc.sbuf_tensor("B_h_fm", [128, 24 * 512], BF16))
        silu_t = es2.enter_context(nc.sbuf_tensor("B_silu_t", [128, 512], F32))
        x1fm_blk = es2.enter_context(nc.sbuf_tensor("B_x1fm_blk", [128, 8 * 512], BF16))
        dq_jobs = []
        for (wsb, wdr, nch, wid) in ((wg_sb, wgT_d, 8, 3072), (wu_sb, wuT_d, 8, 3072),
                                     (wd_sb, wdT_d, 24, D)):
            for j in range(nch):
                for cc2 in range(wid // 512):
                    dq_jobs.append((wsb, wdr, j, wid, cc2))
        dq_prev = {}
        for i, (wsb, wdr, j, wid, cc2) in enumerate(dq_jobs):
            st = i % 2
            w = [dq_prev[st]] if st in dq_prev else []
            lv = sc.op("sync", lambda g, wdr=wdr, j=j, cc2=cc2, st=st: g.dma_start(
                out=stage_i[st][:], in_=wdr[j * 128:(j + 1) * 128, cc2 * 512:(cc2 + 1) * 512]),
                waits=w, inc="lst", dma=True)
            eng = "dve" if (i % 2) == 0 else "pool"
            dq_prev[st] = (eng, sc.op(eng, lambda v, wsb=wsb, j=j, wid=wid, cc2=cc2, st=st: v.tensor_scalar_mul(
                out=wsb[:, j * wid + cc2 * 512: j * wid + (cc2 + 1) * 512], in0=stage_i[st][:],
                scalar1=1.0 / 127.0),
                waits=[("lst", lv)], inc=eng))
        dq_done = sc.c("dve")
        dq_done_pool = sc.c("pool")

        NB = TPC // 512
        for bb in range(NB):
            xv = sc.op("sync", lambda e, bb=bb: e.dma_start(
                out=x1fm_blk[:].rearrange("p (j t) -> p j t", j=8),
                in_=x1fm_dram[:, bb * 512:(bb + 1) * 512].rearrange("(j p) t -> p j t", p=128)),
                waits=[("st1", b1_st)] + ([("dve", sc.c("dve"))] if bb else []),
                inc="ld2", dma=True)
            for hj in range(24):
                gmm = umm = None
                for j in range(8):
                    w = []
                    if j == 0:
                        w = [("ld2", xv), ("dve", dq_done), ("pool", dq_done_pool)]
                        if hj >= 2:
                            w.append(("dve", sc.c("dve")))
                    gmm = sc.op("pe", lambda t, hj=hj, j=j: t.matmul(
                        pb[hj % 2][:],
                        wg_sb[:, j * 3072 + hj * 128: j * 3072 + (hj + 1) * 128],
                        x1fm_blk[:, j * 512:(j + 1) * 512],
                        start=(j == 0), stop=(j == 7)),
                        waits=w, inc="pe" if j == 7 else None)
                for j in range(8):
                    umm = sc.op("pe", lambda t, hj=hj, j=j: t.matmul(
                        pb[2 + hj % 2][:],
                        wu_sb[:, j * 3072 + hj * 128: j * 3072 + (hj + 1) * 128],
                        x1fm_blk[:, j * 512:(j + 1) * 512],
                        start=(j == 0), stop=(j == 7)),
                        inc="pe" if j == 7 else None)
                a1 = sc.op("act", lambda a, hj=hj: a.activation(
                    out=silu_t[:], in_=pb[hj % 2][:], func=AF.Silu),
                    waits=[("pe", gmm)] + ([("dve", sc.c("dve"))] if hj >= 1 else []),
                    inc="act")
                sc.op("dve", lambda v, hj=hj: v.tensor_mul(
                    out=h_fm[:, hj * 512:(hj + 1) * 512], in0=pb[2 + hj % 2][:],
                    in1=silu_t[:]),
                    waits=[("act", a1), ("pe", umm)], inc="dve")
            hdone = sc.c("dve")
            for st_i in range(4):
                tt = bb * 4 + st_i
                xv2 = sc.op("sync", lambda e, tt=tt: e.dma_start(
                    out=xres[:], in_=x1_dram[tt * 128:(tt + 1) * 128, :]),
                    waits=[("st1", b1_st), ("dve", sc.c("dve"))], inc="ld2", dma=True)
                dmm = []
                for oj in range(2):
                    for hj in range(24):
                        w = []
                        if hj == 0:
                            w = [("dve", hdone)]
                            w.append(("dve", sc.c("dve")))
                        mm = sc.op("pe", lambda t, oj=oj, hj=hj, st_i=st_i: t.matmul(
                            pb[4 + oj][:],
                            h_fm[:, hj * 512 + st_i * 128: hj * 512 + (st_i + 1) * 128],
                            wd_sb[:, hj * D + oj * 512: hj * D + (oj + 1) * 512],
                            start=(hj == 0), stop=(hj == 23)),
                            waits=w, inc="pe" if hj == 23 else None)
                    dmm.append(mm)
                for oj in range(2):
                    sc.op("dve", lambda v, oj=oj: v.tensor_add(
                        out=pre[:, oj * 512:(oj + 1) * 512], in0=pb[4 + oj][:],
                        in1=xres[:, oj * 512:(oj + 1) * 512]),
                        waits=[("pe", dmm[oj]), ("ld2", xv2)], inc="dve")
                v2 = sc.op("dve", lambda v: v.reduce_sum(out=mean[:, 0:1], in_=pre[:], axis=AXL.X),
                           inc="dve")
                v3 = sc.op("dve", lambda v: v.tensor_scalar_mul(
                    out=nmean[:, 0:1], in0=mean[:, 0:1], scalar1=-1.0 / D),
                    waits=[("dve", v2)], inc="dve")
                a1 = sc.op("act", lambda a: a.activation(
                    out=xout[:], in_=pre[:], func=AF.Square, bias=nmean[:, 0:1], scale=1.0,
                    accum_out=var[:, 0:1]),
                    waits=[("dve", v3), ("st", sc.c("st")), ("st1", sc.c("st1"))], inc="act")
                v4 = sc.op("dve", lambda v: v.tensor_scalar(
                    out=var[:, 0:1], in0=var[:, 0:1], scalar1=1.0 / D, scalar2=1e-5,
                    op0=ALU.mult, op1=ALU.add),
                    waits=[("act", a1)], inc="dve")
                a2 = sc.op("act", lambda a: a.activation(
                    out=rstd[:, 0:1], in_=var[:, 0:1], func=AF.Sqrt),
                    waits=[("dve", v4)], inc="act")
                v5 = sc.op("dve", lambda v: v.reciprocal(out=rstd[:, 0:1], in_=rstd[:, 0:1]),
                           waits=[("act", a2)], inc="dve")
                v6 = sc.op("dve", lambda v: v.tensor_scalar(
                    out=xout[:], in0=pre[:], scalar1=nmean[:, 0:1], scalar2=rstd[:, 0:1],
                    op0=ALU.add, op1=ALU.mult),
                    waits=[("dve", v5)], inc="dve")
                if not fused:
                    sc.op("sync", lambda e, tt=tt: e.dma_start(
                        out=dram["x2o"][tt * 128:(tt + 1) * 128, :], in_=xout[:]),
                        waits=[("dve", v6)], inc="st", dma=True)
                else:
                    sc.op("sync", lambda e, tt=tt: e.dma_start(
                        out=x2_dram[tt * 128:(tt + 1) * 128, :], in_=xout[:]),
                        waits=[("dve", v6)], inc="st1", dma=True)
                    v7 = sc.op("dve", lambda v: v.tensor_copy(out=xout_bf[:], in_=xout[:]),
                               waits=[("dve", v6)], inc="dve")
                    for sj in range(8):
                        tv = sc.op("pe", lambda t, sj=sj: t.transpose(
                            ptr[:], xout_bf[:, sj * 128:(sj + 1) * 128], ident[:]),
                            waits=[("dve", sc.c("dve"))], inc="pe")
                        sc.op("dve", lambda v, sj=sj: v.tensor_copy(
                            out=x1fm_t[:, sj * 128:(sj + 1) * 128], in_=ptr[:]),
                            waits=[("pe", tv)], inc="dve")
                    vl2 = sc.c("dve")
                    sc.op("sync", lambda e, tt=tt: e.dma_start(
                        out=x2fm_dram[:, tt * 128:(tt + 1) * 128].rearrange("(j p) t -> p j t", p=128),
                        in_=x1fm_t[:].rearrange("p (j t) -> p j t", j=8)),
                        waits=[("dve", vl2)], inc="st1", dma=True)
        if not fused:
            sc.op("sync", lambda e: None, waits=[("st", sc.c("st"))])
        else:
            sc.op("pool", lambda g, ai=dram["ag2_in"], ao=dram["ag2_out"]: g.collective_compute(
                "AllGather", ALU.bypass, replica_groups=[[0, 1, 2, 3], [4, 5, 6, 7]],
                ins=[ai.opt()], outs=[ao.opt()]),
                waits=[("st1", sc.c("st1"))], inc="cc")
        sc.flush(block)
        es2.__exit__(None, None, None)




def build_C(sign=1.0):
    nc = bass.Bass("TRN2", target_bir_lowering=False)
    dram = {}
    dram["x2fm"] = nc.dram_tensor("x2fm", [D, T], F32, kind="ExternalInput").ap()
    for w in ("q", "k", "v"):
        dram[f"w{w}C"] = nc.dram_tensor(f"w{w}C", [D, HPC * HD], F32, kind="ExternalInput").ap()
    dram["bqkvC"] = nc.dram_tensor("bqkvC", [3, HPC * HD], F32, kind="ExternalInput").ap()
    dram["gain"] = nc.dram_tensor("gain", [1, 1], F32, kind="ExternalInput").ap()
    dram["yfmC"] = nc.dram_tensor("yfm", [HPC * HD, T], BF16, kind="ExternalOutput").ap()
    es0 = ExitStack()
    with es0:
        sc = Sched(nc, es0)
        block = es0.enter_context(nc.Block())
        phase_C(nc, sc, block, dram, sign, fused=False)
    return nc


def phase_C(nc, sc, block, dram, sign, fused):
    wC = {w: dram[f"w{w}C"] for w in ("q", "k", "v")}
    bqkv = dram["bqkvC"]
    gain_d = dram["gain"]
    xfm = None if fused else dram["x2fm"]
    ag2_out = dram.get("ag2_out")

    TT = 512
    NTT = T // TT
    NG = T // 64
    NQT = T // 512
    es = ExitStack()
    with es:
        def sb(name, shape, dt=F32):
            return es.enter_context(nc.sbuf_tensor("C_" + name, shape, dt))

        bias_sb = sb("bias", [128, 6], F32)
        w_sb = {w: sb("w" + w, [128, 8 * 256], BF16) for w in ("q", "k", "v")}
        x_t = [sb(f"x_t{i}", [128, 8 * TT], BF16) for i in range(2)]
        q65 = [sb(f"q65_{i}", [65, T], BF16) for i in range(2)]
        k65 = [sb(f"k65_{i}", [65, T], BF16) for i in range(2)]
        vt = sb("vt", [128, 2 * NCHUNK * 65], BF16)
        ident = sb("ident", [128, 128], BF16)
        identf = sb("identf", [128, 128], F32)
        onesf = sb("onesf", [1, 128], F32)
        ksum = sb("ksum", [64, 2], F32)
        ksumb = sb("ksumb", [64, 2], BF16)
        NQS = T // 128
        absum_h = [sb(f"absum{i}", [128, NQS], F32) for i in range(2)]
        epsc_h = [sb(f"epsc{i}", [128, NQS], F32) for i in range(2)]
        acc8 = [sb(f"acc8_{i}", [128, 8], F32) for i in range(2)]
        uabs = sb("uabs", [128, 1024], BF16)
        uabs_d = sb("uabs_d", [128, 1024], BF16)
        v_stage = sb("v_stage", [128, 512], F32)
        gaincol = sb("gaincol", [128, 1], F32)
        g4inv = sb("g4inv", [128, 1], F32)
        scr = sb("scr", [128, 2], F32)
        madt = sb("madt", [128, NQS], F32)
        m2t = sb("m2t", [128, NQS], F32)
        u_t = [sb(f"u_t{i}", [128, 1024], BF16) for i in range(3)]
        r_t = [sb(f"r_t{i}", [128, 1024], BF16) for i in range(3)]
        dmask = [sb(f"dmask{i}", [128, 512], BF16) for i in range(4)]
        epsrow = [sb(f"epsrow{i}", [1, 512], F32) for i in range(2)]
        denrow = [sb(f"denrow{i}", [1, 512], F32) for i in range(2)]
        rden = [sb(f"rdenC{i}", [1, 512], F32) for i in range(2)]
        bcast_sb = [sb(f"bcastC{i}", [64, 512], F32) for i in range(2)]
        bcast_sb2 = [sb(f"bcastD{i}", [64, 512], F32) for i in range(2)]
        y_t = [sb(f"y_t{i}", [128, 512], BF16) for i in range(2)]

        pa = es.enter_context(nc.psum_tensor("C_pa", [128, 2048], F32))
        pa2 = es.enter_context(nc.psum_tensor("C_pa2", [128, 2048], F32))
        ptrf = pa2[:, 1536:1664]
        pproj = {0: [pa[:, 0:512], pa[:, 512:1024], pa[:, 1024:1536]],
                 1: [pa2[:, 0:512], pa2[:, 512:1024], pa2[:, 1024:1536]]}
        psF = [pa[:, 0:1024], pa[:, 1024:2048]]
        psc2 = [pa[:, 0:1024], pa[:, 1024:2048]]
        py = [pa2[0:65, 0:512], pa2[0:65, 512:1024]]
        prow = [pa2[:, 1024:1536], pa2[:, 1536:2048]]
        pmucol = pa2[:, 1536:1664]
        pgain = pa2[:, 2047:2048]

        make_ident(sc, ident)
        v = sc.op("pool", lambda g: g.memset(identf[:], 1.0), inc="pool")
        sc.op("pool", lambda g: g.affine_select(
            out=identf[:], in_=identf[:], pattern=[[-1, 128]],
            compare_op=ALU.is_equal, fill=0.0, base=0, channel_multiplier=1),
            waits=[("pool", v)], inc="pool")
        sc.op("pool", lambda g: g.memset(onesf[:], 1.0), inc="pool")
        sc.op("pool", lambda g: g.memset(vt[:], 1.0), inc="pool")
        for i in range(2):
            sc.op("pool", lambda g, i=i: g.memset(k65[i][:], 1.0), inc="pool")
        for j in range(4):
            v = sc.op("pool", lambda g, j=j: g.memset(dmask[j][:], 1.0), inc="pool")
            v = sc.op("pool", lambda g, j=j: g.affine_select(
                out=dmask[j][:], in_=dmask[j][:], pattern=[[1, 512]],
                compare_op=ALU.is_ge, fill=0.0, base=-128 * j, channel_multiplier=-1),
                waits=[("pool", v)], inc="pool")
        init_done = sc.c("pool")

        for w in ("q", "k", "v"):
            for j in range(8):
                sc.op("pool", lambda g, w=w, j=j: g.dma_start(
                    out=w_sb[w][:, j * 256:(j + 1) * 256],
                    in_=wC[w][j * 128:(j + 1) * 128, :]), inc="ld", dma=True)
        for wi in range(3):
            for hh in range(2):
                sc.op("pool", lambda g, wi=wi, hh=hh: g.dma_start(
                    out=bias_sb[:, wi * 2 + hh: wi * 2 + hh + 1],
                    in_=bqkv[wi:wi + 1, hh * 128:(hh + 1) * 128].rearrange("a e -> e a")),
                    inc="ld", dma=True)
        gv = sc.op("pool", lambda g: g.dma_start(out=scr[0:1, 0:1], in_=gain_d),
                   inc="ld", dma=True)
        ld_w = sc.c("ld")

        gmm = sc.op("pe", lambda t: t.matmul(
            pgain, onesf[:], scr[0:1, 0:1], start=True, stop=True),
            waits=[("ld", gv), ("pool", init_done)], inc="pe")
        a_g = sc.op("act", lambda a: a.copy(out=gaincol[:, 0:1], in_=pgain),
                    waits=[("pe", gmm)], inc="act")
        v1 = sc.op("dve", lambda v: v.tensor_mul(out=scr[:, 0:1], in0=gaincol[:, 0:1],
                                                 in1=gaincol[:, 0:1]),
                   waits=[("act", a_g)], inc="dve")
        v2 = sc.op("dve", lambda v: v.tensor_mul(out=scr[:, 1:2], in0=scr[:, 0:1],
                                                 in1=scr[:, 0:1]),
                   waits=[("dve", v1)], inc="dve")
        g4_done = sc.op("dve", lambda v: v.reciprocal(out=g4inv[:, 0:1], in_=scr[:, 1:2]),
                        waits=[("dve", v2)], inc="dve")

        pair_end = None
        yst = {}
        for pair in range(C_PAIRS):
            if pair == 1:
                sc.op("pe", lambda t: t.nop(), waits=pair_end, inc="pe")
            pe_tile_done = {}
            dve_tile_done = {}
            for ti in range(NTT):
                buf = ti % 2
                if ti >= 2:
                    waits = [("pe", pe_tile_done[ti - 2])]
                elif pair == 1:
                    waits = [("pe", prev_proj_pe)]
                else:
                    waits = []
                if fused:
                    rr = ti // 4
                    t0 = (ti % 4) * TT
                    xv = sc.op("pool", lambda g, buf=buf, rr=rr, t0=t0: g.dma_start(
                        out=x_t[buf][:].rearrange("p (j t) -> p j t", j=8),
                        in_=ag2_out[rr * D:(rr + 1) * D, t0:t0 + TT].rearrange("(j p) t -> p j t", p=128)),
                        waits=waits + [("cc", dram["cc_wait"])], inc=f"lx{buf}", dma=True)
                else:
                    xv = sc.op("pool", lambda g, buf=buf, ti=ti: g.dma_start(
                        out=x_t[buf][:].rearrange("p (j t) -> p j t", j=8),
                        in_=xfm[:, ti * TT:(ti + 1) * TT].rearrange("(j p) t -> p j t", p=128)),
                        waits=waits, inc=f"lx{buf}", dma=True)
                for wi, w in enumerate(("q", "k", "v")):
                    bank = pproj[buf][wi]
                    bcol = bias_sb[:, wi * 2 + pair: wi * 2 + pair + 1]
                    for j in range(8):
                        wts = []
                        if j == 0:
                            wts = [("ld", ld_w), (f"lx{buf}", xv)]
                            if ti > 0:
                                wts.append(("dve", dve_tile_done[ti - 1]))
                        mm = sc.op("pe", lambda t, bank=bank, w=w, j=j, buf=buf, pair=pair: t.matmul(
                            bank, w_sb[w][:, j * 256 + pair * 128: j * 256 + pair * 128 + 128],
                            x_t[buf][:, j * TT:(j + 1) * TT],
                            start=(j == 0), stop=(j == 7)),
                            waits=wts, inc="pe" if j == 7 else None)
                    if w == "q":
                        for hh in range(2):
                            dst = q65[hh][0:64, ti * TT:(ti + 1) * TT]
                            sc.op("dve", lambda v, bank=bank, bcol=bcol, hh=hh, dst=dst: v.tensor_scalar(
                                out=dst, in0=bank[hh * 64:(hh + 1) * 64, :],
                                scalar1=bcol[hh * 64:(hh + 1) * 64, :], scalar2=float(sign),
                                op0=ALU.add, op1=ALU.mult),
                                waits=[("pe", mm), ("ld", ld_w)], inc="dve")
                    elif w == "k":
                        for hh in range(2):
                            dst = k65[hh][0:64, ti * TT:(ti + 1) * TT]
                            sc.op("dve", lambda v, bank=bank, bcol=bcol, hh=hh, dst=dst: v.tensor_scalar_add(
                                out=dst, in0=bank[hh * 64:(hh + 1) * 64, :],
                                scalar1=bcol[hh * 64:(hh + 1) * 64, :]),
                                waits=[("pe", mm), ("ld", ld_w)], inc="dve")
                    else:
                        sc.op("dve", lambda v, bank=bank, bcol=bcol: v.tensor_scalar_add(
                            out=v_stage[:], in0=bank, scalar1=bcol),
                            waits=[("pe", mm), ("ld", ld_w)], inc="dve")
                        for sti in range(4):
                            c = ti * 4 + sti
                            tv = sc.op("pe", lambda t, sti=sti: t.transpose(
                                ptrf, v_stage[:, sti * 128:(sti + 1) * 128], identf[:]),
                                waits=[("dve", sc.c("dve")), ("pool", init_done)], inc="pe")
                            for hh in range(2):
                                sc.op("dve", lambda v, hh=hh, c=c: v.tensor_copy(
                                    out=vt[:, (hh * NCHUNK + c) * 65:(hh * NCHUNK + c) * 65 + 64],
                                    in_=ptrf[:, hh * 64:(hh + 1) * 64]),
                                    waits=[("pe", tv)], inc="dve")
                pe_tile_done[ti] = sc.c("pe")
                dve_tile_done[ti] = sc.c("dve")
            proj_dve = sc.c("dve")
            prev_proj_pe = sc.c("pe")

            for hh in range(2):
                r0 = sc.op("dve", lambda v, hh=hh: v.reduce_sum(
                    out=ksum[:, hh:hh + 1], in_=k65[hh][0:64, :], axis=AXL.X),
                    waits=[("dve", proj_dve)], inc="dve")
                sc.op("dve", lambda v, hh=hh: v.tensor_copy(
                    out=ksumb[:, hh:hh + 1], in_=ksum[:, hh:hh + 1]),
                    waits=[("dve", r0)], inc="dve")
            ksum_done = sc.c("dve")

            murow_ev = {}
            for hh in range(2):
                for tt in range(NQT):
                    idx = hh * NQT + tt
                    bankr = prow[idx % 2][0:1, 0:512]
                    w = ([("dve", ksum_done), ("act", a_g)] if idx < 2
                         else [("act", murow_ev[idx - 2])])
                    mm = sc.op("pe", lambda t, hh=hh, tt=tt, bankr=bankr: t.matmul(
                        bankr, ksumb[:, hh:hh + 1], q65[hh][0:64, tt * 512:(tt + 1) * 512],
                        start=True, stop=True),
                        waits=w, inc="pe")
                    murow_ev[idx] = sc.op("act", lambda a, hh=hh, tt=tt, bankr=bankr: a.activation(
                        out=q65[hh][64:65, tt * 512:(tt + 1) * 512], in_=bankr,
                        func=AF.Copy, scale=-1.0 / T),
                        waits=[("pe", mm)], inc="act")
            murow_act = sc.c("act")

            last_read = {0: None, 1: None}
            red_done = {}
            eps_done = {}
            for hh in range(2):
                for qs in range(NQS):
                    u = hh * NQS + qs
                    upar = u % 2
                    for kq in range(8):
                        t_i = u * 8 + kq
                        buf = t_i % 2
                        for kk in range(2):
                            w = []
                            if kk == 0:
                                if last_read[buf] is not None:
                                    w.append(last_read[buf])
                                if u == 0 and kq == 0:
                                    w.append(("act", murow_act))
                            mm = sc.op("pe", lambda t, hh=hh, qs=qs, buf=buf, kq=kq, kk=kk: t.matmul(
                                psF[buf][:, kk * 512:(kk + 1) * 512],
                                q65[hh][:, qs * 128:(qs + 1) * 128],
                                k65[hh][:, (kq * 2 + kk) * 512:(kq * 2 + kk) * 512 + 512],
                                start=True, stop=True),
                                waits=w, inc="pe" if kk == 1 else None)
                        if buf == 0:
                            w = [("pe", mm)]
                            if u >= 2 and kq < 2:
                                w.append(("dve", red_done[u - 2]))
                            rd = sc.op("dve", lambda v, buf=buf, upar=upar, kq=kq: v.reduce_sum(
                                out=acc8[upar][:, kq:kq + 1], in_=psF[buf],
                                axis=AXL.X, apply_absolute_value=True),
                                waits=w, inc="dve")
                            last_read[buf] = ("dve", rd)
                        else:
                            w = [("pe", mm)]
                            if u >= 2 and kq < 2:
                                w.append(("dve", red_done[u - 2]))
                            rd = sc.op("act", lambda a, buf=buf, upar=upar, kq=kq: a.activation(
                                out=uabs[:], in_=psF[buf],
                                func=AF.Abs, accum_out=acc8[upar][:, kq:kq + 1]),
                                waits=w, inc="act")
                            last_read[buf] = ("act", rd)
                    w = [("act", sc.c("act")), ("dve", sc.c("dve"))]
                    red_done[u] = sc.op("dve", lambda v, hh=hh, qs=qs, upar=upar: v.reduce_sum(
                        out=absum_h[hh][:, qs:qs + 1], in_=acc8[upar][:], axis=AXL.X),
                        waits=w, inc="dve")
            stats_dve = sc.c("dve")
            stats_act = sc.c("act")

            for hh in range(2):
                w1 = sc.op("dve", lambda v, hh=hh: v.tensor_scalar(
                    out=madt[:], in0=absum_h[hh][:], scalar1=1.0 / T, scalar2=1e-6,
                    op0=ALU.mult, op1=ALU.add),
                    waits=[("dve", stats_dve)] if hh == 0 else [], inc="dve")
                sc.op("dve", lambda v: v.tensor_mul(out=m2t[:], in0=madt[:], in1=madt[:]),
                      waits=[("dve", w1)], inc="dve")
                sc.op("dve", lambda v: v.tensor_mul(out=madt[:], in0=m2t[:], in1=m2t[:]),
                      inc="dve")
                eps_done[hh] = sc.op("dve", lambda v, hh=hh: v.tensor_scalar(
                    out=epsc_h[hh][:], in0=madt[:], scalar1=g4inv[:, 0:1], scalar2=1e-6,
                    op0=ALU.mult, op1=ALU.mult),
                    waits=[("dve", g4_done)], inc="dve")

            relu_done = {}
            u_free = {}
            r_free = {}
            sq_done = {}
            bc_free = {}
            avmm = {}
            eps_ev = {}
            eps_free = {}
            prow_free = {}
            ymult_done = 0
            for qt in range(NQT):
                for hh in range(2):
                    for jj in range(4):
                        qs = qt * 4 + jj
                        w = []
                        if jj == 0:
                            w = [("dve", eps_done[hh])]
                            if hh in prow_free:
                                w.append(prow_free[hh])
                        tv = sc.op("pe", lambda t, hh=hh, qs=qs, jj=jj: t.transpose(
                            prow[hh][0:1, jj * 128:(jj + 1) * 128],
                            epsc_h[hh][:, qs:qs + 1], identf[:]),
                            waits=w, inc="pe" if jj == 3 else None)
                    eps_ev[hh] = sc.op("act", lambda a, hh=hh: a.copy(
                        out=epsrow[hh][:], in_=prow[hh][0:1, 0:512]),
                        waits=[("pe", tv)] +
                              ([("dve", eps_free[hh])] if hh in eps_free else []),
                        inc="act")

                kcmax = 4 * qt + 4

                def sched_av(kc, kcmax=kcmax, qt=qt):
                    buf3 = kc % 3
                    for hh in range(2):
                        w = [("dve", sq_done[(kc, "r")])]
                        if kc == 0:
                            w.append(("dve", ymult_done))
                        avmm[hh] = sc.op("pe", lambda t, hh=hh, kc=kc, buf3=buf3, kcmax=kcmax: t.matmul(
                            py[hh],
                            vt[:, (hh * NCHUNK + kc) * 65:(hh * NCHUNK + kc) * 65 + 65],
                            r_t[buf3][:, hh * 512:(hh + 1) * 512],
                            start=(kc == 0), stop=(kc == kcmax - 1)),
                            waits=(w if hh == 0 else []), inc="pe")
                    r_free[buf3] = avmm[1]

                for kc in range(kcmax):
                    buf = kc % 2
                    buf3 = kc % 3
                    w = []
                    if (buf,) in relu_done:
                        w.append(("act", relu_done[(buf,)]))
                    elif qt == 0 and kc < 2:
                        w.append(("dve", stats_dve))
                        w.append(("act", stats_act))
                    for hh in range(2):
                        mm = sc.op("pe", lambda t, buf=buf, hh=hh, kc=kc, qt=qt: t.matmul(
                            psc2[buf][:, hh * 512:(hh + 1) * 512],
                            k65[hh][:, kc * 128:(kc + 1) * 128],
                            q65[hh][:, qt * 512:(qt + 1) * 512],
                            start=True, stop=True),
                            waits=(w if hh == 0 else []), inc="pe")
                    if kc >= 2:
                        sched_av(kc - 2)
                    rv = sc.op("act", lambda a, buf=buf, buf3=buf3: a.activation(
                        out=u_t[buf3][:], in_=psc2[buf][:], func=AF.Relu),
                        waits=[("pe", mm)] +
                              ([("dve", u_free[buf3])] if buf3 in u_free else []),
                        inc="act")
                    relu_done[(buf,)] = rv
                    d1 = sc.op("dve", lambda v, buf3=buf3: v.tensor_mul(
                        out=u_t[buf3][:], in0=u_t[buf3][:], in1=u_t[buf3][:]),
                        waits=[("act", rv)], inc="dve")
                    w = []
                    if buf3 in r_free:
                        w.append(("pe", r_free[buf3]))
                    d2 = sc.op("dve", lambda v, buf3=buf3: v.tensor_mul(
                        out=r_t[buf3][:], in0=u_t[buf3][:], in1=u_t[buf3][:]),
                        waits=w, inc="dve")
                    u_free[buf3] = d2
                    if kc >= 4 * qt:
                        for hh in range(2):
                            d2 = sc.op("dve", lambda v, buf3=buf3, kc=kc, qt=qt, hh=hh: v.tensor_mul(
                                out=r_t[buf3][:, hh * 512:(hh + 1) * 512],
                                in0=r_t[buf3][:, hh * 512:(hh + 1) * 512],
                                in1=dmask[kc - 4 * qt][:]),
                                inc="dve")
                    sq_done[(kc, "r")] = d2
                sched_av(kcmax - 2)
                sched_av(kcmax - 1)
                for hh in range(2):
                    d1 = sc.op("dve", lambda v, hh=hh: v.tensor_add(
                        out=denrow[hh][:], in0=py[hh][64:65, 0:512], in1=epsrow[hh][:]),
                        waits=[("act", eps_ev[hh]), ("pe", avmm[hh])], inc="dve")
                    eps_free[hh] = d1
                    bmm = sc.op("pe", lambda t, hh=hh: t.matmul(
                        prow[hh][0:64, 0:512], onesf[0:1, 0:64], denrow[hh][:],
                        start=True, stop=True),
                        waits=[("dve", d1), ("act", eps_ev[hh])], inc="pe")
                    bev = sc.op("act", lambda a, hh=hh: a.activation(
                        out=bcast_sb[hh][:], in_=prow[hh][0:64, 0:512], func=AF.Ln),
                        waits=[("pe", bmm)] +
                              ([("dve", bc_free[hh])] if hh in bc_free else []),
                        inc="act")
                    prow_free[hh] = ("act", bev)
                    e2 = sc.op("act", lambda a, hh=hh: a.activation(
                        out=bcast_sb2[hh][:], in_=bcast_sb[hh][:], func=AF.Exp, scale=-1.0),
                        waits=([("dve", bc_free[hh])] if hh in bc_free else []), inc="act")
                    rot = qt % 2
                    gq = pair * NQT + qt
                    wy = [("act", e2)]
                    if gq >= 2 and hh == 0:
                        wy.append(("st1", yst[gq - 2]))
                    ym = sc.op("dve", lambda v, hh=hh, rot=rot: v.tensor_mul(
                        out=y_t[rot][hh * 64:(hh + 1) * 64, :],
                        in0=py[hh][0:64, 0:512], in1=bcast_sb2[hh][:]),
                        waits=wy, inc="dve")
                    bc_free[hh] = ym
                    ymult_done = ym
                rot = qt % 2
                gq = pair * NQT + qt
                if not fused:
                    yst[gq] = sc.op("sync", lambda e, qt=qt, rot=rot, pair=pair: e.dma_start(
                        out=dram["yfmC"][pair * 128:(pair + 1) * 128, qt * 512:(qt + 1) * 512],
                        in_=y_t[rot][:]),
                        waits=[("dve", ymult_done)], inc="st1", dma=True)
                else:
                    qq = qt // 4
                    c0 = (qt % 4) * 512
                    for tgt in (qq, qq + 4):
                        yst[gq] = sc.op("sync", lambda e, tgt=tgt, c0=c0, rot=rot, pair=pair: e.dma_start(
                            out=dram["a2a3_in"][tgt * 256 + pair * 128: tgt * 256 + (pair + 1) * 128,
                                                c0:c0 + 512],
                            in_=y_t[rot][:]),
                            waits=[("dve", ymult_done)] if tgt == qq else [], inc="st1", dma=True)
            pair_end = [("act", sc.c("act")), ("dve", sc.c("dve")), ("pool", sc.c("pool")),
                        ("st1", sc.c("st1"))]

        if not fused:
            sc.op("sync", lambda e: None, waits=[("st1", sc.c("st1"))])
        else:
            def _a2a(g, a2a_in=dram["a2a3_in"], a2a_out=dram["a2a3_out"]):
                with allow_ici_all_to_all():
                    return g.collective_compute(
                        "AllToAll", ALU.bypass, replica_groups=[list(range(8))],
                        ins=[a2a_in.opt()], outs=[a2a_out.opt()])
            sc.op("pool", _a2a, waits=[("st1", sc.c("st1"))], inc="cc")
        sc.flush(block)





def build_D(stage=4):
    nc = bass.Bass("TRN2", target_bir_lowering=False)
    dram = {}
    dram["y1fm_sl"] = nc.dram_tensor("y1fm_sl", [D, TPC], BF16, kind="ExternalInput").ap()
    dram["x2_sl"] = nc.dram_tensor("x2_sl", [TPC, D], F32, kind="ExternalInput").ap()
    dram["wproj"] = nc.dram_tensor("wproj", [D, D], F32, kind="ExternalInput").ap()
    dram["bproj"] = nc.dram_tensor("bproj", [1, D], F32, kind="ExternalInput").ap()
    dram["wLT"] = nc.dram_tensor("wLT", [D, D], F32, kind="ExternalInput").ap()
    dram["bL"] = nc.dram_tensor("bL", [1, D], F32, kind="ExternalInput").ap()
    dram["ldT"] = nc.dram_tensor("ldT", [100, 4], F32, kind="ExternalInput").ap()
    dram["da"] = nc.dram_tensor("da", [100, D], F32, kind="ExternalInput").ap()
    dram["luT"] = nc.dram_tensor("luT", [4, 50], F32, kind="ExternalInput").ap()
    dram["uaT"] = nc.dram_tensor("uaT", [50, D], F32, kind="ExternalInput").ap()
    dram["x4o"] = nc.dram_tensor("x4o", [TPC, D], F32, kind="ExternalOutput").ap()
    dram["x3_scr"] = nc.dram_tensor("x3_scr", [TPC, D], F32).ap()
    dram["x3fm_scr"] = nc.dram_tensor("x3fm_scr", [D, TPC], BF16).ap()
    es0 = ExitStack()
    with es0:
        sc = Sched(nc, es0)
        block = es0.enter_context(nc.Block())
        phase_D(nc, sc, block, dram, fused=False, stage=stage)
    return nc


def phase_D(nc, sc, block, dram, fused, stage=4):
    bp_d = dram["bproj"]
    wl_d = dram["wLT"]
    bl_d = dram["bL"]
    ldT_d = dram["ldT"]
    da_d = dram["da"]
    luT_d = dram["luT"]
    uaT_d = dram["uaT"]
    x4o = dram["x4o"]
    x3_dram = dram["x3_scr"]
    x3fm_dram = dram["x3fm_scr"]
    if fused:
        yfm = dram["a2a3_out"]
        wp_d = dram["wp2"]
        NCJ = 16
        x_sl = dram["x2_scr"]
    else:
        yfm = dram["y1fm_sl"]
        wp_d = dram["wproj"]
        NCJ = 8
        x_sl = dram["x2_sl"]

    es = ExitStack()
    with es:
        def sb(name, shape, dt=F32):
            return es.enter_context(nc.sbuf_tensor("D_" + name, shape, dt))

        yfm_t = sb("yfm_t", [128, NCJ * 128], BF16)
        wp_sb = sb("wp_sb", [128, NCJ * D], BF16)
        weff = sb("weff", [128, 8 * D], BF16)
        bp_bc = sb("bp_bc", [128, D], F32)
        bl_bc = sb("bl_bc", [128, D], F32)
        bp_sb = sb("bp_sb", [1, D], F32)
        bl_sb = sb("bl_sb", [1, D], F32)
        xres = sb("xres", [128, D], F32)
        pre = sb("pre", [128, D], F32)
        mean = sb("mean", [128, 1], F32)
        nmean = sb("nmean", [128, 1], F32)
        var = sb("var", [128, 1], F32)
        rstd = sb("rstd", [128, 1], F32)
        xout = sb("xout", [128, D], F32)
        xout_bf = sb("xout_bf", [128, D], BF16)
        ident = sb("ident", [128, 128], BF16)
        onesf = sb("onesf", [1, 128], F32)
        x3fm_t = sb("x3fm_t", [128, 8 * 128], BF16)
        ldT_sb = sb("ldT_sb", [100, 4], BF16)
        da_sb = sb("da_sb", [100, D], BF16)
        luT_sb = sb("luT_sb", [4, 50], BF16)
        uaT_sb = sb("uaT_sb", [50, D], BF16)
        a1_sb = sb("a1_sb", [4, D], BF16)
        a2_sb = sb("a2_sb", [50, D], BF16)

        pb = [es.enter_context(nc.psum_tensor(f"D_pbD{i}", [128, 512], F32)) for i in range(4)]
        ptr = es.enter_context(nc.psum_tensor("D_ptrD", [128, 128], BF16))

        make_ident(sc, ident)
        sc.op("pool", lambda g: g.memset(onesf[:], 1.0), inc="pool")
        init_done = sc.c("pool")

        for j in range(NCJ):
            sc.op("pool", lambda g, j=j: g.dma_start(
                out=wp_sb[:, j * D:(j + 1) * D], in_=wp_d[j * 128:(j + 1) * 128, :]),
                inc="ld", dma=True)
        for nm, dst, srcd in (("ldT", ldT_sb, ldT_d), ("da", da_sb, da_d),
                              ("luT", luT_sb, luT_d), ("uaT", uaT_sb, uaT_d)):
            sc.op("pool", lambda g, dst=dst, srcd=srcd: g.dma_start(out=dst[:], in_=srcd),
                  inc="ld", dma=True)
        bv = sc.op("pool", lambda g: g.dma_start(out=bp_sb[:], in_=bp_d), inc="ld", dma=True)
        bv = sc.op("pool", lambda g: g.dma_start(out=bl_sb[:], in_=bl_d), inc="ld", dma=True)
        ld_w = sc.c("ld")

        evs = []
        for (row, bc) in ((bp_sb, bp_bc), (bl_sb, bl_bc)):
            for oj in range(2):
                mm = sc.op("pe", lambda t, row=row, oj=oj: t.matmul(
                    pb[3][:], onesf[:], row[:, oj * 512:(oj + 1) * 512],
                    start=True, stop=True),
                    waits=[("ld", ld_w), ("pool", init_done)] +
                          ([("act", evs[-1])] if evs else []),
                    inc="pe")
                evs.append(sc.op("act", lambda a, bc=bc, oj=oj: a.copy(
                    out=bc[:, oj * 512:(oj + 1) * 512], in_=pb[3][:]),
                    waits=[("pe", mm)], inc="act"))
        bias_done = evs[-1]

        prev = bias_done
        for oj in range(2 if stage >= 0 else 0):
            mm = sc.op("pe", lambda t, oj=oj: t.matmul(
                pb[3][0:4, :], ldT_sb[:], da_sb[:, oj * 512:(oj + 1) * 512],
                start=True, stop=True),
                waits=[("act", prev)], inc="pe")
            prev = sc.op("act", lambda a, oj=oj: a.copy(
                out=a1_sb[:, oj * 512:(oj + 1) * 512], in_=pb[3][0:4, :]),
                waits=[("pe", mm)], inc="act")
        for oj in range(2 if stage >= 1 else 0):
            mm = sc.op("pe", lambda t, oj=oj: t.matmul(
                pb[3][0:50, :], luT_sb[:], a1_sb[:, oj * 512:(oj + 1) * 512],
                start=True, stop=True),
                waits=[("act", prev)], inc="pe")
            prev = sc.op("act", lambda a, oj=oj: a.copy(
                out=a2_sb[:, oj * 512:(oj + 1) * 512], in_=pb[3][0:50, :]),
                waits=[("pe", mm)], inc="act")
        wl_stage = sb("wl_stage", [128, D], BF16)
        wdone = prev
        dv = None
        for j in range(8 if stage >= 2 else 0):
            lv = sc.op("pool", lambda g, j=j: g.dma_start(
                out=wl_stage[:], in_=wl_d[j * 128:(j + 1) * 128, :]),
                waits=[("dve", sc.c("dve"))], inc="lst", dma=True)
            for oj in range(2):
                mm = sc.op("pe", lambda t, j=j, oj=oj: t.matmul(
                    pb[2][:], a2_sb[:, j * 128:(j + 1) * 128],
                    uaT_sb[:, oj * 512:(oj + 1) * 512],
                    start=True, stop=True),
                    waits=[("act", wdone)] + ([("dve", dv)] if dv else []), inc="pe")
                dv = sc.op("dve", lambda v, j=j, oj=oj: v.tensor_add(
                    out=weff[:, j * D + oj * 512: j * D + (oj + 1) * 512],
                    in0=pb[2][:], in1=wl_stage[:, oj * 512:(oj + 1) * 512]),
                    waits=[("pe", mm), ("lst", lv)], inc="dve")
        weff_done = sc.c("dve")

        NT1 = TPC // 128
        def ln_tile(tt, src_pre, out_dram, fm_dram, fm_tile, extra_store=None):
            v2 = sc.op("dve", lambda v: v.reduce_sum(out=mean[:, 0:1], in_=src_pre, axis=AXL.X),
                       inc="dve")
            v3 = sc.op("dve", lambda v: v.tensor_scalar_mul(
                out=nmean[:, 0:1], in0=mean[:, 0:1], scalar1=-1.0 / D),
                waits=[("dve", v2)], inc="dve")
            a1 = sc.op("act", lambda a: a.activation(
                out=xout[:], in_=src_pre, func=AF.Square, bias=nmean[:, 0:1], scale=1.0,
                accum_out=var[:, 0:1]),
                waits=[("dve", v3), ("st", sc.c("st")), ("st1", sc.c("st1"))], inc="act")
            v4 = sc.op("dve", lambda v: v.tensor_scalar(
                out=var[:, 0:1], in0=var[:, 0:1], scalar1=1.0 / D, scalar2=1e-5,
                op0=ALU.mult, op1=ALU.add),
                waits=[("act", a1)], inc="dve")
            a2 = sc.op("act", lambda a: a.activation(
                out=rstd[:, 0:1], in_=var[:, 0:1], func=AF.Sqrt),
                waits=[("dve", v4)], inc="act")
            v5 = sc.op("dve", lambda v: v.reciprocal(out=rstd[:, 0:1], in_=rstd[:, 0:1]),
                       waits=[("act", a2)], inc="dve")
            v6 = sc.op("dve", lambda v: v.tensor_scalar(
                out=xout[:], in0=src_pre, scalar1=nmean[:, 0:1], scalar2=rstd[:, 0:1],
                op0=ALU.add, op1=ALU.mult),
                waits=[("dve", v5)], inc="dve")
            stsem = "st1" if fm_dram is not None else "st"
            if fm_dram is not None:
                v7 = sc.op("dve", lambda v: v.tensor_copy(out=xout_bf[:], in_=xout[:]),
                           inc="dve")
                for sj in range(8):
                    tv = sc.op("pe", lambda t, sj=sj: t.transpose(
                        ptr[:], xout_bf[:, sj * 128:(sj + 1) * 128], ident[:]),
                        waits=[("dve", sc.c("dve")), ("pool", init_done)], inc="pe")
                    sc.op("dve", lambda v, sj=sj: v.tensor_copy(
                        out=fm_tile[:, sj * 128:(sj + 1) * 128], in_=ptr[:]),
                        waits=[("pe", tv)], inc="dve")
                vlast = sc.c("dve")
                sc.op("sync", lambda e, tt=tt: e.dma_start(
                    out=fm_dram[:, tt * 128:(tt + 1) * 128].rearrange("(j p) t -> p j t", p=128),
                    in_=fm_tile[:].rearrange("p (j t) -> p j t", j=8)),
                    waits=[("dve", vlast)], inc=stsem, dma=True)
            else:
                vlast = v6
            sc.op("sync", lambda e, tt=tt: e.dma_start(
                out=out_dram[tt * 128:(tt + 1) * 128, :], in_=xout[:]),
                waits=[("dve", vlast)], inc=stsem, dma=True)

        for tt in range(NT1 if stage >= 3 else 0):
            xv = sc.op("sync", lambda e, tt=tt: e.dma_start(
                out=xres[:], in_=x_sl[tt * 128:(tt + 1) * 128, :]),
                waits=([("dve", sc.c("dve"))] if tt else []), inc="ld2", dma=True)
            ccw = [("cc", dram["cc_wait"])] if fused else []
            yv = sc.op("sync", lambda e, tt=tt: e.dma_start(
                out=yfm_t[:].rearrange("p (j t) -> p j t", j=NCJ),
                in_=yfm[:, tt * 128:(tt + 1) * 128].rearrange("(j p) t -> p j t", p=128)),
                waits=([("pe", sc.c("pe"))] if tt else []) + ccw, inc="ld2", dma=True)
            mms = []
            for oj in range(2):
                for j in range(NCJ):
                    w = []
                    if j == 0:
                        w = [("ld", ld_w), ("ld2", max(xv, yv))]
                        if tt > 0:
                            w.append(("dve", sc.c("dve")))
                    mm = sc.op("pe", lambda t, oj=oj, j=j: t.matmul(
                        pb[oj][:], yfm_t[:, j * 128:(j + 1) * 128],
                        wp_sb[:, j * D + oj * 512: j * D + (oj + 1) * 512],
                        start=(j == 0), stop=(j == NCJ - 1)),
                        waits=w, inc="pe" if j == NCJ - 1 else None)
                mms.append(mm)
            for oj in range(2):
                sc.op("dve", lambda v, oj=oj: v.tensor_add(
                    out=pre[:, oj * 512:(oj + 1) * 512], in0=pb[oj][:],
                    in1=bp_bc[:, oj * 512:(oj + 1) * 512]),
                    waits=[("pe", mms[oj]), ("act", bias_done)], inc="dve")
            sc.op("dve", lambda v: v.tensor_add(out=pre[:], in0=pre[:], in1=xres[:]),
                  waits=[("ld2", max(xv, yv))], inc="dve")
            ln_tile(tt, pre[:], x3_dram, x3fm_dram, x3fm_t)
        d1_st = sc.c("st1")

        for tt in range(NT1 if stage >= 4 else 0):
            xv = sc.op("sync", lambda e, tt=tt: e.dma_start(
                out=xres[:], in_=x3_dram[tt * 128:(tt + 1) * 128, :]),
                waits=[("st1", d1_st), ("dve", sc.c("dve"))], inc="ld2", dma=True)
            yv = sc.op("sync", lambda e, tt=tt: e.dma_start(
                out=yfm_t[:, 0:1024].rearrange("p (j t) -> p j t", j=8),
                in_=x3fm_dram[:, tt * 128:(tt + 1) * 128].rearrange("(j p) t -> p j t", p=128)),
                waits=[("st1", d1_st), ("pe", sc.c("pe"))], inc="ld2", dma=True)
            mms = []
            for oj in range(2):
                for j in range(8):
                    w = []
                    if j == 0:
                        w = [("dve", weff_done), ("ld2", max(xv, yv))]
                        if tt > 0:
                            w.append(("dve", sc.c("dve")))
                    mm = sc.op("pe", lambda t, oj=oj, j=j: t.matmul(
                        pb[oj][:], yfm_t[:, j * 128:(j + 1) * 128],
                        weff[:, j * D + oj * 512: j * D + (oj + 1) * 512],
                        start=(j == 0), stop=(j == 7)),
                        waits=w, inc="pe" if j == 7 else None)
                mms.append(mm)
            for oj in range(2):
                sc.op("dve", lambda v, oj=oj: v.tensor_add(
                    out=pre[:, oj * 512:(oj + 1) * 512], in0=pb[oj][:],
                    in1=bl_bc[:, oj * 512:(oj + 1) * 512]),
                    waits=[("pe", mms[oj]), ("act", bias_done)], inc="dve")
            sc.op("dve", lambda v: v.tensor_add(out=pre[:], in0=pre[:], in1=xres[:]),
                  waits=[("ld2", max(xv, yv))], inc="dve")
            ln_tile(tt, pre[:], x4o, None, None)
        sc.op("sync", lambda e: None, waits=[("st", sc.c("st"))])
        sc.flush(block)


HEAD_OF_CORE = [(c // 4, 4 * (c % 4)) for c in range(8)]


def kernel_unfused(**inp):
    x = np.asarray(inp["L0_attn_x"])
    Wq = np.asarray(inp["L0_attn_Wq"])
    Wk = np.asarray(inp["L0_attn_Wk"])
    Wv = np.asarray(inp["L0_attn_Wv"])
    bq = np.asarray(inp["L0_attn_bq"])
    bk = np.asarray(inp["L0_attn_bk"])
    bv = np.asarray(inp["L0_attn_bv"])

    nc = build_A()
    in_maps = []
    for c in range(8):
        b, h0 = HEAD_OF_CORE[c]
        sl = slice(h0 * HD, (h0 + HPC) * HD)
        wo2 = np.concatenate([woT, zeros], 0) if b == 0 else np.concatenate([zeros, woT], 0)
        wp2 = np.concatenate([wproj, zeros], 0) if b == 0 else np.concatenate([zeros, wproj], 0)
        in_maps.append({
            "xfm": np.ascontiguousarray(x[b].T),
            "wqT": np.ascontiguousarray(Wq[sl, :].T),
            "wkT": np.ascontiguousarray(Wk[sl, :].T),
            "wvT": np.ascontiguousarray(Wv[sl, :].T),
            "bqkv": np.ascontiguousarray(np.stack([bq[sl], bk[sl], bv[sl]])),
        })
    res = run_bass_kernel_spmd(nc, in_maps, core_ids=list(range(8)))
    y_fm = [np.concatenate([np.asarray(res.results[4 * b_ + i]["yfm"]) for i in range(4)], axis=0)
            for b_ in range(B)]

    nc = build_B(debug=False)
    in_maps = []
    for c in range(8):
        b, q = c // 4, c % 4
        ts = slice(q * TPC, (q + 1) * TPC)
        in_maps.append({
            "yfm_sl": np.ascontiguousarray(y_fm[b][:, ts]),
            "x_sl": np.ascontiguousarray(x[b][ts, :]),
            "woT": np.ascontiguousarray(np.asarray(inp["L0_attn_Wo"]).T),
            "bo": np.ascontiguousarray(np.asarray(inp["L0_attn_bo"])[None, :]),
            "wgT": np.ascontiguousarray(np.asarray(inp["L0_ffn_w_gate"]).T),
            "wuT": np.ascontiguousarray(np.asarray(inp["L0_ffn_w_up"]).T),
            "wdT": np.ascontiguousarray(np.asarray(inp["L0_ffn_w_down"]).T),
        })
    res = run_bass_kernel_spmd(nc, in_maps, core_ids=list(range(8)))
    x2 = np.zeros((B, T, D), np.float32)
    for c in range(8):
        b, q = c // 4, c % 4
        ts = slice(q * TPC, (q + 1) * TPC)
        x2[b, ts] = np.asarray(res.results[c]["x2o"])

    W_attn = np.asarray(inp["L1_attn_W_attn"])
    b_attn = np.asarray(inp["L1_attn_b_attn"])
    gain = np.asarray(inp["L1_attn_score_gain"]).reshape(1, 1)
    sign = 1.0 if float(gain[0, 0]) >= 0 else -1.0
    nc = build_C(sign=sign)
    in_maps = []
    for c in range(8):
        b, h0 = HEAD_OF_CORE[c]
        cs = slice(h0 * HD, (h0 + HPC) * HD)
        in_maps.append({
            "x2fm": np.ascontiguousarray(x2[b].T),
            "wqC": np.ascontiguousarray(W_attn[:, cs]),
            "wkC": np.ascontiguousarray(W_attn[:, 1024 + h0 * HD: 1024 + (h0 + HPC) * HD]),
            "wvC": np.ascontiguousarray(W_attn[:, 2048 + h0 * HD: 2048 + (h0 + HPC) * HD]),
            "bqkvC": np.ascontiguousarray(np.stack([
                b_attn[cs], b_attn[1024 + h0 * HD: 1024 + (h0 + HPC) * HD],
                b_attn[2048 + h0 * HD: 2048 + (h0 + HPC) * HD]])),
            "gain": np.ascontiguousarray(gain.astype(np.float32)),
        })
    res = run_bass_kernel_spmd(nc, in_maps, core_ids=list(range(8)))
    y1_fm = [np.concatenate([np.asarray(res.results[4 * b_ + i]["yfm"]) for i in range(4)], axis=0)
             for b_ in range(B)]

    nc = build_D()
    in_maps = []
    for c in range(8):
        b, q = c // 4, c % 4
        ts = slice(q * TPC, (q + 1) * TPC)
        in_maps.append({
            "y1fm_sl": np.ascontiguousarray(y1_fm[b][:, ts]),
            "x2_sl": np.ascontiguousarray(x2[b][ts, :]),
            "wproj": np.ascontiguousarray(np.asarray(inp["L1_attn_W_proj"])),
            "bproj": np.ascontiguousarray(np.asarray(inp["L1_attn_b_proj"])[None, :]),
            "wLT": np.ascontiguousarray(np.asarray(inp["L1_ffn_W"]).T),
            "bL": np.ascontiguousarray(np.asarray(inp["L1_ffn_b"])[None, :]),
            "ldT": np.ascontiguousarray(np.asarray(inp["L1_ffn_lora_down"]).T),
            "da": np.ascontiguousarray(np.asarray(inp["L1_ffn_down_aux"])),
            "luT": np.ascontiguousarray(np.asarray(inp["L1_ffn_lora_up"]).T),
            "uaT": np.ascontiguousarray(np.asarray(inp["L1_ffn_up_aux"]).T),
        })
    res = run_bass_kernel_spmd(nc, in_maps, core_ids=list(range(8)))
    out = np.zeros((B, T, D), np.float32)
    for c in range(8):
        b, q = c // 4, c % 4
        ts = slice(q * TPC, (q + 1) * TPC)
        out[b, ts] = np.asarray(res.results[c]["x4o"])
    return out


def build_full(sign=1.0):
    nc = bass.Bass("TRN2", target_bir_lowering=False)
    dram = {}
    dram["xfm"] = nc.dram_tensor("xfm", [D, T], F32, kind="ExternalInput").ap()
    for w in ("q", "k", "v"):
        dram[f"w{w}T"] = nc.dram_tensor(f"w{w}T", [D, HPC * HD], F32, kind="ExternalInput").ap()
    dram["bqkv"] = nc.dram_tensor("bqkv", [3, HPC * HD], F32, kind="ExternalInput").ap()
    dram["x_sl"] = nc.dram_tensor("x_sl", [TPC, D], F32, kind="ExternalInput").ap()
    dram["wo2T"] = nc.dram_tensor("wo2T", [2 * D, D], F32, kind="ExternalInput").ap()
    dram["bo"] = nc.dram_tensor("bo", [1, D], F32, kind="ExternalInput").ap()
    dram["wgT"] = nc.dram_tensor("wgT", [D, 3072], I32, kind="ExternalInput").ap()
    dram["wuT"] = nc.dram_tensor("wuT", [D, 3072], I32, kind="ExternalInput").ap()
    dram["wdT"] = nc.dram_tensor("wdT", [3072, D], I32, kind="ExternalInput").ap()
    for w in ("q", "k", "v"):
        dram[f"w{w}C"] = nc.dram_tensor(f"w{w}C", [D, HPC * HD], F32, kind="ExternalInput").ap()
    dram["bqkvC"] = nc.dram_tensor("bqkvC", [3, HPC * HD], F32, kind="ExternalInput").ap()
    dram["gain"] = nc.dram_tensor("gain", [1, 1], F32, kind="ExternalInput").ap()
    dram["wp2"] = nc.dram_tensor("wp2", [2 * D, D], F32, kind="ExternalInput").ap()
    dram["bproj"] = nc.dram_tensor("bproj", [1, D], F32, kind="ExternalInput").ap()
    dram["wLT"] = nc.dram_tensor("wLT", [D, D], F32, kind="ExternalInput").ap()
    dram["bL"] = nc.dram_tensor("bL", [1, D], F32, kind="ExternalInput").ap()
    dram["ldT"] = nc.dram_tensor("ldT", [100, 4], F32, kind="ExternalInput").ap()
    dram["da"] = nc.dram_tensor("da", [100, D], F32, kind="ExternalInput").ap()
    dram["luT"] = nc.dram_tensor("luT", [4, 50], F32, kind="ExternalInput").ap()
    dram["uaT"] = nc.dram_tensor("uaT", [50, D], F32, kind="ExternalInput").ap()
    dram["x4o"] = nc.dram_tensor("x4o", [TPC, D], F32, kind="ExternalOutput").ap()
    for n, sh, dt in (("x1_scr", [TPC, D], F32), ("x1fm_scr", [D, TPC], BF16),
                      ("x2_scr", [TPC, D], F32), ("x3_scr", [TPC, D], F32),
                      ("x3fm_scr", [D, TPC], BF16),
                      ("a2a1_in", [2048, TPC], BF16), ("a2a1_out", [2048, TPC], BF16),
                      ("ag2_in", [D, TPC], BF16), ("ag2_out", [4 * D, TPC], BF16),
                      ("a2a3_in", [2048, TPC], BF16), ("a2a3_out", [2048, TPC], BF16)):
        dram[n] = nc.dram_tensor(n, sh, dt).ap()
    es0 = ExitStack()
    with es0:
        sc = Sched(nc, es0)
        block = es0.enter_context(nc.Block())
        phase_A(nc, sc, block, dram, fused=True)
        sc.barrier()
        dram["cc_wait"] = 1
        phase_B(nc, sc, block, dram, fused=True)
        sc.barrier()
        dram["cc_wait"] = 2
        phase_C(nc, sc, block, dram, sign, fused=True)
        sc.barrier()
        dram["cc_wait"] = 3
        phase_D(nc, sc, block, dram, fused=True)
    return nc


def kernel(**inp):
    inp = {k: np.asarray(v) for k, v in inp.items()}
    x = inp["L0_attn_x"]
    gain = inp["L1_attn_score_gain"].reshape(1, 1).astype(np.float32)
    sign = 1.0 if float(gain[0, 0]) >= 0 else -1.0
    W_attn = inp["L1_attn_W_attn"]
    b_attn = inp["L1_attn_b_attn"]
    woT = np.ascontiguousarray(inp["L0_attn_Wo"].T)
    wproj = np.ascontiguousarray(inp["L1_attn_W_proj"])
    zeros = np.zeros((D, D), np.float32)
    nc = build_full(sign=sign)
    in_maps = []
    for c in range(8):
        b, h0 = HEAD_OF_CORE[c]
        q = c % 4
        sl = slice(h0 * HD, (h0 + HPC) * HD)
        ts = slice(q * TPC, (q + 1) * TPC)
        wo2 = np.concatenate([woT, zeros], 0) if b == 0 else np.concatenate([zeros, woT], 0)
        wp2 = np.concatenate([wproj, zeros], 0) if b == 0 else np.concatenate([zeros, wproj], 0)
        in_maps.append({
            "xfm": np.ascontiguousarray(x[b].T),
            "wqT": np.ascontiguousarray(inp["L0_attn_Wq"][sl, :].T),
            "wkT": np.ascontiguousarray(inp["L0_attn_Wk"][sl, :].T),
            "wvT": np.ascontiguousarray(inp["L0_attn_Wv"][sl, :].T),
            "bqkv": np.ascontiguousarray(np.stack([
                inp["L0_attn_bq"][sl], inp["L0_attn_bk"][sl], inp["L0_attn_bv"][sl]])),
            "x_sl": np.ascontiguousarray(x[b][ts, :]),
            "wo2T": np.ascontiguousarray(wo2),
            "bo": np.ascontiguousarray(inp["L0_attn_bo"][None, :]),
            "wgT": np.ascontiguousarray(inp["L0_ffn_w_gate"].T),
            "wuT": np.ascontiguousarray(inp["L0_ffn_w_up"].T),
            "wdT": np.ascontiguousarray(inp["L0_ffn_w_down"].T),
            "wqC": np.ascontiguousarray(W_attn[:, sl]),
            "wkC": np.ascontiguousarray(W_attn[:, 1024 + h0 * HD: 1024 + (h0 + HPC) * HD]),
            "wvC": np.ascontiguousarray(W_attn[:, 2048 + h0 * HD: 2048 + (h0 + HPC) * HD]),
            "bqkvC": np.ascontiguousarray(np.stack([
                b_attn[sl], b_attn[1024 + h0 * HD: 1024 + (h0 + HPC) * HD],
                b_attn[2048 + h0 * HD: 2048 + (h0 + HPC) * HD]])),
            "gain": np.ascontiguousarray(gain),
            "wp2": np.ascontiguousarray(wp2),
            "bproj": np.ascontiguousarray(inp["L1_attn_b_proj"][None, :]),
            "wLT": np.ascontiguousarray(inp["L1_ffn_W"].T),
            "bL": np.ascontiguousarray(inp["L1_ffn_b"][None, :]),
            "ldT": np.ascontiguousarray(inp["L1_ffn_lora_down"].T),
            "da": np.ascontiguousarray(inp["L1_ffn_down_aux"]),
            "luT": np.ascontiguousarray(inp["L1_ffn_lora_up"].T),
            "uaT": np.ascontiguousarray(inp["L1_ffn_up_aux"].T),
        })
    res = run_bass_kernel_spmd(nc, in_maps, core_ids=list(range(8)))
    out = np.zeros((B, T, D), np.float32)
    for c in range(8):
        b, _ = HEAD_OF_CORE[c]
        q = c % 4
        out[b, q * TPC:(q + 1) * TPC] = np.asarray(res.results[c]["x4o"])
    return out


# version 12 (final)
# speedup vs baseline: 1.3293x; 1.1541x over previous
import numpy as np
import ml_dtypes
from contextlib import ExitStack

import concourse.bass as bass
import concourse.mybir as mybir
from concourse.bass_utils import run_bass_kernel_spmd
from concourse.aot_env import allow_ici_all_to_all

F32 = mybir.dt.float32
BF16 = mybir.dt.bfloat16
I32 = mybir.dt.int32
AF = mybir.ActivationFunctionType
ALU = mybir.AluOpType
AXL = mybir.AxisListType

B = 2
C_PAIRS = 2
T = 8192
D = 1024
NH = 16
HD = 64
HPC = 4
TPC = 2048
CHUNK = 128
NCHUNK = T // CHUNK


class Sem:
    def __init__(self, h):
        self.h = h
        self.c = 0


class Sched:
    ENGS = ("pe", "act", "dve", "pool", "sync")

    def __init__(self, nc, es):
        self.nc = nc
        self.q = {e: [] for e in ("pe", "act", "dve", "pool", "sync")}
        self.sem = {n: Sem(es.enter_context(nc.semaphore(f"s_{n}")))
                    for n in ("ld", "ld2", "lx0", "lx1", "lst", "st1", "pe", "act",
                              "dve", "pool", "sync", "st", "cc")}
        self.pending = {}

    def op(self, eng, fn, waits=(), inc=None, dma=False):
        s = self.sem[inc] if inc else None
        n = 16 if dma else 1
        val = None
        if s is not None:
            s.c += n
            val = s.c
        if self.pending.get(eng):
            waits = tuple(self.pending.pop(eng)) + tuple(waits)
        waits = tuple((self.sem[w], v) for (w, v) in waits)

        def g(e, fn=fn, waits=waits, s=s, n=n):
            for (ss, v) in waits:
                e.wait_ge(ss.h, v)
            ins = fn(e)
            if s is not None:
                ins.then_inc(s.h, n)
        self.q[eng].append(g)
        return val

    def c(self, name):
        return self.sem[name].c

    def barrier(self):
        snap = [(n, s.c) for n, s in self.sem.items() if s.c > 0]
        for e in self.ENGS:
            self.pending[e] = list(snap)

    def flush(self, block):
        self.emit(block)
        self.q = {e: [] for e in self.ENGS}

    def emit(self, block):
        @block.tensor
        def _(eng):
            for f in self.q["pe"]:
                f(eng)

        @block.scalar
        def _(eng):
            for f in self.q["act"]:
                f(eng)

        @block.vector
        def _(eng):
            for f in self.q["dve"]:
                f(eng)

        @block.gpsimd
        def _(eng):
            for f in self.q["pool"]:
                f(eng)

        @block.sync
        def _(eng):
            for f in self.q["sync"]:
                f(eng)


def make_ident(sc, ident):
    v = sc.op("pool", lambda g: g.memset(ident[:], 1.0), inc="pool")
    return sc.op("pool", lambda g: g.affine_select(
        out=ident[:], in_=ident[:], pattern=[[-1, 128]],
        compare_op=ALU.is_equal, fill=0.0, base=0, channel_multiplier=1),
        waits=[("pool", v)], inc="pool")


def make_triu(sc, triu):
    v = sc.op("pool", lambda g: g.memset(triu[:], 1.0), inc="pool")
    return sc.op("pool", lambda g: g.affine_select(
        out=triu[:], in_=triu[:], pattern=[[1, 128]],
        compare_op=ALU.is_ge, fill=0.0, base=0, channel_multiplier=-1),
        waits=[("pool", v)], inc="pool")


def build_A(debug=False):
    nc = bass.Bass("TRN2", target_bir_lowering=False)
    dram = {}
    dram["xfm"] = nc.dram_tensor("xfm", [D, T], F32, kind="ExternalInput").ap()
    for w in ("q", "k", "v"):
        dram[f"w{w}T"] = nc.dram_tensor(f"w{w}T", [D, HPC * HD], F32, kind="ExternalInput").ap()
    dram["bqkv"] = nc.dram_tensor("bqkv", [3, HPC * HD], F32, kind="ExternalInput").ap()
    dram["yfm"] = nc.dram_tensor("yfm", [HPC * HD, T], BF16, kind="ExternalOutput").ap()
    es0 = ExitStack()
    with es0:
        sc = Sched(nc, es0)
        block = es0.enter_context(nc.Block())
        phase_A(nc, sc, block, dram, fused=False)
    return nc


def phase_A(nc, sc, block, dram, fused):
    xfm = dram["xfm"]
    wT = {w: dram[f"w{w}T"] for w in ("q", "k", "v")}
    bqkv = dram["bqkv"]
    TT = 512
    NTT = T // TT
    es = ExitStack()
    with es:
        def sb(name, shape, dt=F32):
            return es.enter_context(nc.sbuf_tensor("A_" + name, shape, dt))

        def psum(name, shape, dt=F32):
            return es.enter_context(nc.psum_tensor("A_" + name, shape, dt))

        w_sb = {w: sb(f"w{w}", [128, 8 * 256], BF16) for w in ("q", "k", "v")}
        bias_sb = sb("bias", [128, 6], F32)
        x_t = [sb(f"x_t{i}", [128, 8 * TT], BF16) for i in range(2)]
        q_fm = sb("q_fm", [128, 2 * T], BF16)
        k_fm = sb("k_fm", [128, 2 * T], BF16)
        kt = sb("kt", [128, HPC * NCHUNK * HD], BF16)
        vt = sb("vt", [128, HPC * NCHUNK * 65], BF16)
        v_stage = sb("v_stage", [128, TT], BF16)
        phi_m = sb("phi_m", [128, TT], F32)
        phi_e = sb("phi_e", [128, TT], F32)
        phi_r = sb("phi_r", [128, TT], F32)
        masked = [sb(f"masked{i}", [128, 128], BF16) for i in range(2)]
        triu = sb("triu", [128, 128], BF16)
        ident = sb("ident", [128, 128], BF16)
        S_arena = sb("S_arena", [128, 2 * 65], BF16)
        ones1 = sb("ones1", [1, 64], F32)
        rden3 = sb("rden3", [1, 384], F32)
        bcast_sb = sb("bcast_sb", [64, 128], F32)
        y_fm = sb("y_fm", [128, 2 * T], BF16)

        pp = [psum(f"pp{i}", [128, TT]) for i in range(6)]
        ptr = psum("ptr", [128, 128], BF16)
        psc_bank = psum("psc", [128, 512])
        psc = [psc_bank[:, 0:128], psc_bank[:, 128:256]]
        pnum3 = [pp[0][0:65, 0:128], pp[0][0:65, 128:256], pp[0][0:65, 256:384]]
        pS_h = [pp[1], pp[3], pp[4], pp[5]]
        pbc = pp[2][0:64, 0:128]

        make_ident(sc, ident)
        make_triu(sc, triu)
        sc.op("pool", lambda g: g.memset(ones1[:], 1.0), inc="pool")
        sc.op("pool", lambda g: g.memset(vt[:], 1.0), inc="pool")
        sc.op("pool", lambda g: g.memset(S_arena[:], 0.0), inc="pool")
        init_done = sc.c("pool")

        for w in ("q", "k", "v"):
            for j in range(8):
                sc.op("pool", lambda g, w=w, j=j: g.dma_start(
                    out=w_sb[w][:, j * 256:(j + 1) * 256],
                    in_=wT[w][j * 128:(j + 1) * 128, :]), inc="ld", dma=True)
        for wi in range(3):
            for p in range(2):
                sc.op("pool", lambda g, wi=wi, p=p: g.dma_start(
                    out=bias_sb[:, wi * 2 + p: wi * 2 + p + 1],
                    in_=bqkv[wi:wi + 1, p * 128:(p + 1) * 128].rearrange("a e -> e a")),
                    inc="ld", dma=True)
        ld_w = sc.c("ld")

        pe_tile_done = {}
        dve_tile_done = {}
        for ti in range(NTT):
            buf = ti % 2
            waits = [("pe", pe_tile_done[ti - 2])] if ti >= 2 else []
            xv = sc.op("pool", lambda g, buf=buf, ti=ti: g.dma_start(
                out=x_t[buf][:].rearrange("p (j t) -> p j t", j=8),
                in_=xfm[:, ti * TT:(ti + 1) * TT].rearrange("(j p) t -> p j t", p=128)),
                waits=waits, inc=f"lx{buf}", dma=True)

            for wi, w in enumerate(("q", "k", "v")):
                for p in range(2):
                    bank = pp[wi * 2 + p]
                    bcol = bias_sb[:, wi * 2 + p: wi * 2 + p + 1]
                    for j in range(8):
                        wts = []
                        if j == 0:
                            wts = [("ld", ld_w), (f"lx{buf}", xv)]
                            if ti > 0:
                                wts.append(("dve", dve_tile_done[ti - 1]))
                        mm = sc.op("pe", lambda t, bank=bank, w=w, j=j, p=p, buf=buf: t.matmul(
                            bank[:],
                            w_sb[w][:, j * 256 + p * 128: j * 256 + (p + 1) * 128],
                            x_t[buf][:, j * TT:(j + 1) * TT],
                            start=(j == 0), stop=(j == 7)),
                            waits=wts, inc="pe" if j == 7 else None)
                    if w in ("q", "k"):
                        dst = q_fm if w == "q" else k_fm
                        dslc = dst[:, p * T + ti * TT: p * T + (ti + 1) * TT]
                        v1 = sc.op("dve", lambda v, bank=bank, bcol=bcol: v.tensor_scalar(
                            out=phi_m[:], in0=bank[:], scalar1=bcol, scalar2=0.0,
                            op0=ALU.add, op1=ALU.min),
                            waits=[("pe", mm), ("ld", ld_w)], inc="dve")
                        a1 = sc.op("act", lambda a: a.activation(
                            out=phi_e[:], in_=phi_m[:], func=AF.Exp),
                            waits=[("dve", v1)], inc="act")
                        sc.op("dve", lambda v, bank=bank, bcol=bcol: v.tensor_scalar(
                            out=phi_r[:], in0=bank[:], scalar1=bcol, scalar2=0.0,
                            op0=ALU.add, op1=ALU.max), inc="dve")
                        v3 = sc.op("dve", lambda v, dslc=dslc: v.tensor_add(
                            out=dslc, in0=phi_e[:], in1=phi_r[:]),
                            waits=[("act", a1)], inc="dve")
                        if w == "k":
                            for sti in range(4):
                                c = ti * 4 + sti
                                src = dst[:, p * T + c * 128:p * T + (c + 1) * 128]
                                tv = sc.op("pe", lambda t, src=src: t.transpose(
                                    ptr[:], src, ident[:]),
                                    waits=[("dve", sc.c("dve")), ("pool", init_done)],
                                    inc="pe")
                                for hh in range(2):
                                    h = 2 * p + hh
                                    sc.op("dve", lambda v, hh=hh, h=h, c=c: v.tensor_copy(
                                        out=kt[:, (h * NCHUNK + c) * 64:(h * NCHUNK + c) * 64 + 64],
                                        in_=ptr[:, hh * 64:(hh + 1) * 64]),
                                        waits=[("pe", tv)], inc="dve")
                    else:
                        vv = sc.op("dve", lambda v, bank=bank, bcol=bcol: v.tensor_scalar_add(
                            out=v_stage[:], in0=bank[:], scalar1=bcol),
                            waits=[("pe", mm), ("ld", ld_w)], inc="dve")
                        for sti in range(4):
                            c = ti * 4 + sti
                            tv = sc.op("pe", lambda t, sti=sti: t.transpose(
                                ptr[:], v_stage[:, sti * 128:(sti + 1) * 128], ident[:]),
                                waits=[("dve", sc.c("dve")), ("pool", init_done)],
                                inc="pe")
                            for hh in range(2):
                                h = 2 * p + hh
                                sc.op("dve", lambda v, hh=hh, h=h, c=c: v.tensor_copy(
                                    out=vt[:, (h * NCHUNK + c) * 65:(h * NCHUNK + c) * 65 + 64],
                                    in_=ptr[:, hh * 64:(hh + 1) * 64]),
                                    waits=[("pe", tv)], inc="dve")
            pe_tile_done[ti] = sc.c("pe")
            dve_tile_done[ti] = sc.c("dve")
        phase1_dve = sc.c("dve")
        phase1_pe = sc.c("pe")

        NJ = NCHUNK * HPC
        mask_done = [0] * NJ
        num1_done = [0] * NJ
        supd_done = [0] * NJ
        ymult_done = [0] * NJ
        recip_done = [0] * NJ
        sevac_done = [0] * NJ
        bc_done = [0] * NJ
        bcevac_done = [0] * NJ

        def jparts(j):
            i, h = j // HPC, j % HPC
            return i, h, h % 2, h // 2

        for j in range(NJ + 2):
            if j >= 2:
                jj = j - 2
                bcw = [("dve", recip_done[jj])]
                if jj > 0:
                    bcw.append(("act", bcevac_done[jj - 1]))
                bc_done[jj] = sc.op("pe", lambda t, jj=jj: t.matmul(
                    pbc, ones1[:], rden3[:, (jj % 3) * 128:(jj % 3) * 128 + 128],
                    start=True, stop=True),
                    waits=bcw, inc="pe")
            if j < NJ:
                i, h, par, pair = jparts(j)
                rowoff = par * 64
                qk_off = pair * T + i * 128
                wts = [("dve", phase1_dve)]
                if j >= 2:
                    wts.append(("dve", mask_done[j - 2]))
                sc.op("pe", lambda t, rowoff=rowoff, qk_off=qk_off, buf=j % 2: t.matmul(
                    psc[buf][:],
                    k_fm[rowoff:rowoff + 64, qk_off:qk_off + 128],
                    q_fm[rowoff:rowoff + 64, qk_off:qk_off + 128],
                    start=True, stop=True, tile_position=(rowoff, 0)),
                    waits=wts, inc="pe")
                sc_done = sc.c("pe")
            if 1 <= j <= NJ:
                jj = j - 1
                i, h, par, pair = jparts(jj)
                rowoff = par * 64
                qk_off = pair * T + i * 128
                Sap = S_arena[rowoff:rowoff + 64, pair * 65: pair * 65 + 65]
                pSap = pS_h[h][rowoff:rowoff + 64, 0:65]
                vslc = vt[:, (h * NCHUNK + i) * 65:(h * NCHUNK + i) * 65 + 65]
                kslc = kt[:, (h * NCHUNK + i) * 64:(h * NCHUNK + i) * 64 + 64]
                nb = pnum3[jj % 3]
                nwts = [("dve", mask_done[jj])]
                if jj >= 3:
                    nwts.append(("dve", ymult_done[jj - 3]))
                num1_done[jj] = sc.op("pe", lambda t, nb=nb, vslc=vslc, buf=jj % 2: t.matmul(
                    nb, vslc, masked[buf][:], start=True, stop=False),
                    waits=nwts, inc="pe")
                n2w = []
                if jj >= HPC:
                    n2w = [("act", sevac_done[jj - HPC])]
                sc.op("pe", lambda t, nb=nb, Sap=Sap, rowoff=rowoff, qk_off=qk_off: t.matmul(
                    nb, Sap, q_fm[rowoff:rowoff + 64, qk_off:qk_off + 128],
                    start=False, stop=True, tile_position=(rowoff, 0)),
                    waits=n2w, inc="pe")
                supd_done[jj] = sc.op("pe", lambda t, pSap=pSap, kslc=kslc, vslc=vslc, i=i, rowoff=rowoff: t.matmul(
                    pSap, kslc, vslc, start=(i == 0), stop=(i == NCHUNK - 1),
                    tile_position=(0, rowoff)),
                    inc="pe")
                sevac_done[jj] = sc.op("act", lambda a, Sap=Sap, pSap=pSap: a.copy(
                    out=Sap, in_=pSap),
                    waits=[("pe", supd_done[jj])], inc="act")
                recip_done[jj] = sc.op("dve", lambda v, nb=nb, jj=jj: v.reciprocal(
                    out=rden3[:, (jj % 3) * 128:(jj % 3) * 128 + 128],
                    in_=nb[64:65, 0:128]),
                    waits=[("pe", supd_done[jj])], inc="dve")
            if j < NJ:
                mwts = [("pe", sc_done)]
                if j >= 2:
                    mwts.append(("pe", num1_done[j - 2]))
                mask_done[j] = sc.op("dve", lambda v, buf=j % 2: v.tensor_mul(
                    out=masked[buf][:], in0=psc[buf][:], in1=triu[:]),
                    waits=mwts, inc="dve")
            if j >= 2:
                jj = j - 2
                i, h, par, pair = jparts(jj)
                bw = [("pe", bc_done[jj])]
                if jj >= 1:
                    bw.append(("dve", ymult_done[jj - 1]))
                bcevac_done[jj] = sc.op("act", lambda a: a.copy(
                    out=bcast_sb[:], in_=pbc),
                    waits=bw, inc="act")
                ydst = y_fm[(h % 2) * 64:(h % 2) * 64 + 64,
                            (h // 2) * T + i * 128:(h // 2) * T + (i + 1) * 128]
                ymult_done[jj] = sc.op("dve", lambda v, nb=pnum3[jj % 3], ydst=ydst: v.tensor_mul(
                    out=ydst, in0=nb[0:64, 0:128], in1=bcast_sb[:]),
                    waits=[("act", bcevac_done[jj])], inc="dve")
        yl = sc.c("dve")

        if not fused:
            for p in range(2):
                sc.op("sync", lambda e, p=p: e.dma_start(
                    out=dram["yfm"][p * 128:(p + 1) * 128, :], in_=y_fm[:, p * T:(p + 1) * T]),
                    waits=[("dve", yl)], inc="st", dma=True)
            sc.op("sync", lambda e: None, waits=[("st", sc.c("st"))])
        else:
            a2a_in, a2a_out = dram["a2a1_in"], dram["a2a1_out"]
            for jj in range(8):
                for p in range(2):
                    sc.op("sync", lambda e, jj=jj, p=p: e.dma_start(
                        out=a2a_in[jj * 256 + p * 128: jj * 256 + (p + 1) * 128, :],
                        in_=y_fm[:, p * T + (jj % 4) * TPC: p * T + (jj % 4) * TPC + TPC]),
                        waits=[("dve", yl)], inc="st1", dma=True)
            def _a2a(g, a2a_in=a2a_in, a2a_out=a2a_out):
                with allow_ici_all_to_all():
                    return g.collective_compute(
                        "AllToAll", ALU.bypass, replica_groups=[list(range(8))],
                        ins=[a2a_in.opt()], outs=[a2a_out.opt()])
            sc.op("pool", _a2a, waits=[("st1", sc.c("st1"))], inc="cc")
        sc.flush(block)




def build_B(debug=False):
    nc = bass.Bass("TRN2", target_bir_lowering=False)
    dram = {}
    dram["yfm_sl"] = nc.dram_tensor("yfm_sl", [D, TPC], BF16, kind="ExternalInput").ap()
    dram["x_sl"] = nc.dram_tensor("x_sl", [TPC, D], F32, kind="ExternalInput").ap()
    dram["woT"] = nc.dram_tensor("woT", [D, D], F32, kind="ExternalInput").ap()
    dram["bo"] = nc.dram_tensor("bo", [1, D], F32, kind="ExternalInput").ap()
    dram["wgT"] = nc.dram_tensor("wgT", [D, 3072], I32, kind="ExternalInput").ap()
    dram["wuT"] = nc.dram_tensor("wuT", [D, 3072], I32, kind="ExternalInput").ap()
    dram["wdT"] = nc.dram_tensor("wdT", [3072, D], I32, kind="ExternalInput").ap()
    dram["x2o"] = nc.dram_tensor("x2o", [TPC, D], F32, kind="ExternalOutput").ap()
    dram["x1_scr"] = nc.dram_tensor("x1_scr", [TPC, D], F32).ap()
    dram["x1fm_scr"] = nc.dram_tensor("x1fm_scr", [D, TPC], BF16).ap()
    es0 = ExitStack()
    with es0:
        sc = Sched(nc, es0)
        block = es0.enter_context(nc.Block())
        phase_B(nc, sc, block, dram, fused=False)
    return nc


def phase_B(nc, sc, block, dram, fused):
    x_sl = dram["x_sl"]
    woT_d = dram.get("woT")
    bo_d = dram["bo"]
    wgT_d = dram["wgT"]
    wuT_d = dram["wuT"]
    wdT_d = dram["wdT"]
    x1_dram = dram["x1_scr"]
    x1fm_dram = dram["x1fm_scr"]
    if fused:
        yfm = dram["a2a1_out"]
        wo2_d = dram["wo2T"]
        NCJ = 16
        x2_dram = dram["x2_scr"]
        x2fm_dram = dram["ag2_in"]
    else:
        yfm = dram["yfm_sl"]
        wo2_d = dram["woT"]
        NCJ = 8
        x2_dram = None
        x2fm_dram = None
    es = ExitStack()
    with es:
        def sb(name, shape, dt=F32):
            return es.enter_context(nc.sbuf_tensor("B_" + name, shape, dt))

        def psum(name, shape, dt=F32):
            return es.enter_context(nc.psum_tensor("B_" + name, shape, dt))

        bo_bc = sb("bo_bc", [128, D], F32)
        bo_sb = sb("bo_sb", [1, D], F32)
        xres = sb("xres", [128, D], F32)
        pre = sb("pre", [128, D], F32)
        mean = sb("mean", [128, 1], F32)
        nmean = sb("nmean", [128, 1], F32)
        var = sb("var", [128, 1], F32)
        rstd = sb("rstd", [128, 1], F32)
        xout = sb("xout", [128, D], F32)
        xout_bf = sb("xout_bf", [128, D], BF16)
        ident = sb("ident", [128, 128], BF16)
        ones1 = sb("ones1", [1, 128], F32)
        x1fm_t = sb("x1fm_t", [128, 8 * 128], BF16)

        pb = [psum(f"pb{i}", [128, 512]) for i in range(6)]
        ptr = psum("ptr", [128, 128], BF16)

        make_ident(sc, ident)
        sc.op("pool", lambda g: g.memset(ones1[:], 1.0), inc="pool")
        init_done = sc.c("pool")

        es1 = ExitStack()
        with es1:
            yfm_t = es1.enter_context(nc.sbuf_tensor("B_yfm_t", [128, NCJ * 128], BF16))
            woT_sb = es1.enter_context(nc.sbuf_tensor("B_woT_sb", [128, NCJ * D], BF16))
            for j in range(NCJ):
                sc.op("pool", lambda g, j=j: g.dma_start(
                    out=woT_sb[:, j * D:(j + 1) * D],
                    in_=wo2_d[j * 128:(j + 1) * 128, :]), inc="ld", dma=True)
            ld_w = sc.c("ld")

            bv = sc.op("pool", lambda g: g.dma_start(out=bo_sb[:], in_=bo_d), inc="ld", dma=True)
            for oj in range(2):
                mm = sc.op("pe", lambda t, oj=oj: t.matmul(
                    pb[4][:, 0:512], ones1[:], bo_sb[:, oj * 512:(oj + 1) * 512],
                    start=True, stop=True),
                    waits=[("ld", bv), ("pool", init_done)] + ([("act", sc.c("act"))] if oj else []),
                    inc="pe")
                sc.op("act", lambda a, oj=oj: a.copy(
                    out=bo_bc[:, oj * 512:(oj + 1) * 512], in_=pb[4][:, 0:512]),
                    waits=[("pe", mm)], inc="act")
            bo_done = sc.c("act")

            NT1 = TPC // 128
            for tt in range(NT1):
                xv = sc.op("sync", lambda e, tt=tt: e.dma_start(
                    out=xres[:], in_=x_sl[tt * 128:(tt + 1) * 128, :]),
                    waits=([("dve", sc.c("dve"))] if tt else []), inc="ld2", dma=True)
                ccw = [("cc", dram["cc_wait"])] if fused else []
                yv = sc.op("sync", lambda e, tt=tt: e.dma_start(
                    out=yfm_t[:].rearrange("p (j t) -> p j t", j=NCJ),
                    in_=yfm[:, tt * 128:(tt + 1) * 128].rearrange("(j p) t -> p j t", p=128)),
                    waits=([("pe", sc.c("pe"))] if tt else []) + ccw, inc="ld2", dma=True)
                mms = []
                for oj in range(2):
                    for j in range(NCJ):
                        w = []
                        if j == 0:
                            w = [("ld", ld_w), ("ld2", max(xv, yv))]
                            if tt > 0:
                                w.append(("dve", sc.c("dve")))
                        mm = sc.op("pe", lambda t, oj=oj, j=j: t.matmul(
                            pb[oj][:],
                            yfm_t[:, j * 128:(j + 1) * 128],
                            woT_sb[:, j * D + oj * 512: j * D + (oj + 1) * 512],
                            start=(j == 0), stop=(j == NCJ - 1)),
                            waits=w, inc="pe" if j == NCJ - 1 else None)
                    mms.append(mm)
                for oj in range(2):
                    sc.op("dve", lambda v, oj=oj: v.tensor_add(
                        out=pre[:, oj * 512:(oj + 1) * 512], in0=pb[oj][:],
                        in1=bo_bc[:, oj * 512:(oj + 1) * 512]),
                        waits=[("pe", mms[oj]), ("act", bo_done)], inc="dve")
                v1 = sc.op("dve", lambda v: v.tensor_add(out=pre[:], in0=pre[:], in1=xres[:]),
                           waits=[("ld2", max(xv, yv))], inc="dve")
                v2 = sc.op("dve", lambda v: v.reduce_sum(out=mean[:, 0:1], in_=pre[:], axis=AXL.X),
                           inc="dve")
                v3 = sc.op("dve", lambda v: v.tensor_scalar_mul(
                    out=nmean[:, 0:1], in0=mean[:, 0:1], scalar1=-1.0 / D),
                    waits=[("dve", v2)], inc="dve")
                a1 = sc.op("act", lambda a: a.activation(
                    out=xout[:], in_=pre[:], func=AF.Square, bias=nmean[:, 0:1], scale=1.0,
                    accum_out=var[:, 0:1]),
                    waits=[("dve", v3), ("st1", sc.c("st1")), ("st", sc.c("st"))], inc="act")
                v4 = sc.op("dve", lambda v: v.tensor_scalar(
                    out=var[:, 0:1], in0=var[:, 0:1], scalar1=1.0 / D, scalar2=1e-5,
                    op0=ALU.mult, op1=ALU.add),
                    waits=[("act", a1)], inc="dve")
                a2 = sc.op("act", lambda a: a.activation(
                    out=rstd[:, 0:1], in_=var[:, 0:1], func=AF.Sqrt),
                    waits=[("dve", v4)], inc="act")
                v5 = sc.op("dve", lambda v: v.reciprocal(out=rstd[:, 0:1], in_=rstd[:, 0:1]),
                           waits=[("act", a2)], inc="dve")
                v6 = sc.op("dve", lambda v: v.tensor_scalar(
                    out=xout[:], in0=pre[:], scalar1=nmean[:, 0:1], scalar2=rstd[:, 0:1],
                    op0=ALU.add, op1=ALU.mult),
                    waits=[("dve", v5)], inc="dve")
                v7 = sc.op("dve", lambda v: v.tensor_copy(out=xout_bf[:], in_=xout[:]),
                           inc="dve")
                for sj in range(8):
                    tv = sc.op("pe", lambda t, sj=sj: t.transpose(
                        ptr[:], xout_bf[:, sj * 128:(sj + 1) * 128], ident[:]),
                        waits=[("dve", v7), ("pool", init_done)] +
                              ([("dve", sc.c("dve"))] if sj else []), inc="pe")
                    sc.op("dve", lambda v, sj=sj: v.tensor_copy(
                        out=x1fm_t[:, sj * 128:(sj + 1) * 128], in_=ptr[:]),
                        waits=[("pe", tv)], inc="dve")
                vlast = sc.c("dve")
                sc.op("sync", lambda e, tt=tt: e.dma_start(
                    out=x1_dram[tt * 128:(tt + 1) * 128, :], in_=xout[:]),
                    waits=[("dve", vlast)], inc="st1", dma=True)
                sc.op("sync", lambda e, tt=tt: e.dma_start(
                    out=x1fm_dram[:, tt * 128:(tt + 1) * 128].rearrange("(j p) t -> p j t", p=128),
                    in_=x1fm_t[:].rearrange("p (j t) -> p j t", j=8)),
                    waits=[("dve", vlast)], inc="st1", dma=True)

            b1_st = sc.c("st1")
            sc.flush(block)
        es2 = ExitStack()
        es2.__enter__()
        wg_sb = es2.enter_context(nc.sbuf_tensor("B_wg_sb", [128, 8 * 3072], BF16))
        wu_sb = es2.enter_context(nc.sbuf_tensor("B_wu_sb", [128, 8 * 3072], BF16))
        wd_sb = es2.enter_context(nc.sbuf_tensor("B_wd_sb", [128, 24 * D], BF16))
        stage_i = [es2.enter_context(nc.sbuf_tensor(f"B_stage_i{i}", [128, 512], I32))
                   for i in range(2)]
        h_fm = es2.enter_context(nc.sbuf_tensor("B_h_fm", [128, 24 * 512], BF16))
        silu_t = es2.enter_context(nc.sbuf_tensor("B_silu_t", [128, 512], F32))
        x1fm_blk = es2.enter_context(nc.sbuf_tensor("B_x1fm_blk", [128, 8 * 512], BF16))
        dq_jobs = []
        for (wsb, wdr, nch, wid) in ((wg_sb, wgT_d, 8, 3072), (wu_sb, wuT_d, 8, 3072),
                                     (wd_sb, wdT_d, 24, D)):
            for j in range(nch):
                for cc2 in range(wid // 512):
                    dq_jobs.append((wsb, wdr, j, wid, cc2))
        dq_prev = {}
        for i, (wsb, wdr, j, wid, cc2) in enumerate(dq_jobs):
            st = i % 2
            w = [dq_prev[st]] if st in dq_prev else []
            lv = sc.op("sync", lambda g, wdr=wdr, j=j, cc2=cc2, st=st: g.dma_start(
                out=stage_i[st][:], in_=wdr[j * 128:(j + 1) * 128, cc2 * 512:(cc2 + 1) * 512]),
                waits=w, inc="lst", dma=True)
            dq_prev[st] = ("act", sc.op("act", lambda a, wsb=wsb, j=j, wid=wid, cc2=cc2, st=st: a.activation(
                out=wsb[:, j * wid + cc2 * 512: j * wid + (cc2 + 1) * 512], in_=stage_i[st][:],
                func=AF.Copy, scale=1.0 / 127.0),
                waits=[("lst", lv)], inc="act"))
        dq_done = sc.c("act")

        NB = TPC // 512
        for bb in range(NB):
            xv = sc.op("sync", lambda e, bb=bb: e.dma_start(
                out=x1fm_blk[:].rearrange("p (j t) -> p j t", j=8),
                in_=x1fm_dram[:, bb * 512:(bb + 1) * 512].rearrange("(j p) t -> p j t", p=128)),
                waits=[("st1", b1_st)] + ([("dve", sc.c("dve"))] if bb else []),
                inc="ld2", dma=True)
            for hj in range(24):
                gmm = umm = None
                for j in range(8):
                    w = []
                    if j == 0:
                        w = [("ld2", xv), ("act", dq_done)]
                        if hj >= 2:
                            w.append(("dve", sc.c("dve")))
                    gmm = sc.op("pe", lambda t, hj=hj, j=j: t.matmul(
                        pb[hj % 2][:],
                        wg_sb[:, j * 3072 + hj * 128: j * 3072 + (hj + 1) * 128],
                        x1fm_blk[:, j * 512:(j + 1) * 512],
                        start=(j == 0), stop=(j == 7)),
                        waits=w, inc="pe" if j == 7 else None)
                for j in range(8):
                    umm = sc.op("pe", lambda t, hj=hj, j=j: t.matmul(
                        pb[2 + hj % 2][:],
                        wu_sb[:, j * 3072 + hj * 128: j * 3072 + (hj + 1) * 128],
                        x1fm_blk[:, j * 512:(j + 1) * 512],
                        start=(j == 0), stop=(j == 7)),
                        inc="pe" if j == 7 else None)
                a1 = sc.op("act", lambda a, hj=hj: a.activation(
                    out=silu_t[:], in_=pb[hj % 2][:], func=AF.Silu),
                    waits=[("pe", gmm)] + ([("dve", sc.c("dve"))] if hj >= 1 else []),
                    inc="act")
                sc.op("dve", lambda v, hj=hj: v.tensor_mul(
                    out=h_fm[:, hj * 512:(hj + 1) * 512], in0=pb[2 + hj % 2][:],
                    in1=silu_t[:]),
                    waits=[("act", a1), ("pe", umm)], inc="dve")
            hdone = sc.c("dve")
            for st_i in range(4):
                tt = bb * 4 + st_i
                xv2 = sc.op("sync", lambda e, tt=tt: e.dma_start(
                    out=xres[:], in_=x1_dram[tt * 128:(tt + 1) * 128, :]),
                    waits=[("st1", b1_st), ("dve", sc.c("dve"))], inc="ld2", dma=True)
                dmm = []
                for oj in range(2):
                    for hj in range(24):
                        w = []
                        if hj == 0:
                            w = [("dve", hdone)]
                            w.append(("dve", sc.c("dve")))
                        mm = sc.op("pe", lambda t, oj=oj, hj=hj, st_i=st_i: t.matmul(
                            pb[4 + oj][:],
                            h_fm[:, hj * 512 + st_i * 128: hj * 512 + (st_i + 1) * 128],
                            wd_sb[:, hj * D + oj * 512: hj * D + (oj + 1) * 512],
                            start=(hj == 0), stop=(hj == 23)),
                            waits=w, inc="pe" if hj == 23 else None)
                    dmm.append(mm)
                for oj in range(2):
                    sc.op("dve", lambda v, oj=oj: v.tensor_add(
                        out=pre[:, oj * 512:(oj + 1) * 512], in0=pb[4 + oj][:],
                        in1=xres[:, oj * 512:(oj + 1) * 512]),
                        waits=[("pe", dmm[oj]), ("ld2", xv2)], inc="dve")
                v2 = sc.op("dve", lambda v: v.reduce_sum(out=mean[:, 0:1], in_=pre[:], axis=AXL.X),
                           inc="dve")
                v3 = sc.op("dve", lambda v: v.tensor_scalar_mul(
                    out=nmean[:, 0:1], in0=mean[:, 0:1], scalar1=-1.0 / D),
                    waits=[("dve", v2)], inc="dve")
                a1 = sc.op("act", lambda a: a.activation(
                    out=xout[:], in_=pre[:], func=AF.Square, bias=nmean[:, 0:1], scale=1.0,
                    accum_out=var[:, 0:1]),
                    waits=[("dve", v3), ("st", sc.c("st")), ("st1", sc.c("st1"))], inc="act")
                v4 = sc.op("dve", lambda v: v.tensor_scalar(
                    out=var[:, 0:1], in0=var[:, 0:1], scalar1=1.0 / D, scalar2=1e-5,
                    op0=ALU.mult, op1=ALU.add),
                    waits=[("act", a1)], inc="dve")
                a2 = sc.op("act", lambda a: a.activation(
                    out=rstd[:, 0:1], in_=var[:, 0:1], func=AF.Sqrt),
                    waits=[("dve", v4)], inc="act")
                v5 = sc.op("dve", lambda v: v.reciprocal(out=rstd[:, 0:1], in_=rstd[:, 0:1]),
                           waits=[("act", a2)], inc="dve")
                v6 = sc.op("dve", lambda v: v.tensor_scalar(
                    out=xout[:], in0=pre[:], scalar1=nmean[:, 0:1], scalar2=rstd[:, 0:1],
                    op0=ALU.add, op1=ALU.mult),
                    waits=[("dve", v5)], inc="dve")
                if not fused:
                    sc.op("sync", lambda e, tt=tt: e.dma_start(
                        out=dram["x2o"][tt * 128:(tt + 1) * 128, :], in_=xout[:]),
                        waits=[("dve", v6)], inc="st", dma=True)
                else:
                    sc.op("sync", lambda e, tt=tt: e.dma_start(
                        out=x2_dram[tt * 128:(tt + 1) * 128, :], in_=xout[:]),
                        waits=[("dve", v6)], inc="st1", dma=True)
                    v7 = sc.op("dve", lambda v: v.tensor_copy(out=xout_bf[:], in_=xout[:]),
                               waits=[("dve", v6)], inc="dve")
                    for sj in range(8):
                        tv = sc.op("pe", lambda t, sj=sj: t.transpose(
                            ptr[:], xout_bf[:, sj * 128:(sj + 1) * 128], ident[:]),
                            waits=[("dve", sc.c("dve"))], inc="pe")
                        sc.op("dve", lambda v, sj=sj: v.tensor_copy(
                            out=x1fm_t[:, sj * 128:(sj + 1) * 128], in_=ptr[:]),
                            waits=[("pe", tv)], inc="dve")
                    vl2 = sc.c("dve")
                    sc.op("sync", lambda e, tt=tt: e.dma_start(
                        out=x2fm_dram[:, tt * 128:(tt + 1) * 128].rearrange("(j p) t -> p j t", p=128),
                        in_=x1fm_t[:].rearrange("p (j t) -> p j t", j=8)),
                        waits=[("dve", vl2)], inc="st1", dma=True)
        if not fused:
            sc.op("sync", lambda e: None, waits=[("st", sc.c("st"))])
        else:
            sc.op("pool", lambda g, ai=dram["ag2_in"], ao=dram["ag2_out"]: g.collective_compute(
                "AllGather", ALU.bypass, replica_groups=[[0, 1, 2, 3], [4, 5, 6, 7]],
                ins=[ai.opt()], outs=[ao.opt()]),
                waits=[("st1", sc.c("st1"))], inc="cc")
        sc.flush(block)
        es2.__exit__(None, None, None)




def build_C(sign=1.0):
    nc = bass.Bass("TRN2", target_bir_lowering=False)
    dram = {}
    dram["x2fm"] = nc.dram_tensor("x2fm", [D, T], F32, kind="ExternalInput").ap()
    for w in ("q", "k", "v"):
        dram[f"w{w}C"] = nc.dram_tensor(f"w{w}C", [D, HPC * HD], F32, kind="ExternalInput").ap()
    dram["bqkvC"] = nc.dram_tensor("bqkvC", [3, HPC * HD], F32, kind="ExternalInput").ap()
    dram["gain"] = nc.dram_tensor("gain", [1, 1], F32, kind="ExternalInput").ap()
    dram["yfmC"] = nc.dram_tensor("yfm", [HPC * HD, T], BF16, kind="ExternalOutput").ap()
    es0 = ExitStack()
    with es0:
        sc = Sched(nc, es0)
        block = es0.enter_context(nc.Block())
        phase_C(nc, sc, block, dram, sign, fused=False)
    return nc


def phase_C(nc, sc, block, dram, sign, fused):
    wC = {w: dram[f"w{w}C"] for w in ("q", "k", "v")}
    bqkv = dram["bqkvC"]
    gain_d = dram["gain"]
    xfm = None if fused else dram["x2fm"]
    ag2_out = dram.get("ag2_out")

    TT = 512
    NTT = T // TT
    NG = T // 64
    NQT = T // 512
    es = ExitStack()
    with es:
        def sb(name, shape, dt=F32):
            return es.enter_context(nc.sbuf_tensor("C_" + name, shape, dt))

        bias_sb = sb("bias", [128, 6], F32)
        w_sb = {w: sb("w" + w, [128, 8 * 256], BF16) for w in ("q", "k", "v")}
        x_t = [sb(f"x_t{i}", [128, 8 * TT], BF16) for i in range(2)]
        q65 = [sb(f"q65_{i}", [65, T], BF16) for i in range(2)]
        k65 = [sb(f"k65_{i}", [65, T], BF16) for i in range(2)]
        vt = sb("vt", [128, 2 * NCHUNK * 65], BF16)
        ident = sb("ident", [128, 128], BF16)
        identf = sb("identf", [128, 128], F32)
        onesf = sb("onesf", [1, 128], F32)
        ksum = sb("ksum", [64, 2], F32)
        ksumb = sb("ksumb", [64, 2], BF16)
        NQS = T // 128
        absum_h = [sb(f"absum{i}", [128, NQS], F32) for i in range(2)]
        epsc_h = [sb(f"epsc{i}", [128, NQS], F32) for i in range(2)]
        acc8 = [sb(f"acc8_{i}", [128, 8], F32) for i in range(2)]
        uabs = sb("uabs", [128, 1024], BF16)
        uabs_d = sb("uabs_d", [128, 1024], BF16)
        v_stage = sb("v_stage", [128, 512], F32)
        gaincol = sb("gaincol", [128, 1], F32)
        g4inv = sb("g4inv", [128, 1], F32)
        scr = sb("scr", [128, 2], F32)
        madt = sb("madt", [128, NQS], F32)
        m2t = sb("m2t", [128, NQS], F32)
        u_t = [sb(f"u_t{i}", [128, 1024], BF16) for i in range(3)]
        r_t = [sb(f"r_t{i}", [128, 1024], BF16) for i in range(3)]
        dmask = [sb(f"dmask{i}", [128, 512], BF16) for i in range(4)]
        epsrow = [sb(f"epsrow{i}", [1, 512], F32) for i in range(2)]
        denrow = [sb(f"denrow{i}", [1, 512], F32) for i in range(2)]
        rden = [sb(f"rdenC{i}", [1, 512], F32) for i in range(2)]
        bcast_sb = [sb(f"bcastC{i}", [64, 512], F32) for i in range(2)]
        bcast_sb2 = [sb(f"bcastD{i}", [64, 512], F32) for i in range(2)]
        y_t = [sb(f"y_t{i}", [128, 512], BF16) for i in range(2)]

        pa = es.enter_context(nc.psum_tensor("C_pa", [128, 2048], F32))
        pa2 = es.enter_context(nc.psum_tensor("C_pa2", [128, 2048], F32))
        ptrf = pa2[:, 1536:1664]
        pproj = {0: [pa[:, 0:512], pa[:, 512:1024], pa[:, 1024:1536]],
                 1: [pa2[:, 0:512], pa2[:, 512:1024], pa2[:, 1024:1536]]}
        psF = [pa[:, 0:1024], pa[:, 1024:2048], pa2[:, 0:1024], pa2[:, 1024:2048]]
        psc2 = [pa[:, 0:1024], pa[:, 1024:2048]]
        py = [pa2[0:65, 0:512], pa2[0:65, 512:1024]]
        prow = [pa2[:, 1024:1536], pa2[:, 1536:2048]]
        pmucol = pa2[:, 1536:1664]
        pgain = pa2[:, 2047:2048]

        make_ident(sc, ident)
        v = sc.op("pool", lambda g: g.memset(identf[:], 1.0), inc="pool")
        sc.op("pool", lambda g: g.affine_select(
            out=identf[:], in_=identf[:], pattern=[[-1, 128]],
            compare_op=ALU.is_equal, fill=0.0, base=0, channel_multiplier=1),
            waits=[("pool", v)], inc="pool")
        sc.op("pool", lambda g: g.memset(onesf[:], 1.0), inc="pool")
        sc.op("pool", lambda g: g.memset(vt[:], 1.0), inc="pool")
        for i in range(2):
            sc.op("pool", lambda g, i=i: g.memset(k65[i][:], 1.0), inc="pool")
        for j in range(4):
            v = sc.op("pool", lambda g, j=j: g.memset(dmask[j][:], 1.0), inc="pool")
            v = sc.op("pool", lambda g, j=j: g.affine_select(
                out=dmask[j][:], in_=dmask[j][:], pattern=[[1, 512]],
                compare_op=ALU.is_ge, fill=0.0, base=-128 * j, channel_multiplier=-1),
                waits=[("pool", v)], inc="pool")
        init_done = sc.c("pool")

        for w in ("q", "k", "v"):
            for j in range(8):
                sc.op("pool", lambda g, w=w, j=j: g.dma_start(
                    out=w_sb[w][:, j * 256:(j + 1) * 256],
                    in_=wC[w][j * 128:(j + 1) * 128, :]), inc="ld", dma=True)
        for wi in range(3):
            for hh in range(2):
                sc.op("pool", lambda g, wi=wi, hh=hh: g.dma_start(
                    out=bias_sb[:, wi * 2 + hh: wi * 2 + hh + 1],
                    in_=bqkv[wi:wi + 1, hh * 128:(hh + 1) * 128].rearrange("a e -> e a")),
                    inc="ld", dma=True)
        gv = sc.op("pool", lambda g: g.dma_start(out=scr[0:1, 0:1], in_=gain_d),
                   inc="ld", dma=True)
        ld_w = sc.c("ld")

        gmm = sc.op("pe", lambda t: t.matmul(
            pgain, onesf[:], scr[0:1, 0:1], start=True, stop=True),
            waits=[("ld", gv), ("pool", init_done)], inc="pe")
        a_g = sc.op("act", lambda a: a.copy(out=gaincol[:, 0:1], in_=pgain),
                    waits=[("pe", gmm)], inc="act")
        v1 = sc.op("dve", lambda v: v.tensor_mul(out=scr[:, 0:1], in0=gaincol[:, 0:1],
                                                 in1=gaincol[:, 0:1]),
                   waits=[("act", a_g)], inc="dve")
        v2 = sc.op("dve", lambda v: v.tensor_mul(out=scr[:, 1:2], in0=scr[:, 0:1],
                                                 in1=scr[:, 0:1]),
                   waits=[("dve", v1)], inc="dve")
        g4_done = sc.op("dve", lambda v: v.reciprocal(out=g4inv[:, 0:1], in_=scr[:, 1:2]),
                        waits=[("dve", v2)], inc="dve")

        pair_end = None
        yst = {}
        for pair in range(C_PAIRS):
            if pair == 1:
                sc.op("pe", lambda t: t.nop(), waits=pair_end, inc="pe")
            pe_tile_done = {}
            dve_tile_done = {}
            for ti in range(NTT):
                buf = ti % 2
                if ti >= 2:
                    waits = [("pe", pe_tile_done[ti - 2])]
                elif pair == 1:
                    waits = [("pe", prev_proj_pe)]
                else:
                    waits = []
                if fused:
                    rr = ti // 4
                    t0 = (ti % 4) * TT
                    xv = sc.op("pool", lambda g, buf=buf, rr=rr, t0=t0: g.dma_start(
                        out=x_t[buf][:].rearrange("p (j t) -> p j t", j=8),
                        in_=ag2_out[rr * D:(rr + 1) * D, t0:t0 + TT].rearrange("(j p) t -> p j t", p=128)),
                        waits=waits + [("cc", dram["cc_wait"])], inc=f"lx{buf}", dma=True)
                else:
                    xv = sc.op("pool", lambda g, buf=buf, ti=ti: g.dma_start(
                        out=x_t[buf][:].rearrange("p (j t) -> p j t", j=8),
                        in_=xfm[:, ti * TT:(ti + 1) * TT].rearrange("(j p) t -> p j t", p=128)),
                        waits=waits, inc=f"lx{buf}", dma=True)
                for wi, w in enumerate(("q", "k", "v")):
                    bank = pproj[buf][wi]
                    bcol = bias_sb[:, wi * 2 + pair: wi * 2 + pair + 1]
                    for j in range(8):
                        wts = []
                        if j == 0:
                            wts = [("ld", ld_w), (f"lx{buf}", xv)]
                            if ti > 0:
                                wts.append(("dve", dve_tile_done[ti - 1]))
                        mm = sc.op("pe", lambda t, bank=bank, w=w, j=j, buf=buf, pair=pair: t.matmul(
                            bank, w_sb[w][:, j * 256 + pair * 128: j * 256 + pair * 128 + 128],
                            x_t[buf][:, j * TT:(j + 1) * TT],
                            start=(j == 0), stop=(j == 7)),
                            waits=wts, inc="pe" if j == 7 else None)
                    if w == "q":
                        for hh in range(2):
                            dst = q65[hh][0:64, ti * TT:(ti + 1) * TT]
                            sc.op("dve", lambda v, bank=bank, bcol=bcol, hh=hh, dst=dst: v.tensor_scalar(
                                out=dst, in0=bank[hh * 64:(hh + 1) * 64, :],
                                scalar1=bcol[hh * 64:(hh + 1) * 64, :], scalar2=float(sign),
                                op0=ALU.add, op1=ALU.mult),
                                waits=[("pe", mm), ("ld", ld_w)], inc="dve")
                    elif w == "k":
                        for hh in range(2):
                            dst = k65[hh][0:64, ti * TT:(ti + 1) * TT]
                            sc.op("dve", lambda v, bank=bank, bcol=bcol, hh=hh, dst=dst: v.tensor_scalar_add(
                                out=dst, in0=bank[hh * 64:(hh + 1) * 64, :],
                                scalar1=bcol[hh * 64:(hh + 1) * 64, :]),
                                waits=[("pe", mm), ("ld", ld_w)], inc="dve")
                    else:
                        sc.op("dve", lambda v, bank=bank, bcol=bcol: v.tensor_scalar_add(
                            out=v_stage[:], in0=bank, scalar1=bcol),
                            waits=[("pe", mm), ("ld", ld_w)], inc="dve")
                        for sti in range(4):
                            c = ti * 4 + sti
                            tv = sc.op("pe", lambda t, sti=sti: t.transpose(
                                ptrf, v_stage[:, sti * 128:(sti + 1) * 128], identf[:]),
                                waits=[("dve", sc.c("dve")), ("pool", init_done)], inc="pe")
                            for hh in range(2):
                                sc.op("dve", lambda v, hh=hh, c=c: v.tensor_copy(
                                    out=vt[:, (hh * NCHUNK + c) * 65:(hh * NCHUNK + c) * 65 + 64],
                                    in_=ptrf[:, hh * 64:(hh + 1) * 64]),
                                    waits=[("pe", tv)], inc="dve")
                pe_tile_done[ti] = sc.c("pe")
                dve_tile_done[ti] = sc.c("dve")
            proj_dve = sc.c("dve")
            prev_proj_pe = sc.c("pe")

            for hh in range(2):
                r0 = sc.op("dve", lambda v, hh=hh: v.reduce_sum(
                    out=ksum[:, hh:hh + 1], in_=k65[hh][0:64, :], axis=AXL.X),
                    waits=[("dve", proj_dve)], inc="dve")
                sc.op("dve", lambda v, hh=hh: v.tensor_copy(
                    out=ksumb[:, hh:hh + 1], in_=ksum[:, hh:hh + 1]),
                    waits=[("dve", r0)], inc="dve")
            ksum_done = sc.c("dve")

            murow_ev = {}
            for hh in range(2):
                for tt in range(NQT):
                    idx = hh * NQT + tt
                    bankr = prow[idx % 2][0:1, 0:512]
                    w = ([("dve", ksum_done), ("act", a_g)] if idx < 2
                         else [("act", murow_ev[idx - 2])])
                    mm = sc.op("pe", lambda t, hh=hh, tt=tt, bankr=bankr: t.matmul(
                        bankr, ksumb[:, hh:hh + 1], q65[hh][0:64, tt * 512:(tt + 1) * 512],
                        start=True, stop=True),
                        waits=w, inc="pe")
                    murow_ev[idx] = sc.op("act", lambda a, hh=hh, tt=tt, bankr=bankr: a.activation(
                        out=q65[hh][64:65, tt * 512:(tt + 1) * 512], in_=bankr,
                        func=AF.Copy, scale=-1.0 / T),
                        waits=[("pe", mm)], inc="act")
            murow_act = sc.c("act")

            last_read = {0: None, 1: None, 2: None, 3: None}
            red_done = {}
            eps_done = {}
            for hh in range(2):
                for qs in range(NQS):
                    u = hh * NQS + qs
                    upar = u % 2
                    for kq in range(8):
                        t_i = u * 8 + kq
                        buf = t_i % 4
                        for kk in range(2):
                            w = []
                            if kk == 0:
                                if last_read[buf] is not None:
                                    w.append(last_read[buf])
                                if u == 0 and kq < 4:
                                    w.append(("act", murow_act))
                            mm = sc.op("pe", lambda t, hh=hh, qs=qs, buf=buf, kq=kq, kk=kk: t.matmul(
                                psF[buf][:, kk * 512:(kk + 1) * 512],
                                q65[hh][:, qs * 128:(qs + 1) * 128],
                                k65[hh][:, (kq * 2 + kk) * 512:(kq * 2 + kk) * 512 + 512],
                                start=True, stop=True),
                                waits=w, inc="pe" if kk == 1 else None)
                        if t_i % 2 == 0:
                            w = [("pe", mm)]
                            if u >= 2 and kq < 2:
                                w.append(("dve", red_done[u - 2]))
                            rd = sc.op("dve", lambda v, buf=buf, upar=upar, kq=kq: v.reduce_sum(
                                out=acc8[upar][:, kq:kq + 1], in_=psF[buf],
                                axis=AXL.X, apply_absolute_value=True),
                                waits=w, inc="dve")
                            last_read[buf] = ("dve", rd)
                        else:
                            w = [("pe", mm)]
                            if u >= 2 and kq < 2:
                                w.append(("dve", red_done[u - 2]))
                            rd = sc.op("act", lambda a, buf=buf, upar=upar, kq=kq: a.activation(
                                out=uabs[:], in_=psF[buf],
                                func=AF.Abs, accum_out=acc8[upar][:, kq:kq + 1]),
                                waits=w, inc="act")
                            last_read[buf] = ("act", rd)
                    w = [("act", sc.c("act")), ("dve", sc.c("dve"))]
                    red_done[u] = sc.op("dve", lambda v, hh=hh, qs=qs, upar=upar: v.reduce_sum(
                        out=absum_h[hh][:, qs:qs + 1], in_=acc8[upar][:], axis=AXL.X),
                        waits=w, inc="dve")
            stats_dve = sc.c("dve")
            stats_act = sc.c("act")

            for hh in range(2):
                w1 = sc.op("dve", lambda v, hh=hh: v.tensor_scalar(
                    out=madt[:], in0=absum_h[hh][:], scalar1=1.0 / T, scalar2=1e-6,
                    op0=ALU.mult, op1=ALU.add),
                    waits=[("dve", stats_dve)] if hh == 0 else [], inc="dve")
                sc.op("dve", lambda v: v.tensor_mul(out=m2t[:], in0=madt[:], in1=madt[:]),
                      waits=[("dve", w1)], inc="dve")
                sc.op("dve", lambda v: v.tensor_mul(out=madt[:], in0=m2t[:], in1=m2t[:]),
                      inc="dve")
                eps_done[hh] = sc.op("dve", lambda v, hh=hh: v.tensor_scalar(
                    out=epsc_h[hh][:], in0=madt[:], scalar1=g4inv[:, 0:1], scalar2=1e-6,
                    op0=ALU.mult, op1=ALU.mult),
                    waits=[("dve", g4_done)], inc="dve")

            relu_done = {}
            u_free = {}
            r_free = {}
            sq_done = {}
            bc_free = {}
            avmm = {}
            eps_ev = {}
            eps_free = {}
            prow_free = {}
            ymult_done = 0
            for qt in range(NQT):
                for hh in range(2):
                    for jj in range(4):
                        qs = qt * 4 + jj
                        w = []
                        if jj == 0:
                            w = [("dve", eps_done[hh])]
                            if qt == 0:
                                w.append(("act", stats_act))
                            if hh in prow_free:
                                w.append(prow_free[hh])
                        tv = sc.op("pe", lambda t, hh=hh, qs=qs, jj=jj: t.transpose(
                            prow[hh][0:1, jj * 128:(jj + 1) * 128],
                            epsc_h[hh][:, qs:qs + 1], identf[:]),
                            waits=w, inc="pe" if jj == 3 else None)
                    eps_ev[hh] = sc.op("act", lambda a, hh=hh: a.copy(
                        out=epsrow[hh][:], in_=prow[hh][0:1, 0:512]),
                        waits=[("pe", tv)] +
                              ([("dve", eps_free[hh])] if hh in eps_free else []),
                        inc="act")

                kcmax = 4 * qt + 4

                def sched_av(kc, kcmax=kcmax, qt=qt):
                    buf3 = kc % 3
                    for hh in range(2):
                        w = [("dve", sq_done[(kc, "r")])]
                        if kc == 0:
                            w.append(("dve", ymult_done))
                            if qt == 0:
                                w.append(("act", stats_act))
                        avmm[hh] = sc.op("pe", lambda t, hh=hh, kc=kc, buf3=buf3, kcmax=kcmax: t.matmul(
                            py[hh],
                            vt[:, (hh * NCHUNK + kc) * 65:(hh * NCHUNK + kc) * 65 + 65],
                            r_t[buf3][:, hh * 512:(hh + 1) * 512],
                            start=(kc == 0), stop=(kc == kcmax - 1)),
                            waits=(w if hh == 0 else []), inc="pe")
                    r_free[buf3] = avmm[1]

                for kc in range(kcmax):
                    buf = kc % 2
                    buf3 = kc % 3
                    w = []
                    if (buf,) in relu_done:
                        w.append(("act", relu_done[(buf,)]))
                    elif qt == 0 and kc < 2:
                        w.append(("dve", stats_dve))
                        w.append(("act", stats_act))
                    for hh in range(2):
                        mm = sc.op("pe", lambda t, buf=buf, hh=hh, kc=kc, qt=qt: t.matmul(
                            psc2[buf][:, hh * 512:(hh + 1) * 512],
                            k65[hh][:, kc * 128:(kc + 1) * 128],
                            q65[hh][:, qt * 512:(qt + 1) * 512],
                            start=True, stop=True),
                            waits=(w if hh == 0 else []), inc="pe")
                    if kc >= 2:
                        sched_av(kc - 2)
                    rv = sc.op("act", lambda a, buf=buf, buf3=buf3: a.activation(
                        out=u_t[buf3][:], in_=psc2[buf][:], func=AF.Relu),
                        waits=[("pe", mm)] +
                              ([("dve", u_free[buf3])] if buf3 in u_free else []),
                        inc="act")
                    relu_done[(buf,)] = rv
                    d1 = sc.op("dve", lambda v, buf3=buf3: v.tensor_mul(
                        out=u_t[buf3][:], in0=u_t[buf3][:], in1=u_t[buf3][:]),
                        waits=[("act", rv)], inc="dve")
                    w = []
                    if buf3 in r_free:
                        w.append(("pe", r_free[buf3]))
                    d2 = sc.op("dve", lambda v, buf3=buf3: v.tensor_mul(
                        out=r_t[buf3][:], in0=u_t[buf3][:], in1=u_t[buf3][:]),
                        waits=w, inc="dve")
                    u_free[buf3] = d2
                    if kc >= 4 * qt:
                        for hh in range(2):
                            d2 = sc.op("dve", lambda v, buf3=buf3, kc=kc, qt=qt, hh=hh: v.tensor_mul(
                                out=r_t[buf3][:, hh * 512:(hh + 1) * 512],
                                in0=r_t[buf3][:, hh * 512:(hh + 1) * 512],
                                in1=dmask[kc - 4 * qt][:]),
                                inc="dve")
                    sq_done[(kc, "r")] = d2
                sched_av(kcmax - 2)
                sched_av(kcmax - 1)
                for hh in range(2):
                    d1 = sc.op("dve", lambda v, hh=hh: v.tensor_add(
                        out=denrow[hh][:], in0=py[hh][64:65, 0:512], in1=epsrow[hh][:]),
                        waits=[("act", eps_ev[hh]), ("pe", avmm[hh])], inc="dve")
                    eps_free[hh] = d1
                    bmm = sc.op("pe", lambda t, hh=hh: t.matmul(
                        prow[hh][0:64, 0:512], onesf[0:1, 0:64], denrow[hh][:],
                        start=True, stop=True),
                        waits=[("dve", d1), ("act", eps_ev[hh])], inc="pe")
                    bev = sc.op("act", lambda a, hh=hh: a.activation(
                        out=bcast_sb[hh][:], in_=prow[hh][0:64, 0:512], func=AF.Ln),
                        waits=[("pe", bmm)] +
                              ([("dve", bc_free[hh])] if hh in bc_free else []),
                        inc="act")
                    prow_free[hh] = ("act", bev)
                    e2 = sc.op("act", lambda a, hh=hh: a.activation(
                        out=bcast_sb2[hh][:], in_=bcast_sb[hh][:], func=AF.Exp, scale=-1.0),
                        waits=([("dve", bc_free[hh])] if hh in bc_free else []), inc="act")
                    rot = qt % 2
                    gq = pair * NQT + qt
                    wy = [("act", e2)]
                    if gq >= 2 and hh == 0:
                        wy.append(("st1", yst[gq - 2]))
                    ym = sc.op("dve", lambda v, hh=hh, rot=rot: v.tensor_mul(
                        out=y_t[rot][hh * 64:(hh + 1) * 64, :],
                        in0=py[hh][0:64, 0:512], in1=bcast_sb2[hh][:]),
                        waits=wy, inc="dve")
                    bc_free[hh] = ym
                    ymult_done = ym
                rot = qt % 2
                gq = pair * NQT + qt
                if not fused:
                    yst[gq] = sc.op("sync", lambda e, qt=qt, rot=rot, pair=pair: e.dma_start(
                        out=dram["yfmC"][pair * 128:(pair + 1) * 128, qt * 512:(qt + 1) * 512],
                        in_=y_t[rot][:]),
                        waits=[("dve", ymult_done)], inc="st1", dma=True)
                else:
                    qq = qt // 4
                    c0 = (qt % 4) * 512
                    for tgt in (qq, qq + 4):
                        yst[gq] = sc.op("sync", lambda e, tgt=tgt, c0=c0, rot=rot, pair=pair: e.dma_start(
                            out=dram["a2a3_in"][tgt * 256 + pair * 128: tgt * 256 + (pair + 1) * 128,
                                                c0:c0 + 512],
                            in_=y_t[rot][:]),
                            waits=[("dve", ymult_done)] if tgt == qq else [], inc="st1", dma=True)
            pair_end = [("act", sc.c("act")), ("dve", sc.c("dve")), ("pool", sc.c("pool")),
                        ("st1", sc.c("st1"))]

        if not fused:
            sc.op("sync", lambda e: None, waits=[("st1", sc.c("st1"))])
        else:
            def _a2a(g, a2a_in=dram["a2a3_in"], a2a_out=dram["a2a3_out"]):
                with allow_ici_all_to_all():
                    return g.collective_compute(
                        "AllToAll", ALU.bypass, replica_groups=[list(range(8))],
                        ins=[a2a_in.opt()], outs=[a2a_out.opt()])
            sc.op("pool", _a2a, waits=[("st1", sc.c("st1"))], inc="cc")
        sc.flush(block)





def build_D(stage=4):
    nc = bass.Bass("TRN2", target_bir_lowering=False)
    dram = {}
    dram["y1fm_sl"] = nc.dram_tensor("y1fm_sl", [D, TPC], BF16, kind="ExternalInput").ap()
    dram["x2_sl"] = nc.dram_tensor("x2_sl", [TPC, D], F32, kind="ExternalInput").ap()
    dram["wproj"] = nc.dram_tensor("wproj", [D, D], F32, kind="ExternalInput").ap()
    dram["bproj"] = nc.dram_tensor("bproj", [1, D], F32, kind="ExternalInput").ap()
    dram["wLT"] = nc.dram_tensor("wLT", [D, D], F32, kind="ExternalInput").ap()
    dram["bL"] = nc.dram_tensor("bL", [1, D], F32, kind="ExternalInput").ap()
    dram["ldT"] = nc.dram_tensor("ldT", [100, 4], F32, kind="ExternalInput").ap()
    dram["da"] = nc.dram_tensor("da", [100, D], F32, kind="ExternalInput").ap()
    dram["luT"] = nc.dram_tensor("luT", [4, 50], F32, kind="ExternalInput").ap()
    dram["uaT"] = nc.dram_tensor("uaT", [50, D], F32, kind="ExternalInput").ap()
    dram["x4o"] = nc.dram_tensor("x4o", [TPC, D], F32, kind="ExternalOutput").ap()
    dram["x3_scr"] = nc.dram_tensor("x3_scr", [TPC, D], F32).ap()
    dram["x3fm_scr"] = nc.dram_tensor("x3fm_scr", [D, TPC], BF16).ap()
    es0 = ExitStack()
    with es0:
        sc = Sched(nc, es0)
        block = es0.enter_context(nc.Block())
        phase_D(nc, sc, block, dram, fused=False, stage=stage)
    return nc


def phase_D(nc, sc, block, dram, fused, stage=4):
    bp_d = dram["bproj"]
    wl_d = dram["wLT"]
    bl_d = dram["bL"]
    ldT_d = dram["ldT"]
    da_d = dram["da"]
    luT_d = dram["luT"]
    uaT_d = dram["uaT"]
    x4o = dram["x4o"]
    x3_dram = dram["x3_scr"]
    x3fm_dram = dram["x3fm_scr"]
    if fused:
        yfm = dram["a2a3_out"]
        wp_d = dram["wp2"]
        NCJ = 16
        x_sl = dram["x2_scr"]
    else:
        yfm = dram["y1fm_sl"]
        wp_d = dram["wproj"]
        NCJ = 8
        x_sl = dram["x2_sl"]

    es = ExitStack()
    with es:
        def sb(name, shape, dt=F32):
            return es.enter_context(nc.sbuf_tensor("D_" + name, shape, dt))

        yfm_t = sb("yfm_t", [128, NCJ * 128], BF16)
        wp_sb = sb("wp_sb", [128, NCJ * D], BF16)
        weff = sb("weff", [128, 8 * D], BF16)
        bp_bc = sb("bp_bc", [128, D], F32)
        bl_bc = sb("bl_bc", [128, D], F32)
        bp_sb = sb("bp_sb", [1, D], F32)
        bl_sb = sb("bl_sb", [1, D], F32)
        xres = sb("xres", [128, D], F32)
        pre = sb("pre", [128, D], F32)
        mean = sb("mean", [128, 1], F32)
        nmean = sb("nmean", [128, 1], F32)
        var = sb("var", [128, 1], F32)
        rstd = sb("rstd", [128, 1], F32)
        xout = sb("xout", [128, D], F32)
        xout_bf = sb("xout_bf", [128, D], BF16)
        ident = sb("ident", [128, 128], BF16)
        onesf = sb("onesf", [1, 128], F32)
        x3fm_t = sb("x3fm_t", [128, 8 * 128], BF16)
        ldT_sb = sb("ldT_sb", [100, 4], BF16)
        da_sb = sb("da_sb", [100, D], BF16)
        luT_sb = sb("luT_sb", [4, 50], BF16)
        uaT_sb = sb("uaT_sb", [50, D], BF16)
        a1_sb = sb("a1_sb", [4, D], BF16)
        a2_sb = sb("a2_sb", [50, D], BF16)

        pb = [es.enter_context(nc.psum_tensor(f"D_pbD{i}", [128, 512], F32)) for i in range(4)]
        ptr = es.enter_context(nc.psum_tensor("D_ptrD", [128, 128], BF16))

        make_ident(sc, ident)
        sc.op("pool", lambda g: g.memset(onesf[:], 1.0), inc="pool")
        init_done = sc.c("pool")

        for j in range(NCJ):
            sc.op("pool", lambda g, j=j: g.dma_start(
                out=wp_sb[:, j * D:(j + 1) * D], in_=wp_d[j * 128:(j + 1) * 128, :]),
                inc="ld", dma=True)
        for nm, dst, srcd in (("ldT", ldT_sb, ldT_d), ("da", da_sb, da_d),
                              ("luT", luT_sb, luT_d), ("uaT", uaT_sb, uaT_d)):
            sc.op("pool", lambda g, dst=dst, srcd=srcd: g.dma_start(out=dst[:], in_=srcd),
                  inc="ld", dma=True)
        bv = sc.op("pool", lambda g: g.dma_start(out=bp_sb[:], in_=bp_d), inc="ld", dma=True)
        bv = sc.op("pool", lambda g: g.dma_start(out=bl_sb[:], in_=bl_d), inc="ld", dma=True)
        ld_w = sc.c("ld")

        evs = []
        for (row, bc) in ((bp_sb, bp_bc), (bl_sb, bl_bc)):
            for oj in range(2):
                mm = sc.op("pe", lambda t, row=row, oj=oj: t.matmul(
                    pb[3][:], onesf[:], row[:, oj * 512:(oj + 1) * 512],
                    start=True, stop=True),
                    waits=[("ld", ld_w), ("pool", init_done)] +
                          ([("act", evs[-1])] if evs else []),
                    inc="pe")
                evs.append(sc.op("act", lambda a, bc=bc, oj=oj: a.copy(
                    out=bc[:, oj * 512:(oj + 1) * 512], in_=pb[3][:]),
                    waits=[("pe", mm)], inc="act"))
        bias_done = evs[-1]

        prev = bias_done
        for oj in range(2 if stage >= 0 else 0):
            mm = sc.op("pe", lambda t, oj=oj: t.matmul(
                pb[3][0:4, :], ldT_sb[:], da_sb[:, oj * 512:(oj + 1) * 512],
                start=True, stop=True),
                waits=[("act", prev)], inc="pe")
            prev = sc.op("act", lambda a, oj=oj: a.copy(
                out=a1_sb[:, oj * 512:(oj + 1) * 512], in_=pb[3][0:4, :]),
                waits=[("pe", mm)], inc="act")
        for oj in range(2 if stage >= 1 else 0):
            mm = sc.op("pe", lambda t, oj=oj: t.matmul(
                pb[3][0:50, :], luT_sb[:], a1_sb[:, oj * 512:(oj + 1) * 512],
                start=True, stop=True),
                waits=[("act", prev)], inc="pe")
            prev = sc.op("act", lambda a, oj=oj: a.copy(
                out=a2_sb[:, oj * 512:(oj + 1) * 512], in_=pb[3][0:50, :]),
                waits=[("pe", mm)], inc="act")
        wl_stage = sb("wl_stage", [128, D], BF16)
        wdone = prev
        dv = None
        for j in range(8 if stage >= 2 else 0):
            lv = sc.op("pool", lambda g, j=j: g.dma_start(
                out=wl_stage[:], in_=wl_d[j * 128:(j + 1) * 128, :]),
                waits=[("dve", sc.c("dve"))], inc="lst", dma=True)
            for oj in range(2):
                mm = sc.op("pe", lambda t, j=j, oj=oj: t.matmul(
                    pb[2][:], a2_sb[:, j * 128:(j + 1) * 128],
                    uaT_sb[:, oj * 512:(oj + 1) * 512],
                    start=True, stop=True),
                    waits=[("act", wdone)] + ([("dve", dv)] if dv else []), inc="pe")
                dv = sc.op("dve", lambda v, j=j, oj=oj: v.tensor_add(
                    out=weff[:, j * D + oj * 512: j * D + (oj + 1) * 512],
                    in0=pb[2][:], in1=wl_stage[:, oj * 512:(oj + 1) * 512]),
                    waits=[("pe", mm), ("lst", lv)], inc="dve")
        weff_done = sc.c("dve")

        NT1 = TPC // 128
        def ln_tile(tt, src_pre, out_dram, fm_dram, fm_tile, extra_store=None):
            v2 = sc.op("dve", lambda v: v.reduce_sum(out=mean[:, 0:1], in_=src_pre, axis=AXL.X),
                       inc="dve")
            v3 = sc.op("dve", lambda v: v.tensor_scalar_mul(
                out=nmean[:, 0:1], in0=mean[:, 0:1], scalar1=-1.0 / D),
                waits=[("dve", v2)], inc="dve")
            a1 = sc.op("act", lambda a: a.activation(
                out=xout[:], in_=src_pre, func=AF.Square, bias=nmean[:, 0:1], scale=1.0,
                accum_out=var[:, 0:1]),
                waits=[("dve", v3), ("st", sc.c("st")), ("st1", sc.c("st1"))], inc="act")
            v4 = sc.op("dve", lambda v: v.tensor_scalar(
                out=var[:, 0:1], in0=var[:, 0:1], scalar1=1.0 / D, scalar2=1e-5,
                op0=ALU.mult, op1=ALU.add),
                waits=[("act", a1)], inc="dve")
            a2 = sc.op("act", lambda a: a.activation(
                out=rstd[:, 0:1], in_=var[:, 0:1], func=AF.Sqrt),
                waits=[("dve", v4)], inc="act")
            v5 = sc.op("dve", lambda v: v.reciprocal(out=rstd[:, 0:1], in_=rstd[:, 0:1]),
                       waits=[("act", a2)], inc="dve")
            v6 = sc.op("dve", lambda v: v.tensor_scalar(
                out=xout[:], in0=src_pre, scalar1=nmean[:, 0:1], scalar2=rstd[:, 0:1],
                op0=ALU.add, op1=ALU.mult),
                waits=[("dve", v5)], inc="dve")
            stsem = "st1" if fm_dram is not None else "st"
            if fm_dram is not None:
                v7 = sc.op("dve", lambda v: v.tensor_copy(out=xout_bf[:], in_=xout[:]),
                           inc="dve")
                for sj in range(8):
                    tv = sc.op("pe", lambda t, sj=sj: t.transpose(
                        ptr[:], xout_bf[:, sj * 128:(sj + 1) * 128], ident[:]),
                        waits=[("dve", sc.c("dve")), ("pool", init_done)], inc="pe")
                    sc.op("dve", lambda v, sj=sj: v.tensor_copy(
                        out=fm_tile[:, sj * 128:(sj + 1) * 128], in_=ptr[:]),
                        waits=[("pe", tv)], inc="dve")
                vlast = sc.c("dve")
                sc.op("sync", lambda e, tt=tt: e.dma_start(
                    out=fm_dram[:, tt * 128:(tt + 1) * 128].rearrange("(j p) t -> p j t", p=128),
                    in_=fm_tile[:].rearrange("p (j t) -> p j t", j=8)),
                    waits=[("dve", vlast)], inc=stsem, dma=True)
            else:
                vlast = v6
            sc.op("sync", lambda e, tt=tt: e.dma_start(
                out=out_dram[tt * 128:(tt + 1) * 128, :], in_=xout[:]),
                waits=[("dve", vlast)], inc=stsem, dma=True)

        for tt in range(NT1 if stage >= 3 else 0):
            xv = sc.op("sync", lambda e, tt=tt: e.dma_start(
                out=xres[:], in_=x_sl[tt * 128:(tt + 1) * 128, :]),
                waits=([("dve", sc.c("dve"))] if tt else []), inc="ld2", dma=True)
            ccw = [("cc", dram["cc_wait"])] if fused else []
            yv = sc.op("sync", lambda e, tt=tt: e.dma_start(
                out=yfm_t[:].rearrange("p (j t) -> p j t", j=NCJ),
                in_=yfm[:, tt * 128:(tt + 1) * 128].rearrange("(j p) t -> p j t", p=128)),
                waits=([("pe", sc.c("pe"))] if tt else []) + ccw, inc="ld2", dma=True)
            mms = []
            for oj in range(2):
                for j in range(NCJ):
                    w = []
                    if j == 0:
                        w = [("ld", ld_w), ("ld2", max(xv, yv))]
                        if tt > 0:
                            w.append(("dve", sc.c("dve")))
                    mm = sc.op("pe", lambda t, oj=oj, j=j: t.matmul(
                        pb[oj][:], yfm_t[:, j * 128:(j + 1) * 128],
                        wp_sb[:, j * D + oj * 512: j * D + (oj + 1) * 512],
                        start=(j == 0), stop=(j == NCJ - 1)),
                        waits=w, inc="pe" if j == NCJ - 1 else None)
                mms.append(mm)
            for oj in range(2):
                sc.op("dve", lambda v, oj=oj: v.tensor_add(
                    out=pre[:, oj * 512:(oj + 1) * 512], in0=pb[oj][:],
                    in1=bp_bc[:, oj * 512:(oj + 1) * 512]),
                    waits=[("pe", mms[oj]), ("act", bias_done)], inc="dve")
            sc.op("dve", lambda v: v.tensor_add(out=pre[:], in0=pre[:], in1=xres[:]),
                  waits=[("ld2", max(xv, yv))], inc="dve")
            ln_tile(tt, pre[:], x3_dram, x3fm_dram, x3fm_t)
        d1_st = sc.c("st1")

        for tt in range(NT1 if stage >= 4 else 0):
            xv = sc.op("sync", lambda e, tt=tt: e.dma_start(
                out=xres[:], in_=x3_dram[tt * 128:(tt + 1) * 128, :]),
                waits=[("st1", d1_st), ("dve", sc.c("dve"))], inc="ld2", dma=True)
            yv = sc.op("sync", lambda e, tt=tt: e.dma_start(
                out=yfm_t[:, 0:1024].rearrange("p (j t) -> p j t", j=8),
                in_=x3fm_dram[:, tt * 128:(tt + 1) * 128].rearrange("(j p) t -> p j t", p=128)),
                waits=[("st1", d1_st), ("pe", sc.c("pe"))], inc="ld2", dma=True)
            mms = []
            for oj in range(2):
                for j in range(8):
                    w = []
                    if j == 0:
                        w = [("dve", weff_done), ("ld2", max(xv, yv))]
                        if tt > 0:
                            w.append(("dve", sc.c("dve")))
                    mm = sc.op("pe", lambda t, oj=oj, j=j: t.matmul(
                        pb[oj][:], yfm_t[:, j * 128:(j + 1) * 128],
                        weff[:, j * D + oj * 512: j * D + (oj + 1) * 512],
                        start=(j == 0), stop=(j == 7)),
                        waits=w, inc="pe" if j == 7 else None)
                mms.append(mm)
            for oj in range(2):
                sc.op("dve", lambda v, oj=oj: v.tensor_add(
                    out=pre[:, oj * 512:(oj + 1) * 512], in0=pb[oj][:],
                    in1=bl_bc[:, oj * 512:(oj + 1) * 512]),
                    waits=[("pe", mms[oj]), ("act", bias_done)], inc="dve")
            sc.op("dve", lambda v: v.tensor_add(out=pre[:], in0=pre[:], in1=xres[:]),
                  waits=[("ld2", max(xv, yv))], inc="dve")
            ln_tile(tt, pre[:], x4o, None, None)
        sc.op("sync", lambda e: None, waits=[("st", sc.c("st"))])
        sc.flush(block)


HEAD_OF_CORE = [(c // 4, 4 * (c % 4)) for c in range(8)]


def kernel_unfused(**inp):
    x = np.asarray(inp["L0_attn_x"])
    Wq = np.asarray(inp["L0_attn_Wq"])
    Wk = np.asarray(inp["L0_attn_Wk"])
    Wv = np.asarray(inp["L0_attn_Wv"])
    bq = np.asarray(inp["L0_attn_bq"])
    bk = np.asarray(inp["L0_attn_bk"])
    bv = np.asarray(inp["L0_attn_bv"])

    nc = build_A()
    in_maps = []
    for c in range(8):
        b, h0 = HEAD_OF_CORE[c]
        sl = slice(h0 * HD, (h0 + HPC) * HD)
        wo2 = np.concatenate([woT, zeros], 0) if b == 0 else np.concatenate([zeros, woT], 0)
        wp2 = np.concatenate([wproj, zeros], 0) if b == 0 else np.concatenate([zeros, wproj], 0)
        in_maps.append({
            "xfm": np.ascontiguousarray(x[b].T),
            "wqT": np.ascontiguousarray(Wq[sl, :].T),
            "wkT": np.ascontiguousarray(Wk[sl, :].T),
            "wvT": np.ascontiguousarray(Wv[sl, :].T),
            "bqkv": np.ascontiguousarray(np.stack([bq[sl], bk[sl], bv[sl]])),
        })
    res = run_bass_kernel_spmd(nc, in_maps, core_ids=list(range(8)))
    y_fm = [np.concatenate([np.asarray(res.results[4 * b_ + i]["yfm"]) for i in range(4)], axis=0)
            for b_ in range(B)]

    nc = build_B(debug=False)
    in_maps = []
    for c in range(8):
        b, q = c // 4, c % 4
        ts = slice(q * TPC, (q + 1) * TPC)
        in_maps.append({
            "yfm_sl": np.ascontiguousarray(y_fm[b][:, ts]),
            "x_sl": np.ascontiguousarray(x[b][ts, :]),
            "woT": np.ascontiguousarray(np.asarray(inp["L0_attn_Wo"]).T),
            "bo": np.ascontiguousarray(np.asarray(inp["L0_attn_bo"])[None, :]),
            "wgT": np.ascontiguousarray(np.asarray(inp["L0_ffn_w_gate"]).T),
            "wuT": np.ascontiguousarray(np.asarray(inp["L0_ffn_w_up"]).T),
            "wdT": np.ascontiguousarray(np.asarray(inp["L0_ffn_w_down"]).T),
        })
    res = run_bass_kernel_spmd(nc, in_maps, core_ids=list(range(8)))
    x2 = np.zeros((B, T, D), np.float32)
    for c in range(8):
        b, q = c // 4, c % 4
        ts = slice(q * TPC, (q + 1) * TPC)
        x2[b, ts] = np.asarray(res.results[c]["x2o"])

    W_attn = np.asarray(inp["L1_attn_W_attn"])
    b_attn = np.asarray(inp["L1_attn_b_attn"])
    gain = np.asarray(inp["L1_attn_score_gain"]).reshape(1, 1)
    sign = 1.0 if float(gain[0, 0]) >= 0 else -1.0
    nc = build_C(sign=sign)
    in_maps = []
    for c in range(8):
        b, h0 = HEAD_OF_CORE[c]
        cs = slice(h0 * HD, (h0 + HPC) * HD)
        in_maps.append({
            "x2fm": np.ascontiguousarray(x2[b].T),
            "wqC": np.ascontiguousarray(W_attn[:, cs]),
            "wkC": np.ascontiguousarray(W_attn[:, 1024 + h0 * HD: 1024 + (h0 + HPC) * HD]),
            "wvC": np.ascontiguousarray(W_attn[:, 2048 + h0 * HD: 2048 + (h0 + HPC) * HD]),
            "bqkvC": np.ascontiguousarray(np.stack([
                b_attn[cs], b_attn[1024 + h0 * HD: 1024 + (h0 + HPC) * HD],
                b_attn[2048 + h0 * HD: 2048 + (h0 + HPC) * HD]])),
            "gain": np.ascontiguousarray(gain.astype(np.float32)),
        })
    res = run_bass_kernel_spmd(nc, in_maps, core_ids=list(range(8)))
    y1_fm = [np.concatenate([np.asarray(res.results[4 * b_ + i]["yfm"]) for i in range(4)], axis=0)
             for b_ in range(B)]

    nc = build_D()
    in_maps = []
    for c in range(8):
        b, q = c // 4, c % 4
        ts = slice(q * TPC, (q + 1) * TPC)
        in_maps.append({
            "y1fm_sl": np.ascontiguousarray(y1_fm[b][:, ts]),
            "x2_sl": np.ascontiguousarray(x2[b][ts, :]),
            "wproj": np.ascontiguousarray(np.asarray(inp["L1_attn_W_proj"])),
            "bproj": np.ascontiguousarray(np.asarray(inp["L1_attn_b_proj"])[None, :]),
            "wLT": np.ascontiguousarray(np.asarray(inp["L1_ffn_W"]).T),
            "bL": np.ascontiguousarray(np.asarray(inp["L1_ffn_b"])[None, :]),
            "ldT": np.ascontiguousarray(np.asarray(inp["L1_ffn_lora_down"]).T),
            "da": np.ascontiguousarray(np.asarray(inp["L1_ffn_down_aux"])),
            "luT": np.ascontiguousarray(np.asarray(inp["L1_ffn_lora_up"]).T),
            "uaT": np.ascontiguousarray(np.asarray(inp["L1_ffn_up_aux"]).T),
        })
    res = run_bass_kernel_spmd(nc, in_maps, core_ids=list(range(8)))
    out = np.zeros((B, T, D), np.float32)
    for c in range(8):
        b, q = c // 4, c % 4
        ts = slice(q * TPC, (q + 1) * TPC)
        out[b, ts] = np.asarray(res.results[c]["x4o"])
    return out


def build_full(sign=1.0):
    nc = bass.Bass("TRN2", target_bir_lowering=False)
    dram = {}
    dram["xfm"] = nc.dram_tensor("xfm", [D, T], F32, kind="ExternalInput").ap()
    for w in ("q", "k", "v"):
        dram[f"w{w}T"] = nc.dram_tensor(f"w{w}T", [D, HPC * HD], F32, kind="ExternalInput").ap()
    dram["bqkv"] = nc.dram_tensor("bqkv", [3, HPC * HD], F32, kind="ExternalInput").ap()
    dram["x_sl"] = nc.dram_tensor("x_sl", [TPC, D], F32, kind="ExternalInput").ap()
    dram["wo2T"] = nc.dram_tensor("wo2T", [2 * D, D], F32, kind="ExternalInput").ap()
    dram["bo"] = nc.dram_tensor("bo", [1, D], F32, kind="ExternalInput").ap()
    dram["wgT"] = nc.dram_tensor("wgT", [D, 3072], I32, kind="ExternalInput").ap()
    dram["wuT"] = nc.dram_tensor("wuT", [D, 3072], I32, kind="ExternalInput").ap()
    dram["wdT"] = nc.dram_tensor("wdT", [3072, D], I32, kind="ExternalInput").ap()
    for w in ("q", "k", "v"):
        dram[f"w{w}C"] = nc.dram_tensor(f"w{w}C", [D, HPC * HD], F32, kind="ExternalInput").ap()
    dram["bqkvC"] = nc.dram_tensor("bqkvC", [3, HPC * HD], F32, kind="ExternalInput").ap()
    dram["gain"] = nc.dram_tensor("gain", [1, 1], F32, kind="ExternalInput").ap()
    dram["wp2"] = nc.dram_tensor("wp2", [2 * D, D], F32, kind="ExternalInput").ap()
    dram["bproj"] = nc.dram_tensor("bproj", [1, D], F32, kind="ExternalInput").ap()
    dram["wLT"] = nc.dram_tensor("wLT", [D, D], F32, kind="ExternalInput").ap()
    dram["bL"] = nc.dram_tensor("bL", [1, D], F32, kind="ExternalInput").ap()
    dram["ldT"] = nc.dram_tensor("ldT", [100, 4], F32, kind="ExternalInput").ap()
    dram["da"] = nc.dram_tensor("da", [100, D], F32, kind="ExternalInput").ap()
    dram["luT"] = nc.dram_tensor("luT", [4, 50], F32, kind="ExternalInput").ap()
    dram["uaT"] = nc.dram_tensor("uaT", [50, D], F32, kind="ExternalInput").ap()
    dram["x4o"] = nc.dram_tensor("x4o", [TPC, D], F32, kind="ExternalOutput").ap()
    for n, sh, dt in (("x1_scr", [TPC, D], F32), ("x1fm_scr", [D, TPC], BF16),
                      ("x2_scr", [TPC, D], F32), ("x3_scr", [TPC, D], F32),
                      ("x3fm_scr", [D, TPC], BF16),
                      ("a2a1_in", [2048, TPC], BF16), ("a2a1_out", [2048, TPC], BF16),
                      ("ag2_in", [D, TPC], BF16), ("ag2_out", [4 * D, TPC], BF16),
                      ("a2a3_in", [2048, TPC], BF16), ("a2a3_out", [2048, TPC], BF16)):
        dram[n] = nc.dram_tensor(n, sh, dt).ap()
    es0 = ExitStack()
    with es0:
        sc = Sched(nc, es0)
        block = es0.enter_context(nc.Block())
        phase_A(nc, sc, block, dram, fused=True)
        sc.barrier()
        dram["cc_wait"] = 1
        phase_B(nc, sc, block, dram, fused=True)
        sc.barrier()
        dram["cc_wait"] = 2
        phase_C(nc, sc, block, dram, sign, fused=True)
        sc.barrier()
        dram["cc_wait"] = 3
        phase_D(nc, sc, block, dram, fused=True)
    return nc


def kernel(**inp):
    inp = {k: np.asarray(v) for k, v in inp.items()}
    x = inp["L0_attn_x"]
    gain = inp["L1_attn_score_gain"].reshape(1, 1).astype(np.float32)
    sign = 1.0 if float(gain[0, 0]) >= 0 else -1.0
    W_attn = inp["L1_attn_W_attn"]
    b_attn = inp["L1_attn_b_attn"]
    woT = np.ascontiguousarray(inp["L0_attn_Wo"].T)
    wproj = np.ascontiguousarray(inp["L1_attn_W_proj"])
    zeros = np.zeros((D, D), np.float32)
    nc = build_full(sign=sign)
    in_maps = []
    for c in range(8):
        b, h0 = HEAD_OF_CORE[c]
        q = c % 4
        sl = slice(h0 * HD, (h0 + HPC) * HD)
        ts = slice(q * TPC, (q + 1) * TPC)
        wo2 = np.concatenate([woT, zeros], 0) if b == 0 else np.concatenate([zeros, woT], 0)
        wp2 = np.concatenate([wproj, zeros], 0) if b == 0 else np.concatenate([zeros, wproj], 0)
        in_maps.append({
            "xfm": np.ascontiguousarray(x[b].T),
            "wqT": np.ascontiguousarray(inp["L0_attn_Wq"][sl, :].T),
            "wkT": np.ascontiguousarray(inp["L0_attn_Wk"][sl, :].T),
            "wvT": np.ascontiguousarray(inp["L0_attn_Wv"][sl, :].T),
            "bqkv": np.ascontiguousarray(np.stack([
                inp["L0_attn_bq"][sl], inp["L0_attn_bk"][sl], inp["L0_attn_bv"][sl]])),
            "x_sl": np.ascontiguousarray(x[b][ts, :]),
            "wo2T": np.ascontiguousarray(wo2),
            "bo": np.ascontiguousarray(inp["L0_attn_bo"][None, :]),
            "wgT": np.ascontiguousarray(inp["L0_ffn_w_gate"].T),
            "wuT": np.ascontiguousarray(inp["L0_ffn_w_up"].T),
            "wdT": np.ascontiguousarray(inp["L0_ffn_w_down"].T),
            "wqC": np.ascontiguousarray(W_attn[:, sl]),
            "wkC": np.ascontiguousarray(W_attn[:, 1024 + h0 * HD: 1024 + (h0 + HPC) * HD]),
            "wvC": np.ascontiguousarray(W_attn[:, 2048 + h0 * HD: 2048 + (h0 + HPC) * HD]),
            "bqkvC": np.ascontiguousarray(np.stack([
                b_attn[sl], b_attn[1024 + h0 * HD: 1024 + (h0 + HPC) * HD],
                b_attn[2048 + h0 * HD: 2048 + (h0 + HPC) * HD]])),
            "gain": np.ascontiguousarray(gain),
            "wp2": np.ascontiguousarray(wp2),
            "bproj": np.ascontiguousarray(inp["L1_attn_b_proj"][None, :]),
            "wLT": np.ascontiguousarray(inp["L1_ffn_W"].T),
            "bL": np.ascontiguousarray(inp["L1_ffn_b"][None, :]),
            "ldT": np.ascontiguousarray(inp["L1_ffn_lora_down"].T),
            "da": np.ascontiguousarray(inp["L1_ffn_down_aux"]),
            "luT": np.ascontiguousarray(inp["L1_ffn_lora_up"].T),
            "uaT": np.ascontiguousarray(inp["L1_ffn_up_aux"].T),
        })
    res = run_bass_kernel_spmd(nc, in_maps, core_ids=list(range(8)))
    out = np.zeros((B, T, D), np.float32)
    for c in range(8):
        b, _ = HEAD_OF_CORE[c]
        q = c % 4
        out[b, q * TPC:(q + 1) * TPC] = np.asarray(res.results[c]["x4o"])
    return out
